# version 18
# baseline (speedup 1.0000x reference)
import numpy as np
import concourse.bass as bass
import concourse.mybir as mybir
from contextlib import ExitStack

F32 = mybir.dt.float32
BF16 = mybir.dt.bfloat16
AF = mybir.ActivationFunctionType
ALU = mybir.AluOpType
AX = mybir.AxisListType

D, DC = 1024, 8
H4, HC4 = 4096, 32
HE, HEC = 2048, 16
NE = 8
EPS = 1e-5
NEWTON_LINES = (4.0, 1.0, 0.25, 0.0625)
NEWTON_ITERS = 3


def build(nc, T=4096):
    assert T % 1024 == 0
    n_tiles, n_groups, n_supers = T // 128, T // 512, T // 1024

    def din(name, shape, dt=F32):
        return nc.dram_tensor(name, shape, dt, kind="ExternalInput").ap()

    x_d = din("x", [T, D])
    ln1_g, ln1_b = din("ln1_g", [D]), din("ln1_b", [D])
    proj_w, proj_b = din("proj_w", [D, D]), din("proj_b", [D])
    ln2_g, ln2_b = din("ln2_g", [D]), din("ln2_b", [D])
    fc1_w, fc1_b = din("fc1_w", [H4, D]), din("fc1_b", [H4])
    lnh_g, lnh_b = din("lnh_g", [H4]), din("lnh_b", [H4])
    fc2_w, fc2_b = din("fc2_w", [D, H4]), din("fc2_b", [D])
    gamma1, gamma2 = din("gamma1", [D]), din("gamma2", [D])
    rw1, rb1 = din("rw1", [D, D]), din("rb1", [D])
    rw2, rb2 = din("rw2", [D, NE]), din("rb2", [NE])
    ew1, eb1 = din("ew1", [NE, D, HE]), din("eb1", [NE, HE])
    ew2, eb2 = din("ew2", [NE, HE, D]), din("eb2", [NE, D])
    eye_b_d = din("eye_b", [128, 128], BF16)
    eye_f_d = din("eye_f", [128, 128], F32)
    out_d = nc.dram_tensor("out", [T, D], F32, kind="ExternalOutput").ap()

    pbf_d = nc.dram_tensor("pbf", [D, D], BF16, kind="Internal").ap()
    f1bf_d = nc.dram_tensor("f1bf", [H4, D], BF16, kind="Internal").ap()
    f2bf_d = nc.dram_tensor("f2bf", [D, H4], BF16, kind="Internal").ap()
    ew1bf_d = nc.dram_tensor("ew1bf", [NE, D, HE], BF16, kind="Internal").ap()
    ew2bf_d = nc.dram_tensor("ew2bf", [NE, HE, D], BF16, kind="Internal").ap()
    xn_d = nc.dram_tensor("xn_sc", [T, D], F32, kind="Internal").ap()
    xnT_d = nc.dram_tensor("xnT_sc", [n_supers, DC, 128, 1024], BF16, kind="Internal").ap()
    x1_d = nc.dram_tensor("x1_sc", [T, D], F32, kind="Internal").ap()
    s_d = nc.dram_tensor("s_sc", [T, D], F32, kind="Internal").ap()
    wT_d = nc.dram_tensor("wT_sc", [8, T // 128 * 128], BF16, kind="Internal").ap()

    es = ExitStack()
    ARENA_BYTES = 207 * 1024
    arena = es.enter_context(nc.sbuf_tensor("arena", [128, ARENA_BYTES], mybir.dt.int8))[:]

    class Alloc:
        def __init__(self):
            self.off, self.marks = 0, []
        def buf(self, cols, dt=F32, parts=128):
            isz = 4 if dt == F32 else 2
            self.off = (self.off + 31) & ~31
            ap = arena[0:parts, self.off:self.off + cols * isz].bitcast(dt)
            self.off += cols * isz
            assert self.off <= ARENA_BYTES, f"arena overflow {self.off}"
            return ap
        def push(self):
            self.marks.append(self.off)
        def pop(self):
            self.off = self.marks.pop()
    al = Alloc()

    P4 = es.enter_context(nc.psum_tensor("P4", [128, 2048], F32))[:]
    P2 = es.enter_context(nc.psum_tensor("P2", [128, 1024], F32))[:]
    P1a = es.enter_context(nc.psum_tensor("P1a", [128, 512], F32))[:]
    P1b = es.enter_context(nc.psum_tensor("P1b", [128, 512], F32))[:]
    ps_tr_bf = P4[:, 0:512].bitcast(BF16)
    ps_lT = P4[0:8, 512:1024]
    ps_lt2 = P4[:, 1024:1056]
    ps_wT = P4[0:8, 1536:1664]

    sched = {e: [] for e in ("y", "g", "v", "s", "t")}

    class Sem:
        def __init__(self, name):
            self.h = es.enter_context(nc.semaphore(name))
            self.n = 0
            self.name = name
    vp, sp, tp = Sem("vp"), Sem("sp"), Sem("tp")
    g_pbf, g_f1, g_f2 = Sem("g_pbf"), Sem("g_f1"), Sem("g_f2")
    g_small, g_pa, g_ew, g_out = Sem("g_small"), Sem("g_pa"), Sem("g_ew"), Sem("g_out")
    lm0, lm0b, lm1, lm_pj = Sem("lm0"), Sem("lm0b"), Sem("lm1"), Sem("lm_pj")
    ch_xg = [Sem("ch_xg0"), Sem("ch_xg1")]
    ch_sxn = [Sem("ch_sxn0"), Sem("ch_sxn1")]
    ch_sxT = [Sem("ch_sxT0"), Sem("ch_sxT1")]
    ch_sx1 = [Sem("ch_sx10"), Sem("ch_sx11")]
    ch_l1 = [Sem("ch_l10"), Sem("ch_l11"), Sem("ch_l12")]
    ch_ss = [Sem("ch_ss0"), Sem("ch_ss1")]
    ch_w = [Sem("ch_w0"), Sem("ch_w1"), Sem("ch_w2")]
    ch_xt, ch_ac, ch_ac2 = Sem("ch_xt"), Sem("ch_ac"), Sem("ch_ac2")
    ch_wt, g_eb2 = Sem("ch_wt"), Sem("g_eb2")
    s_f2, g_outb = Sem("s_f2"), Sem("g_outb")
    ch_row = Sem("ch_row")
    ch_sf = [Sem("ch_sf0"), Sem("ch_sf1")]
    ch_xf = [Sem("ch_xf0"), Sem("ch_xf1")]

    def wait(e, sem, val):
        if val <= 0:
            return
        assert val <= sem.n, f"wait on future {val}>{sem.n} ({sem.name})"
        sched[e].append(lambda h, s=sem.h, v=val: h.wait_ge(s, v))

    def wmark(e, mark):
        if not mark:
            return
        if isinstance(mark, list):
            for m in mark:
                wmark(e, m)
        else:
            wait(e, mark[0], mark[1])

    def dma(e, ch, out, in_, slow=False, transpose=False):
        ch.n += 16
        def fn(h, out=out, in_=in_, slow=slow, tr=transpose, s=ch.h):
            if slow:
                with nc.allow_non_contiguous_dma(reason="tiny chunked vector load"):
                    h.dma_start(out=out, in_=in_).then_inc(s, 16)
            elif tr:
                h.dma_start(out=out, in_=in_, transpose=True).then_inc(s, 16)
            else:
                h.dma_start(out=out, in_=in_).then_inc(s, 16)
        sched[e].append(fn)
        return ch.n

    def vop(name, *args, drain=True, **kw):
        vp.n += 1
        def fn(h, name=name, args=args, kw=kw, drain=drain):
            i = getattr(h, name)(*args, **kw)
            if drain:
                h.drain().then_inc(vp.h, 1)
            else:
                i.then_inc(vp.h, 1)
        sched["v"].append(fn)
        return vp.n

    def sop(name, *args, drain=True, **kw):
        sp.n += 1
        def fn(h, name=name, args=args, kw=kw, drain=drain):
            i = getattr(h, name)(*args, **kw)
            if drain:
                h.drain().then_inc(sp.h, 1)
            else:
                i.then_inc(sp.h, 1)
        sched["s"].append(fn)
        return sp.n

    def top(name, *args, inc=0, **kw):
        tp.n += inc
        def fn(h, name=name, args=args, kw=kw, inc=inc):
            i = getattr(h, name)(*args, **kw)
            if inc:
                i.then_inc(tp.h, inc)
        sched["t"].append(fn)
        return tp.n

    def barrier(engines, vm, sm, tm):
        for e in engines:
            wait(e, vp, vm); wait(e, sp, sm); wait(e, tp, tm)

    eye_b = al.buf(128, BF16)
    eye_f = al.buf(128, F32)
    ones_b = al.buf(512, BF16, parts=1)
    w_buf = al.buf(n_tiles * 8, F32)
    eb1T = al.buf(HEC * NE, F32)
    rb2b = al.buf(NE, BF16, parts=1)
    nscr = al.buf(16, F32)
    nscr2 = al.buf(48, F32)
    zmean = al.buf(n_tiles, F32)
    zrstd = al.buf(n_tiles, F32)

    def newton_rstd(tcol, outcol, s0, s1, s2, lines=NEWTON_LINES, iters=NEWTON_ITERS):
        first = True
        for aa in lines:
            vop("tensor_scalar", s1, tcol, aa * aa, 1.0 / (2 * aa), ALU.add, ALU.mult)
            if first:
                vop("tensor_copy", s0, s1); first = False
            else:
                vop("tensor_tensor", s0, s0, s1, ALU.min)
        vop("tensor_scalar", s2, tcol, 0.5, None, ALU.mult)
        for _ in range(iters):
            vop("reciprocal", s1, s0)
            vop("tensor_tensor", s1, s2, s1, ALU.mult)
            vop("scalar_tensor_tensor", s0, s0, 0.5, s1, ALU.mult, ALU.add)
        return vop("reciprocal", outcol, s0)

    rowst = {"tp_mark": 0, "vp_mark": 0}

    def load_vecT(vec_d_ap, dest, nch, vrow):
        wait("t", lm0, lm0_all)
        for r0 in range(0, nch, 8):
            k = min(8, nch - r0)
            wait("y", tp, rowst["tp_mark"])
            lr = dma("y", ch_row, vrow[0:1, 0:k * 128], vec_d_ap[None, r0 * 128:(r0 + k) * 128])
            wait("t", ch_row, lr)
            wait("t", vp, rowst["vp_mark"])
            for j in range(k):
                top("transpose", P1b[0:128, j:j + 1], vrow[0:1, j * 128:(j + 1) * 128],
                    eye_f[0:1, 0:1], inc=(1 if j == k - 1 else 0))
            rowst["tp_mark"] = tp.n
            wait("v", tp, rowst["tp_mark"])
            rowst["vp_mark"] = vop("tensor_copy", dest[:, r0:r0 + k], P1b[0:128, 0:k])

    dma("g", g_pbf, pbf_d[:], proj_w[:])

    al.push()
    projT = al.buf(DC * D, BF16)
    rw1b = al.buf(DC * D, BF16)
    rw2b = al.buf(DC * NE, BF16)
    xg = [al.buf(4 * D, F32) for _ in range(2)]
    xnf = [al.buf(4 * D, F32) for _ in range(2)]
    xnbb = al.buf(4 * D, BF16)
    xnT_g = [al.buf(8 * 512, BF16) for _ in range(2)]
    uT_g = [al.buf(8 * 512, BF16) for _ in range(2)]
    x1s = [al.buf(D, F32) for _ in range(2)]
    lT_s = [al.buf(512, F32, parts=8) for _ in range(2)]
    l_s = al.buf(32, F32)
    gw = al.buf(128, F32)
    p0stat = al.buf(128, F32)
    bc_c1 = al.buf(D, F32)
    rb1T = al.buf(DC, F32)
    wT_buf = al.buf(n_tiles * 128, BF16, parts=8)
    g1T = al.buf(DC, F32)
    b1T = al.buf(DC, BF16)
    grow1 = al.buf(D, F32, parts=1)
    growb = al.buf(D, BF16, parts=1)
    prow = al.buf(D, F32, parts=1)
    crow = al.buf(D, F32, parts=1)
    crow_b = al.buf(D, BF16, parts=1)
    bcg = al.buf(D, BF16)
    eb1n = al.buf(HE, F32, parts=8)

    dma("g", g_small, rb2b[:], rb2[None, :])
    dma("g", g_small, rw1b.rearrange("p (c n) -> p c n", c=DC),
        rw1.rearrange("(c p) n -> p c n", p=128))
    dma("g", g_small, rw2b.rearrange("p (c n) -> p c n", c=DC),
        rw2.rearrange("(c p) n -> p c n", p=128))
    g_small_all = dma("g", g_small, growb[:], gamma1[None, :])
    g_ew_all = 0

    dma("y", lm0, eye_b[:], eye_b_d[:])
    dma("y", lm0, eye_f[:], eye_f_d[:])
    dma("y", lm0, grow1[:], gamma1[None, :])
    dma("y", lm0, prow[0:1, :], proj_b[None, :])
    lm0_all = lm0.n
    lm0b_all = dma("y", lm0b, eb1n[:], eb1[:])

    ld_x0_early = dma("y", ch_xg[0], xg[0].rearrange("p (s n) -> p s n", s=4),
                      x_d[0:512, :].rearrange("(s p) n -> p s n", p=128))
    wait("y", g_pbf, 16)
    pj_stage = xg[1].bitcast(BF16)
    lm_pj_all = dma("y", lm_pj, pj_stage.rearrange("p (s n) -> p s n", s=8),
                    pbf_d.rearrange("(s p) n -> p s n", p=128))
    wait("t", lm_pj, lm_pj_all)
    wait("t", lm0, lm0_all)
    pj_prev_evac = [0]
    for oc in range(8):
        wait("t", sp, pj_prev_evac[0])
        for ic in range(DC):
            top("transpose", ps_tr_bf[:, ic * 128:(ic + 1) * 128],
                pj_stage[:, oc * D + ic * 128: oc * D + (ic + 1) * 128], eye_b[:],
                inc=(1 if ic == DC - 1 else 0))
        wait("s", tp, tp.n)
        pj_prev_evac[0] = sop("activation",
                              projT.rearrange("p (ic n) -> p ic n", ic=DC)[:, :, oc * 128:(oc + 1) * 128],
                              ps_tr_bf[:].rearrange("p (ic n) -> p ic n", ic=DC), AF.Copy)
    s_pj_done = pj_prev_evac[0]
    pj_tr_done = tp.n
    load_vecT(ln1_g, g1T, DC, crow)
    load_vecT(rb1, rb1T, DC, crow)
    load_vecT(ln1_b, b1T, DC, crow)

    v_ones = vop("memset", ones_b[0:1, :], 1.0)
    wait("v", lm0, lm0_all)
    wait("v", sp, s_pj_done)
    for ic in range(DC):
        vop("tensor_scalar", projT[:, ic * D:(ic + 1) * D], projT[:, ic * D:(ic + 1) * D],
            g1T[:, ic:ic + 1], None, ALU.mult)
    v_p1 = vp.n
    wait("t", vp, v_p1)
    wait("t", g_small, g_small_all)
    for half in range(2):
        n0 = half * 512
        for ic in range(DC):
            top("matmul", P2[0:1, n0:n0 + 512], b1T[:, ic:ic + 1],
                projT[:, ic * D + n0: ic * D + n0 + 512],
                start=(ic == 0), stop=(ic == DC - 1),
                inc=(1 if (half == 1 and ic == DC - 1) else 0))
    t_c1 = tp.n
    wait("v", tp, t_c1)
    vop("tensor_tensor", crow[0:1, :], P2[0:1, 0:D], prow[0:1, :], ALU.add)
    vop("tensor_tensor", crow[0:1, :], crow[0:1, :], grow1[0:1, :], ALU.mult)
    v_c1r = vop("tensor_copy", crow_b[0:1, :], crow[0:1, :])
    wait("t", vp, v_c1r)
    top("matmul", P2[:, 0:512], ones_b[0:1, 0:128], crow_b[0:1, 0:512], start=True, stop=True)
    t_bc1 = top("matmul", P2[:, 512:1024], ones_b[0:1, 0:128], crow_b[0:1, 512:1024],
                start=True, stop=True, inc=1)
    wait("v", tp, t_bc1)
    v_bc1 = vop("tensor_copy", bc_c1[:], P2[:])
    top("matmul", P4[:, 0:512], ones_b[0:1, 0:128], growb[0:1, 0:512], start=True, stop=True)
    t_bg = top("matmul", P4[:, 512:1024], ones_b[0:1, 0:128], growb[0:1, 512:1024],
               start=True, stop=True, inc=1)
    wait("v", tp, t_bg)
    v_bcg = vop("tensor_copy", bcg[:], P4[:, 0:1024])
    for ic in range(DC):
        vop("tensor_tensor", projT[:, ic * D:(ic + 1) * D], projT[:, ic * D:(ic + 1) * D],
            bcg[:], ALU.mult)
    v_projdone = vp.n
    wait("t", lm0b, lm0b_all)
    for hc in range(HEC):
        top("transpose", P1a[0:128, hc * 8:(hc + 1) * 8], eb1n[0:8, hc * 128:(hc + 1) * 128],
            eye_f[0:8, 0:8], inc=(1 if hc == HEC - 1 else 0))
    t_eb1 = tp.n
    wait("v", tp, t_eb1)
    v_eb1 = vop("tensor_copy", eb1T[:], P1a[:, 0:HEC * NE])

    xg_free = [None, None]
    xnf_free = [None, None]
    xnT_free = [None, None]
    uT_free = [None, None]
    x1s_free = [None, None]
    p2_free = (vp, v_bc1)
    p1_free = [(vp, v_eb1), None]
    p4b0_free = (vp, v_bcg)
    p4b1_free = None
    p4b2_free = None
    p4b3_free = None
    lT_free = [None, None]


    w_ready = [0] * n_tiles
    wT_ready = [0] * n_tiles
    pend_lt2 = []
    pend_wT = []

    def flush_wT(force=False):
        nonlocal p4b3_free
        if not pend_wT or (len(pend_wT) < 2 and not force):
            return
        g, w_grp = pend_wT.pop(0)
        wslab = w_buf[:, g * 32:(g + 1) * 32]
        for t in range(4):
            tile = g * 4 + t
            wait("t", vp, w_grp)
            wmark("t", p4b3_free)
            t_wT = top("transpose", ps_wT[:], wslab[:, t * 8:(t + 1) * 8], eye_f[:], inc=1)
            wait("s", tp, t_wT)
            wT_ready[tile] = sop("activation", wT_buf[0:8, tile * 128:(tile + 1) * 128],
                                 ps_wT[:], AF.Copy)
            p4b3_free = (sp, wT_ready[tile])

    def flush_lt2():
        nonlocal p4b2_free, lT_free
        if not pend_lt2:
            return
        g, s_lT = pend_lt2.pop(0)
        wait("t", sp, s_lT)
        wmark("t", p4b2_free)
        for t in range(4):
            top("transpose", ps_lt2[:, t * 8:(t + 1) * 8], lT_s[g % 2][0:8, t * 128:(t + 1) * 128],
                eye_f[0:8, 0:8], inc=(1 if t == 3 else 0))
        t_lt2 = tp.n
        lT_free[g % 2] = (tp, t_lt2)
        wait("v", tp, t_lt2)
        v_l = vop("tensor_copy", l_s[:], ps_lt2[:])
        p4b2_free = (vp, v_l)
        wslab = w_buf[:, g * 32:(g + 1) * 32]
        vop("tensor_reduce", gw[:, 0:4], v3(l_s[:]), AX.X, ALU.max)
        vop("tensor_tensor", v3(gw[:, 32:64]), v3(l_s[:]), bc4(gw[:, 0:4]), ALU.is_equal)
        vop("scalar_tensor_tensor", v3(gw[:, 96:128]), v3(gw[:, 32:64]), -1e30, v3(l_s[:]),
            ALU.mult, ALU.add)
        vop("tensor_reduce", gw[:, 4:8], v3(gw[:, 96:128]), AX.X, ALU.max)
        vop("tensor_tensor", v3(gw[:, 64:96]), v3(l_s[:]), bc4(gw[:, 4:8]), ALU.is_equal)
        vop("tensor_tensor", gw[:, 8:12], gw[:, 0:4], gw[:, 4:8], ALU.subtract)
        v_dc = vop("tensor_scalar", gw[:, 12:16], gw[:, 8:12], 1e-30, None, ALU.max)
        wait("s", vp, v_dc)
        s_sv = sop("activation", gw[:, 16:20], gw[:, 12:16], AF.Silu)
        wait("v", sp, s_sv)
        vop("reciprocal", gw[:, 20:24], gw[:, 12:16])
        vop("tensor_tensor", gw[:, 24:28], gw[:, 16:20], gw[:, 20:24], ALU.mult)
        vop("tensor_scalar", gw[:, 28:32], gw[:, 24:28], -1.0, 1.0, ALU.mult, ALU.add)
        vop("tensor_tensor", v3(wslab), v3(gw[:, 32:64]), bc4(gw[:, 24:28]), ALU.mult, drain=False)
        vop("tensor_tensor", v3(gw[:, 96:128]), v3(gw[:, 64:96]), bc4(gw[:, 28:32]), ALU.mult)
        w_grp = vop("tensor_tensor", wslab, wslab, gw[:, 96:128], ALU.add)
        for t in range(4):
            w_ready[g * 4 + t] = w_grp
        pend_wT.append((g, w_grp))

    def v3(ap32):
        return ap32.rearrange("p (t e) -> p t e", e=8)

    def bc4(ap4):
        return ap4.rearrange("p (t o) -> p t o", o=1).broadcast_to((128, 4, 8))

    def _xv(par):
        o = par * 64
        blk = p0stat[:, o:o + 8].rearrange("p (t w) -> p t w", w=2)
        return dict(mean=blk[:, :, 0:1], var=blk[:, :, 1:2],
                    tx4=p0stat[:, o + 8:o + 12], rx4=p0stat[:, o + 12:o + 16],
                    t24=p0stat[:, o + 16:o + 20], r24=p0stat[:, o + 20:o + 24],
                    ns0=p0stat[:, o + 24:o + 28], ns1=p0stat[:, o + 28:o + 32],
                    aggr=lambda t: p0stat[:, o + 2 * t:o + 2 * t + 2])
    XV = [_xv(0), _xv(1)]
    z_mean = p0stat[:, 32:40].rearrange("p (t w) -> p t w", w=2)[:, :, 0:1]
    z_var = p0stat[:, 32:40].rearrange("p (t w) -> p t w", w=2)[:, :, 1:2]
    ns2 = p0stat[:, 40:44]
    tz4, rz4 = p0stat[:, 44:48], p0stat[:, 48:52]
    zaggr = lambda t: p0stat[:, 32 + 2 * t:34 + 2 * t]

    ld_x_t = [0] * (n_groups + 1)
    v_casts_t = [0] * n_groups
    v_xn_last_t = [0] * n_groups
    t_tr_last = [0]

    xpose_done = [0] * n_groups

    def emit_xpose(g):
        nonlocal p4b0_free
        par = g % 2
        wait("t", sp, v_casts_t[g])
        for t in range(4):
            wmark("t", p4b0_free)
            for ic in range(DC):
                top("transpose", ps_tr_bf[:, ic * 128:(ic + 1) * 128],
                    xnbb[:, t * D + ic * 128: t * D + (ic + 1) * 128], eye_b[:],
                    inc=(1 if ic == DC - 1 else 0))
            t_tr = tp.n
            wait("s", tp, t_tr)
            if t == 0:
                wmark("s", xnT_free[par])
            s_ev = sop("activation",
                       xnT_g[par].rearrange("p (ic n) -> p ic n", ic=DC)[:, :, t * 128:(t + 1) * 128],
                       ps_tr_bf[:].rearrange("p (ic n) -> p ic n", ic=DC), AF.Copy)
            p4b0_free = (sp, s_ev)
        t_tr_last[0] = tp.n
        xpose_done[g] = sp.n

    def emit_stats(g):
        par = g % 2
        X = XV[par]
        wait("v", ch_xg[par], ld_x_t[g])
        for t in range(4):
            xt = xg[par][:, t * D:(t + 1) * D]
            vop("bn_stats", nscr2[:, 0:6], xt[:, 0:512], drain=False)
            vop("bn_stats", nscr2[:, 6:12], xt[:, 512:1024])
            vop("bn_aggr", X["aggr"](t), nscr2[:, 0:12])
        vop("tensor_scalar", X["tx4"], X["var"], EPS, None, ALU.add)
        newton_rstd(X["tx4"], X["rx4"], X["ns0"], X["ns1"], ns2, lines=(1.0,), iters=2)
        vop("tensor_tensor", X["t24"], X["var"], X["rx4"], ALU.mult)
        vop("tensor_tensor", X["t24"], X["t24"], X["rx4"], ALU.mult)
        vop("tensor_scalar", X["t24"], X["t24"], EPS, None, ALU.add)
        newton_rstd(X["t24"], X["r24"], X["ns0"], X["ns1"], ns2, lines=(1.0,), iters=2)
        wmark("v", xnf_free[par])
        for t in range(4):
            vop("tensor_scalar", xnf[par][:, t * D:(t + 1) * D], xg[par][:, t * D:(t + 1) * D],
                X["mean"][:, t], X["rx4"][:, t:t + 1], ALU.subtract, ALU.mult,
                drain=(t == 3))
        v_xn_last_t[g] = vp.n
        wait("s", vp, v_xn_last_t[g])
        wait("s", tp, t_tr_last[0])
        v_casts_t[g] = sop("activation", xnbb[:], xnf[par][:], AF.Copy)

    ld_x_t[0] = ld_x0_early
    emit_stats(0)
    emit_xpose(0)

    for g in range(n_groups):
        par = g % 2
        if g == min(2, n_groups - 1):
            wait("g", sp, v_casts_t[g - 1])
            dma("g", g_f1, f1bf_d[:], fc1_w[:])
            dma("g", g_f2, f2bf_d[:], fc2_w[:])
        if g == min(6, n_groups - 1):
            wait("g", sp, v_casts_t[g - 1])
            for e in range(NE):
                dma("g", g_ew, ew1bf_d[e], ew1[e])
                g_ew_all = dma("g", g_ew, ew2bf_d[e], ew2[e])
        if g + 1 < n_groups:
            np_ = (g + 1) % 2
            if g == 0:
                wait("y", tp, pj_tr_done)
            wmark("y", xg_free[np_])
            ld_x_t[g + 1] = dma("y", ch_xg[np_], xg[np_].rearrange("p (s n) -> p s n", s=4),
                                x_d[(g + 1) * 512:(g + 2) * 512, :].rearrange("(s p) n -> p s n", p=128))
        if g > 0:
            emit_xpose(g)
        flush_lt2()
        v_xn_last = v_xn_last_t[g]
        v_xnT = xpose_done[g]
        wait("y", vp, v_xn_last)
        st_xn = dma("y", ch_sxn[par], xn_d[g * 512:(g + 1) * 512, :].rearrange("(s p) n -> p s n", p=128),
                    xnf[par].rearrange("p (s n) -> p s n", s=4))
        wait("y", sp, v_xnT)
        wait("t", sp, v_xnT)
        st_i, hf = g // 2, g % 2
        st_xnT = dma("y", ch_sxT[par],
                     xnT_d[st_i, :, :, hf * 512:(hf + 1) * 512].rearrange("ic p n -> p ic n"),
                     xnT_g[par].rearrange("p (ic n) -> p ic n", ic=DC))
        wait("t", sp, v_xnT)
        for oc in range(DC):
            ppar = oc % 2
            PU = P1a if ppar == 0 else P1b
            wmark("t", p1_free[ppar])
            for ic in range(DC):
                top("matmul", PU[:], rw1b[:, ic * D + oc * 128: ic * D + (oc + 1) * 128],
                    xnT_g[par][:, ic * 512:(ic + 1) * 512],
                    start=(ic == 0), stop=(ic == DC - 1), inc=(1 if ic == DC - 1 else 0))
            t_u = tp.n
            wait("s", tp, t_u)
            if g == 0 and oc == 0:
                wait("s", lm0, lm0_all)
            if oc == 0:
                wmark("s", uT_free[par])
            s_u = sop("activation", uT_g[par][:, oc * 512:(oc + 1) * 512], PU[:],
                      AF.Silu, bias=rb1T[:, oc:oc + 1])
            p1_free[ppar] = (sp, s_u)
        s_uT = sp.n
        wait("t", sp, s_uT)
        wmark("t", p4b1_free)
        for oc in range(DC):
            top("matmul", ps_lT[:], rw2b[:, oc * NE:(oc + 1) * NE],
                uT_g[par][:, oc * 512:(oc + 1) * 512], start=(oc == 0), stop=False)
        t_lg = top("matmul", ps_lT[:], rb2b[0:1, 0:NE], ones_b[0:1, 0:512],
                   start=False, stop=True, inc=1)
        uT_free[par] = (tp, t_lg)
        wait("s", tp, t_lg)
        wmark("s", lT_free[g % 2])
        s_lT = sop("activation", lT_s[g % 2][:], ps_lT[:], AF.Copy)
        p4b1_free = (sp, s_lT)
        pend_lt2.append((g, s_lT))
        for t in range(4):
            tpar = t % 2
            wmark("t", p2_free)
            if g == 0 and t == 0:
                wait("t", vp, v_projdone)
            for hn in range(2):
                n0 = hn * 512
                for ic in range(DC):
                    top("matmul", P2[:, n0:n0 + 512],
                        xnT_g[par][:, ic * 512 + t * 128: ic * 512 + (t + 1) * 128],
                        projT[:, ic * D + n0: ic * D + n0 + 512],
                        start=(ic == 0), stop=(ic == DC - 1),
                        inc=(1 if (hn == 1 and ic == DC - 1) else 0))
            t_proj = tp.n
            wait("v", tp, t_proj)
            wmark("v", x1s_free[tpar])
            vop("scalar_tensor_tensor", x1s[tpar][:], P2[:],
                XV[par]["r24"][:, t:t + 1], bc_c1[:], ALU.mult, ALU.add)
            v_x1 = vop("tensor_tensor", x1s[tpar][:],
                       x1s[tpar][:], xnf[par][:, t * D:(t + 1) * D], ALU.add)
            p2_free = (vp, v_x1)
            wait("y", vp, v_x1)
            st_x1 = dma("y", ch_sx1[tpar], x1_d[(g * 4 + t) * 128:(g * 4 + t + 1) * 128, :],
                        x1s[tpar][:])
            x1s_free[tpar] = (ch_sx1[tpar], st_x1)
            vop("bn_stats", nscr2[:, 0:6], x1s[tpar][:, 0:512], drain=False)
            vop("bn_stats", nscr2[:, 6:12], x1s[tpar][:, 512:1024])
            vop("bn_aggr", zaggr(t), nscr2[:, 0:12])
        vop("tensor_scalar", tz4, z_var, EPS, None, ALU.add)
        newton_rstd(tz4, rz4, XV[par]["ns0"], XV[par]["ns1"], ns2, lines=(1.0,), iters=3)
        vop("tensor_copy", zmean[:, g * 4:(g + 1) * 4], z_mean, drain=False)
        v_zst = vop("tensor_copy", zrstd[:, g * 4:(g + 1) * 4], rz4)
        v_x1done = vp.n
        xg_free[par] = (vp, v_xn_last)
        xnf_free[par] = [(vp, v_x1done), (ch_sxn[par], st_xn)]
        xnT_free[par] = [(tp, tp.n), (ch_sxT[par], st_xnT)]
        if g + 1 < n_groups:
            emit_stats(g + 1)
        flush_wT()

    flush_lt2()
    flush_wT(force=True)
    flush_wT(force=True)
    wait("y", sp, wT_ready[n_tiles - 1])
    st_wt = dma("y", ch_wt, wT_d[:], wT_buf[:])
    p0_v, p0_s, p0_t = vp.n, sp.n, tp.n
    al.pop()

    al.push()
    fc1T = al.buf(DC * H4, BF16)
    fc2T = al.buf(HC4 * D, BF16)
    c1f_b = al.buf(H4, BF16, parts=1)
    c2f_b = al.buf(D, BF16, parts=1)
    al.push()
    g2T = al.buf(DC, F32)
    ghT = al.buf(HC4, F32)
    b2T = al.buf(DC, BF16)
    bhT = al.buf(HC4, BF16)
    grow2 = al.buf(D, F32, parts=1)
    growb2 = al.buf(D, BF16, parts=1)
    f1row = al.buf(H4, F32, parts=1)
    f2row = al.buf(D, F32, parts=1)
    crowA = al.buf(H4, F32, parts=1)
    bcg2 = al.buf(D, BF16)
    al.pop()
    zb = [al.buf(D, BF16) for _ in range(2)]
    zT = [al.buf(D, BF16) for _ in range(2)]
    hb = [al.buf(H4, BF16) for _ in range(2)]
    hT = [al.buf(H4, BF16) for _ in range(2)]
    x1t = [al.buf(D, F32) for _ in range(3)]
    sst = al.buf(D, F32)

    p0_store_done = [(ch_sxn[0], ch_sxn[0].n), (ch_sxn[1], ch_sxn[1].n),
                     (ch_sxT[0], ch_sxT[0].n), (ch_sxT[1], ch_sxT[1].n),
                     (ch_sx1[0], ch_sx1[0].n), (ch_sx1[1], ch_sx1[1].n),
                     (ch_xg[0], ch_xg[0].n), (ch_xg[1], ch_xg[1].n),
                     (ch_wt, ch_wt.n)]
    wait("y", vp, p0_v); wait("y", sp, p0_s); wait("y", tp, p0_t)
    wmark("y", p0_store_done)
    barrier(("v", "s", "t"), p0_v, p0_s, p0_t)
    wait("y", g_f1, 16)
    for ic in range(DC):
        dma("y", lm1, fc1T[:, ic * H4:(ic + 1) * H4], f1bf_d[:, ic * 128:(ic + 1) * 128],
            transpose=True)
    wait("s", g_f2, 16)
    wait("s", vp, p0_v); wait("s", tp, p0_t)
    wmark("s", p0_store_done)
    for hc in range(HC4):
        dma("s", s_f2, fc2T[:, hc * D:(hc + 1) * D], f2bf_d[:, hc * 128:(hc + 1) * 128],
            transpose=True)
    dma("y", lm1, f1row[0:1, :], fc1_b[None, :])
    dma("y", lm1, f2row[0:1, :], fc2_b[None, :])
    lm1_all = dma("y", lm1, grow2[0:1, :], gamma2[None, :])
    load_vecT(ln2_g, g2T, DC, crowA[0:1, 0:D])
    load_vecT(lnh_g, ghT, HC4, crowA[0:1, 0:D])
    load_vecT(ln2_b, b2T, DC, crowA[0:1, 0:D])
    load_vecT(lnh_b, bhT, HC4, crowA[0:1, 0:D])
    wait("g", vp, p0_v); wait("g", sp, p0_s); wait("g", tp, p0_t)
    wmark("g", p0_store_done)
    g_pa_all = dma("g", g_pa, growb2[:], gamma2[None, :])

    wait("v", lm1, lm1_all)
    for ic in range(DC):
        vop("tensor_scalar", fc1T[:, ic * H4:(ic + 1) * H4], fc1T[:, ic * H4:(ic + 1) * H4],
            g2T[:, ic:ic + 1], None, ALU.mult)
    v_f1fold = vp.n
    wait("t", vp, v_f1fold)
    for ch in range(8):
        if ch > 0:
            wait("t", vp, vp.n)
        for ic in range(DC):
            top("matmul", P2[0:1, 0:512], b2T[:, ic:ic + 1],
                fc1T[:, ic * H4 + ch * 512: ic * H4 + ch * 512 + 512],
                start=(ic == 0), stop=(ic == DC - 1), inc=(1 if ic == DC - 1 else 0))
        t_ch = tp.n
        wait("v", tp, t_ch)
        vop("tensor_copy", crowA[0:1, ch * 512:(ch + 1) * 512], P2[0:1, 0:512])
    vop("tensor_tensor", crowA[0:1, :], crowA[0:1, :], f1row[0:1, :], ALU.add)
    v_c1f = vop("tensor_copy", c1f_b[0:1, :], crowA[0:1, :])
    wait("v", s_f2, s_f2.n)
    for hc in range(HC4):
        vop("tensor_scalar", fc2T[:, hc * D:(hc + 1) * D], fc2T[:, hc * D:(hc + 1) * D],
            ghT[:, hc:hc + 1], None, ALU.mult)
    v_f2fold = vp.n
    wait("t", vp, v_f2fold)
    for ch in range(2):
        n0 = ch * 512
        if ch > 0:
            wait("t", vp, vp.n)
        for hc in range(HC4):
            top("matmul", P2[0:1, 0:512], bhT[:, hc:hc + 1],
                fc2T[:, hc * D + n0: hc * D + n0 + 512],
                start=(hc == 0), stop=(hc == HC4 - 1), inc=(1 if hc == HC4 - 1 else 0))
        t_ch = tp.n
        wait("v", tp, t_ch)
        vop("tensor_copy", crowA[0:1, n0:n0 + 512], P2[0:1, 0:512])
    vop("tensor_tensor", crowA[0:1, 0:D], crowA[0:1, 0:D], f2row[0:1, :], ALU.add)
    vop("tensor_tensor", crowA[0:1, 0:D], crowA[0:1, 0:D], grow2[0:1, :], ALU.mult)
    v_c2f = vop("tensor_copy", c2f_b[0:1, :], crowA[0:1, 0:D])
    wait("t", g_pa, g_pa_all)
    top("matmul", P4[:, 0:512], ones_b[0:1, 0:128], growb2[0:1, 0:512], start=True, stop=True)
    t_bg2 = top("matmul", P4[:, 512:1024], ones_b[0:1, 0:128], growb2[0:1, 512:1024],
                start=True, stop=True, inc=1)
    wait("v", tp, t_bg2)
    vop("tensor_copy", bcg2[:], P4[:, 0:1024])
    for hc in range(HC4):
        vop("tensor_tensor", fc2T[:, hc * D:(hc + 1) * D], fc2T[:, hc * D:(hc + 1) * D],
            bcg2[:], ALU.mult)
    v_f2done = vp.n

    x1t_free = [None, None, None]
    zb_free = [None, None]
    zT_free = [None, None]
    hb_free = [None, None]
    hT_free = [None, None]
    sst_free = [None]
    p4h_free = [(tp, t_bg2), (tp, t_bg2)]
    p2_free_pa = None
    p1a_free = None
    p1b_free = None
    ps_trA = P1a[:].bitcast(BF16)
    ps_trB = P1b[:].bitcast(BF16)

    pa_fold_done = [(vp, v_f2done), (vp, v_c1f), (vp, v_c2f)]
    n_t = n_tiles
    v_zb_t = [0] * (n_t + 1)
    v_hq_t = [0] * n_t
    s_zT_t = [0] * n_t
    s_hb_t = [0] * n_t
    s_hT_t = [0] * n_t
    t_ztr_t = [0] * n_t
    t_fc1_t = [[0] * 4 for _ in range(n_t)]
    t_fc2_t = [0] * n_t
    v_s_t = [0] * n_t
    ld_x1_t = [0] * n_t

    wmark("y", pa_fold_done)
    ld_x1_t[0] = dma("y", ch_l1[0], x1t[0], x1_d[0:128, :])
    ld_x1_t[1] = dma("y", ch_l1[1], x1t[1], x1_d[128:256, :])
    wait("v", ch_l1[0], ld_x1_t[0])
    wmark("v", pa_fold_done)
    v_zb_t[0] = vop("tensor_scalar", zb[0][:], x1t[0][:], zmean[:, 0:1], zrstd[:, 0:1],
                    ALU.subtract, ALU.mult)

    for t in range(n_t + 1):
        cur = t if t < n_t else None
        prv = t - 1 if t >= 1 else None
        if 2 <= t + 1 < n_t:
            sl = (t + 1) % 3
            wmark("y", x1t_free[sl])
            ld_x1_t[t + 1] = dma("y", ch_l1[sl], x1t[sl],
                                 x1_d[(t + 1) * 128:(t + 2) * 128, :])
        if cur is not None:
            par = cur % 2
            wait("t", vp, v_zb_t[cur])
            wmark("t", p1b_free)
            for ic in range(DC):
                top("transpose", ps_trB[:, ic * 128:(ic + 1) * 128],
                    zb[par][:, ic * 128:(ic + 1) * 128], eye_b[:],
                    inc=(1 if ic == DC - 1 else 0))
            t_ztr_t[cur] = tp.n
            zb_free[par] = (tp, t_ztr_t[cur])
            wait("s", tp, t_ztr_t[cur])
            wmark("s", zT_free[par])
            s_zT_t[cur] = sop("activation", zT[par][:], ps_trB[:, 0:1024], AF.Copy)
            p1b_free = (sp, s_zT_t[cur])
            wait("t", sp, s_zT_t[cur])
            if cur == 0:
                wait("t", vp, v_c1f)
            for q in range(4):
                qh = q % 2
                h0 = q * 1024
                PQ = P4[:, qh * 1024:(qh + 1) * 1024]
                wmark("t", p4h_free[qh])
                for ic in range(DC):
                    for nn in range(2):
                        n0 = nn * 512
                        top("matmul", PQ[:, n0:n0 + 512],
                            zT[par][:, ic * 128:(ic + 1) * 128],
                            fc1T[:, ic * H4 + h0 + n0: ic * H4 + h0 + n0 + 512],
                            start=(ic == 0), stop=False)
                for nn in range(2):
                    n0 = nn * 512
                    top("matmul", PQ[:, n0:n0 + 512], ones_b[0:1, 0:128],
                        c1f_b[0:1, h0 + n0: h0 + n0 + 512], start=False, stop=True,
                        inc=(1 if nn == 1 else 0))
                t_fc1_t[cur][q] = tp.n
                wait("s", tp, t_fc1_t[cur][q])
                if q == 0:
                    wmark("s", hb_free[par])
                s_g = sop("activation", hb[par][:, h0:h0 + 1024], PQ[:], AF.Gelu)
                p4h_free[qh] = (sp, s_g)
                if q == 3:
                    s_hb_t[cur] = s_g
            zT_free[par] = (tp, t_fc1_t[cur][3])
        if prv is not None:
            par1 = prv % 2
            wait("v", sp, s_hb_t[prv])
            for ch in range(8):
                vop("bn_stats", nscr2[:, ch * 6:(ch + 1) * 6],
                    hb[par1][:, ch * 512:(ch + 1) * 512], drain=(ch == 7))
            vop("bn_aggr", nscr[:, 8:10], nscr2[:, 0:48])
            vop("tensor_scalar", nscr[:, 0:1], nscr[:, 9:10], EPS, None, ALU.add)
            newton_rstd(nscr[:, 0:1], nscr[:, 10:11], nscr[:, 2:3], nscr[:, 3:4], nscr[:, 4:5])
            v_hq_t[prv] = vop("tensor_scalar", hb[par1][:], hb[par1][:],
                              nscr[:, 8:9], nscr[:, 10:11], ALU.subtract, ALU.mult)
        if t + 1 < n_t:
            nxt = t + 1
            wait("v", ch_l1[nxt % 3 if nxt >= 2 else nxt], ld_x1_t[nxt])
            wmark("v", zb_free[nxt % 2])
            v_zb_t[nxt] = vop("tensor_scalar", zb[nxt % 2][:], x1t[nxt % 3][:],
                              zmean[:, nxt:nxt + 1], zrstd[:, nxt:nxt + 1],
                              ALU.subtract, ALU.mult)
        if prv is not None:
            par1 = prv % 2
            wait("t", vp, v_hq_t[prv])
            for rnd in range(4):
                PB = ps_trA if rnd % 2 == 0 else ps_trB
                wmark("t", p1a_free if rnd % 2 == 0 else p1b_free)
                for j in range(8):
                    hc = rnd * 8 + j
                    top("transpose", PB[:, j * 128:(j + 1) * 128],
                        hb[par1][:, hc * 128:(hc + 1) * 128], eye_b[:],
                        inc=(1 if j == 7 else 0))
                t_htr = tp.n
                wait("s", tp, t_htr)
                if rnd == 0:
                    wmark("s", hT_free[par1])
                s_hT = sop("activation", hT[par1][:, rnd * 1024:(rnd + 1) * 1024],
                           PB[:, 0:1024], AF.Copy)
                if rnd % 2 == 0:
                    p1a_free = (sp, s_hT)
                else:
                    p1b_free = (sp, s_hT)
            s_hT_t[prv] = sp.n
            hb_free[par1] = (tp, tp.n)
            wait("t", sp, s_hT_t[prv])
            wmark("t", p2_free_pa)
            if prv == 0:
                wait("t", vp, v_c2f)
            for hc in range(HC4):
                for nn in range(2):
                    n0 = nn * 512
                    top("matmul", P2[:, n0:n0 + 512], hT[par1][:, hc * 128:(hc + 1) * 128],
                        fc2T[:, hc * D + n0: hc * D + n0 + 512],
                        start=(hc == 0), stop=False)
            for nn in range(2):
                n0 = nn * 512
                top("matmul", P2[:, n0:n0 + 512], ones_b[0:1, 0:128], c2f_b[0:1, n0:n0 + 512],
                    start=False, stop=True, inc=(1 if nn == 1 else 0))
            t_fc2_t[prv] = tp.n
            hT_free[par1] = (tp, t_fc2_t[prv])
            wait("v", tp, t_fc2_t[prv])
            wmark("v", sst_free[0])
            v_s_t[prv] = vop("scalar_tensor_tensor", sst[:], P2[:], 1.0,
                             x1t[prv % 3][:], ALU.mult, ALU.add)
            p2_free_pa = (vp, v_s_t[prv])
            x1t_free[prv % 3] = (vp, v_s_t[prv])
            wait("y", vp, v_s_t[prv])
            st_s = dma("y", ch_ss[0], s_d[prv * 128:(prv + 1) * 128, :], sst[:])
            sst_free[0] = (ch_ss[0], st_s)

    pa_v, pa_s, pa_t = vp.n, sp.n, tp.n
    al.pop()

    al.push()
    xnT_sup = al.buf(DC * 1024, BF16)
    acc = al.buf(8 * D, F32)
    ehT = [al.buf(HEC * 512, BF16) for _ in range(2)]
    wslot = [al.buf(16 * 1024, BF16) for _ in range(3)]
    wT_buf = al.buf(n_tiles * 128, BF16, parts=8)
    eb2b = al.buf(D, BF16, parts=8)
    sf = [al.buf(D, BF16) for _ in range(2)]
    gf = [al.buf(D, BF16) for _ in range(2)]
    xf = [al.buf(D, F32) for _ in range(2)]

    barrier(("v", "s", "t"), pa_v, pa_s, pa_t)
    wait("y", vp, pa_v); wait("y", sp, pa_s); wait("y", tp, pa_t)
    wmark("y", [(ch_ss[0], ch_ss[0].n), (ch_ss[1], ch_ss[1].n),
                (ch_l1[0], ch_l1[0].n), (ch_l1[1], ch_l1[1].n), (ch_l1[2], ch_l1[2].n)])
    wait("y", g_ew, g_ew_all)
    ld_wt = dma("y", ch_wt, wT_buf[:], wT_d[:])
    wait("g", vp, pa_v); wait("g", sp, pa_s); wait("g", tp, pa_t)
    ld_eb2 = dma("g", g_eb2, eb2b[:], eb2[:])

    S1 = lambda k: (0, 2, 1)[k % 3]
    S2 = lambda k: (1, 0, 2)[k % 3]
    slot_free = [None, None, None]
    ps_eh_free = [None, None]
    ps_eo_free = [None, None]
    ps_init_free_h = [None]
    ehT_free = [None, None]
    acc_free = None
    xnT_sup_free = None
    sfxf_free = [None, None, None, None]
    gf_free = [None, None]
    ps_eo = [P2, P4[:, 0:1024]]
    ps_init = P4[:, 1024:2048]
    out_last = 0

    def load_slot(slot, which, e_idx):
        wmark("y", slot_free[slot])
        c = DC if which == 1 else HEC
        src = ew1bf_d if which == 1 else ew2bf_d
        return dma("y", ch_w[slot], wslot[slot].rearrange("p (c n) -> p c n", c=c),
                   src[e_idx].rearrange("(c p) n -> p c n", p=128))

    n_exp_total = n_supers * NE
    ld_w1v = {0: load_slot(S1(0), 1, 0)}
    ld_w2v = {0: load_slot(S2(0), 2, 0)}
    ld_xnT_v = {}
    wmark("y", xnT_sup_free)
    ld_xnT_v[0] = dma("y", ch_xt, xnT_sup.rearrange("p (ic n) -> p ic n", ic=DC),
                      xnT_d[0].rearrange("ic p n -> p ic n"))

    for st in range(n_supers):
        ld_xnT = ld_xnT_v[st]
        wmark("y", acc_free[0] if acc_free else None)
        ld_acc_a = dma("y", ch_ac, acc.rearrange("p (t n) -> p t n", t=8)[:, 0:4, :],
                       xn_d[st * 1024:st * 1024 + 512, :].rearrange("(t p) n -> p t n", p=128))
        wmark("y", acc_free[1] if acc_free else None)
        ld_acc_b = dma("y", ch_ac2, acc.rearrange("p (t n) -> p t n", t=8)[:, 4:8, :],
                       xn_d[st * 1024 + 512:(st + 1) * 1024, :].rearrange("(t p) n -> p t n", p=128))
        def emit_init_tile(t8):
            tile = st * 8 + t8
            if st == 0 and t8 == 0:
                wait("t", ch_wt, ld_wt)
                wait("t", g_eb2, ld_eb2)
            wmark("t", ps_init_free_h[0])
            for nn in range(2):
                n0 = nn * 512
                top("matmul", ps_init[:, n0:n0 + 512],
                    wT_buf[0:8, tile * 128:(tile + 1) * 128],
                    eb2b[0:8, n0:n0 + 512], start=True, stop=True,
                    inc=(1 if nn == 1 else 0))
            t_init = tp.n
            wait("v", tp, t_init)
            if t8 == 0:
                wait("v", ch_ac, ld_acc_a)
            if t8 == 4:
                wait("v", ch_ac2, ld_acc_b)
            v_init = vop("scalar_tensor_tensor", acc[:, t8 * D:(t8 + 1) * D], ps_init,
                         1.0, acc[:, t8 * D:(t8 + 1) * D], ALU.mult, ALU.add)
            ps_init_free_h[0] = (vp, v_init)
        for e in range(NE):
            k = st * 8 + e
            i1, i2 = S1(k), S2(k)
            ld_w1, ld_w2 = ld_w1v[k], ld_w2v[k]
            if k + 1 < n_exp_total:
                ld_w1v[k + 1] = load_slot(S1(k + 1), 1, (k + 1) % NE)
            for hh in range(2):
                hpar = hh
                for hc in range(HEC):
                    pp = hc % 2
                    PE_ = P1a if pp == 0 else P1b
                    wmark("t", ps_eh_free[pp])
                    if hc == 0 and hh == 0:
                        wait("t", ch_w[i1], ld_w1)
                        if e == 0:
                            wait("t", ch_xt, ld_xnT)
                    for ic in range(DC):
                        top("matmul", PE_[:],
                            wslot[i1][:, ic * HE + hc * 128: ic * HE + (hc + 1) * 128],
                            xnT_sup[:, ic * 1024 + hh * 512: ic * 1024 + hh * 512 + 512],
                            start=(ic == 0), stop=(ic == DC - 1), inc=(1 if ic == DC - 1 else 0))
                    t_mm1 = tp.n
                    wait("s", tp, t_mm1)
                    if hc == 0:
                        wmark("s", ehT_free[hpar])
                        if st == 0 and e == 0 and hh == 0:
                            wait("s", vp, v_eb1)
                    s_eh = sop("activation", ehT[hpar][:, hc * 512:(hc + 1) * 512], PE_[:],
                               AF.Silu, bias=eb1T[:, hc * 8 + e: hc * 8 + e + 1])
                    ps_eh_free[pp] = (sp, s_eh)
                    if e == 1 and hc % 4 == 3:
                        emit_init_tile((hh * 16 + hc) // 4)
                if hh == 1:
                    slot_free[i1] = (tp, tp.n)
                    if k + 1 < n_exp_total:
                        ld_w2v[k + 1] = load_slot(S2(k + 1), 2, (k + 1) % NE)
                s_eh_done = sp.n
                wait("t", sp, s_eh_done)
                if hh == 0:
                    wait("t", ch_w[i2], ld_w2)
                for tl in range(4):
                    t8 = hh * 4 + tl
                    tile = st * 8 + t8
                    pp = tl % 2
                    wmark("t", ps_eo_free[pp])
                    for hc in range(HEC):
                        for nn in range(2):
                            n0 = nn * 512
                            top("matmul", ps_eo[pp][:, n0:n0 + 512],
                                ehT[hpar][:, hc * 512 + tl * 128: hc * 512 + (tl + 1) * 128],
                                wslot[i2][:, hc * 1024 + n0: hc * 1024 + n0 + 512],
                                start=(hc == 0), stop=(hc == HEC - 1),
                                inc=(1 if (hc == HEC - 1 and nn == 1) else 0))
                    t_mm2 = tp.n
                    wait("v", tp, t_mm2)
                    if e == 0 and tl == 0:
                        wait("v", ch_ac if hh == 0 else ch_ac2, ld_acc_a if hh == 0 else ld_acc_b)
                    v_acc = vop("scalar_tensor_tensor", acc[:, t8 * D:(t8 + 1) * D], ps_eo[pp],
                                w_buf[:, tile * 8 + e: tile * 8 + e + 1],
                                acc[:, t8 * D:(t8 + 1) * D], ALU.mult, ALU.add)
                    ps_eo_free[pp] = (vp, v_acc)
                ehT_free[hpar] = (tp, tp.n)
            slot_free[i2] = (tp, tp.n)
        v_experts, t_experts, s_experts = vp.n, tp.n, sp.n
        xnT_sup_free = (tp, t_experts)
        if st + 1 < n_supers:
            wmark("y", xnT_sup_free)
            ld_xnT_v[st + 1] = dma("y", ch_xt, xnT_sup.rearrange("p (ic n) -> p ic n", ic=DC),
                                   xnT_d[st + 1].rearrange("ic p n -> p ic n"))
        for t8 in range(8):
            fpar = t8 % 2
            wmark("g", sfxf_free[fpar])
            ld_s = dma("g", ch_sf[fpar], sf[fpar],
                       s_d[(st * 8 + t8) * 128:(st * 8 + t8 + 1) * 128, :])
            wmark("y", sfxf_free[2 + fpar])
            ld_xx = dma("y", ch_xf[fpar], xf[fpar],
                        x_d[(st * 8 + t8) * 128:(st * 8 + t8 + 1) * 128, :])
            if t8 == 0:
                wait("s", vp, v_experts)
            wmark("s", gf_free[fpar])
            s_g = sop("activation", gf[fpar], acc[:, t8 * D:(t8 + 1) * D], AF.Silu)
            wait("v", sp, s_g)
            wait("v", ch_sf[fpar], ld_s)
            vop("tensor_tensor", acc[:, t8 * D:(t8 + 1) * D], gf[fpar], sf[fpar], ALU.mult)
            gf_free[fpar] = (vp, vp.n)
            sfxf_free[fpar] = (vp, vp.n)
            wait("v", ch_xf[fpar], ld_xx)
            v_out = vop("tensor_tensor", acc[:, t8 * D:(t8 + 1) * D],
                        acc[:, t8 * D:(t8 + 1) * D], xf[fpar], ALU.add)
            sfxf_free[2 + fpar] = (vp, v_out)
            wait("g", vp, v_out)
            och = g_out if t8 < 4 else g_outb
            out_last = dma("g", och, out_d[(st * 8 + t8) * 128:(st * 8 + t8 + 1) * 128, :],
                           acc[:, t8 * D:(t8 + 1) * D])
            if t8 == 3:
                out_a = out_last
        acc_free = [(g_out, out_a), (g_outb, out_last)]

    wait("g", g_out, g_out.n)
    wait("g", g_outb, g_outb.n)
    al.pop()

    with nc.Block() as blk:
        @blk.sync
        def _(h):
            for fn in sched["y"]:
                fn(h)

        @blk.gpsimd
        def _(h):
            for fn in sched["g"]:
                fn(h)

        @blk.vector
        def _(h):
            for fn in sched["v"]:
                fn(h)

        @blk.scalar
        def _(h):
            for fn in sched["s"]:
                fn(h)

        @blk.tensor
        def _(h):
            for fn in sched["t"]:
                fn(h)

    es.close()
    return {e: len(sched[e]) for e in sched}

from concourse.bass_utils import run_bass_kernel_spmd
import ml_dtypes

N_CORES = 8
T_CORE = 4096

_WMAP = {
    "ln1_g": "L0_attn_ln1_g", "ln1_b": "L0_attn_ln1_b",
    "proj_w": "L0_attn_proj_w", "proj_b": "L0_attn_proj_b",
    "ln2_g": "L0_attn_ln2_g", "ln2_b": "L0_attn_ln2_b",
    "fc1_w": "L0_attn_fc1_w", "fc1_b": "L0_attn_fc1_b",
    "lnh_g": "L0_attn_lnh_g", "lnh_b": "L0_attn_lnh_b",
    "fc2_w": "L0_attn_fc2_w", "fc2_b": "L0_attn_fc2_b",
    "gamma1": "L0_attn_gamma1", "gamma2": "L0_attn_gamma2",
    "rw1": "L0_ffn_rw1", "rb1": "L0_ffn_rb1",
    "rw2": "L0_ffn_rw2", "rb2": "L0_ffn_rb2",
    "ew1": "L0_ffn_ew1", "eb1": "L0_ffn_eb1",
    "ew2": "L0_ffn_ew2", "eb2": "L0_ffn_eb2",
}


def kernel(**inputs):
    x = np.ascontiguousarray(np.asarray(inputs["L0_attn_x"], np.float32).reshape(-1, 1024))
    assert x.shape[0] == N_CORES * T_CORE
    shards = np.split(x, N_CORES, axis=0)

    base = {k: np.ascontiguousarray(np.asarray(inputs[v], np.float32)) for k, v in _WMAP.items()}
    base["eye_b"] = np.eye(128, dtype=ml_dtypes.bfloat16)
    base["eye_f"] = np.eye(128, dtype=np.float32)

    nc = bass.Bass("TRN2", target_bir_lowering=False)
    build(nc, T=T_CORE)

    in_maps = [dict(base, x=shards[c]) for c in range(N_CORES)]
    res = run_bass_kernel_spmd(nc, in_maps, core_ids=list(range(N_CORES)))
    out = np.concatenate([np.asarray(res.results[c]["out"], np.float32) for c in range(N_CORES)], axis=0)
    return out.reshape(4, 8192, 1024)


# version 19
# speedup vs baseline: 1.1570x; 1.1570x over previous
import numpy as np
import concourse.bass as bass
import concourse.mybir as mybir
from contextlib import ExitStack

F32 = mybir.dt.float32
BF16 = mybir.dt.bfloat16
AF = mybir.ActivationFunctionType
ALU = mybir.AluOpType
AX = mybir.AxisListType

D, DC = 1024, 8
H4, HC4 = 4096, 32
HE, HEC = 2048, 16
NE = 8
EPS = 1e-5
NEWTON_LINES = (4.0, 1.0, 0.25, 0.0625)
NEWTON_ITERS = 3


def build(nc, T=4096):
    assert T % 1024 == 0
    n_tiles, n_groups, n_supers = T // 128, T // 512, T // 1024

    def din(name, shape, dt=F32):
        return nc.dram_tensor(name, shape, dt, kind="ExternalInput").ap()

    x_d = din("x", [T, D])
    ln1_g, ln1_b = din("ln1_g", [D]), din("ln1_b", [D])
    proj_w, proj_b = din("proj_w", [D, D]), din("proj_b", [D])
    ln2_g, ln2_b = din("ln2_g", [D]), din("ln2_b", [D])
    fc1_w, fc1_b = din("fc1_w", [H4, D]), din("fc1_b", [H4])
    lnh_g, lnh_b = din("lnh_g", [H4]), din("lnh_b", [H4])
    fc2_w, fc2_b = din("fc2_w", [D, H4]), din("fc2_b", [D])
    gamma1, gamma2 = din("gamma1", [D]), din("gamma2", [D])
    rw1, rb1 = din("rw1", [D, D]), din("rb1", [D])
    rw2, rb2 = din("rw2", [D, NE]), din("rb2", [NE])
    ew1, eb1 = din("ew1", [NE, D, HE]), din("eb1", [NE, HE])
    ew2, eb2 = din("ew2", [NE, HE, D]), din("eb2", [NE, D])
    eye_b_d = din("eye_b", [128, 128], BF16)
    eye_f_d = din("eye_f", [128, 128], F32)
    out_d = nc.dram_tensor("out", [T, D], F32, kind="ExternalOutput").ap()

    pbf_d = nc.dram_tensor("pbf", [D, D], BF16, kind="Internal").ap()
    f1bf_d = nc.dram_tensor("f1bf", [H4, D], BF16, kind="Internal").ap()
    f2bf_d = nc.dram_tensor("f2bf", [D, H4], BF16, kind="Internal").ap()
    ew1bf_d = nc.dram_tensor("ew1bf", [NE, D, HE], BF16, kind="Internal").ap()
    ew2bf_d = nc.dram_tensor("ew2bf", [NE, HE, D], BF16, kind="Internal").ap()
    xn_d = nc.dram_tensor("xn_sc", [T, D], F32, kind="Internal").ap()
    xnT_d = nc.dram_tensor("xnT_sc", [n_supers, DC, 128, 1024], BF16, kind="Internal").ap()
    x1_d = nc.dram_tensor("x1_sc", [T, D], F32, kind="Internal").ap()
    s_d = nc.dram_tensor("s_sc", [T, D], F32, kind="Internal").ap()
    wT_d = nc.dram_tensor("wT_sc", [8, T // 128 * 128], BF16, kind="Internal").ap()

    es = ExitStack()
    ARENA_BYTES = 207 * 1024
    arena = es.enter_context(nc.sbuf_tensor("arena", [128, ARENA_BYTES], mybir.dt.int8))[:]

    class Alloc:
        def __init__(self):
            self.off, self.marks = 0, []
        def buf(self, cols, dt=F32, parts=128):
            isz = 4 if dt == F32 else 2
            self.off = (self.off + 31) & ~31
            ap = arena[0:parts, self.off:self.off + cols * isz].bitcast(dt)
            self.off += cols * isz
            assert self.off <= ARENA_BYTES, f"arena overflow {self.off}"
            return ap
        def push(self):
            self.marks.append(self.off)
        def pop(self):
            self.off = self.marks.pop()
    al = Alloc()

    P4 = es.enter_context(nc.psum_tensor("P4", [128, 2048], F32))[:]
    P2 = es.enter_context(nc.psum_tensor("P2", [128, 1024], F32))[:]
    P1a = es.enter_context(nc.psum_tensor("P1a", [128, 512], F32))[:]
    P1b = es.enter_context(nc.psum_tensor("P1b", [128, 512], F32))[:]
    ps_tr_bf = P4[:, 0:512].bitcast(BF16)
    ps_lT = P4[0:8, 512:1024]
    ps_lt2 = P4[:, 1024:1056]
    ps_wT = P4[0:8, 1536:1664]

    sched = {e: [] for e in ("y", "g", "v", "s", "t")}

    class Sem:
        def __init__(self, name):
            self.h = es.enter_context(nc.semaphore(name))
            self.n = 0
            self.name = name
    vp, sp, tp = Sem("vp"), Sem("sp"), Sem("tp")
    g_pbf, g_f1, g_f2 = Sem("g_pbf"), Sem("g_f1"), Sem("g_f2")
    g_small, g_pa, g_ew, g_out = Sem("g_small"), Sem("g_pa"), Sem("g_ew"), Sem("g_out")
    lm0, lm0b, lm1, lm_pj = Sem("lm0"), Sem("lm0b"), Sem("lm1"), Sem("lm_pj")
    ch_xg = [Sem("ch_xg0"), Sem("ch_xg1")]
    ch_sxn = [Sem("ch_sxn0"), Sem("ch_sxn1")]
    ch_sxT = [Sem("ch_sxT0"), Sem("ch_sxT1")]
    ch_sx1 = [Sem("ch_sx10"), Sem("ch_sx11")]
    ch_l1 = [Sem("ch_l10"), Sem("ch_l11"), Sem("ch_l12")]
    ch_ss = [Sem("ch_ss0"), Sem("ch_ss1")]
    ch_w = [Sem("ch_w0"), Sem("ch_w1"), Sem("ch_w2")]
    ch_xt, ch_ac, ch_ac2 = Sem("ch_xt"), Sem("ch_ac"), Sem("ch_ac2")
    ch_wt, g_eb2 = Sem("ch_wt"), Sem("g_eb2")
    s_f2, g_outb = Sem("s_f2"), Sem("g_outb")
    ch_row = Sem("ch_row")
    ch_sf = [Sem("ch_sf0"), Sem("ch_sf1")]
    ch_xf = [Sem("ch_xf0"), Sem("ch_xf1")]

    def wait(e, sem, val):
        if val <= 0:
            return
        assert val <= sem.n, f"wait on future {val}>{sem.n} ({sem.name})"
        sched[e].append(lambda h, s=sem.h, v=val: h.wait_ge(s, v))

    def wmark(e, mark):
        if not mark:
            return
        if isinstance(mark, list):
            for m in mark:
                wmark(e, m)
        else:
            wait(e, mark[0], mark[1])

    def dma(e, ch, out, in_, slow=False, transpose=False):
        ch.n += 16
        def fn(h, out=out, in_=in_, slow=slow, tr=transpose, s=ch.h):
            if slow:
                with nc.allow_non_contiguous_dma(reason="tiny chunked vector load"):
                    h.dma_start(out=out, in_=in_).then_inc(s, 16)
            elif tr:
                h.dma_start(out=out, in_=in_, transpose=True).then_inc(s, 16)
            else:
                h.dma_start(out=out, in_=in_).then_inc(s, 16)
        sched[e].append(fn)
        return ch.n

    def vop(name, *args, drain=True, **kw):
        vp.n += 1
        def fn(h, name=name, args=args, kw=kw, drain=drain):
            i = getattr(h, name)(*args, **kw)
            if drain:
                h.drain().then_inc(vp.h, 1)
            else:
                i.then_inc(vp.h, 1)
        sched["v"].append(fn)
        return vp.n

    def sop(name, *args, drain=True, **kw):
        sp.n += 1
        def fn(h, name=name, args=args, kw=kw, drain=drain):
            i = getattr(h, name)(*args, **kw)
            if drain:
                h.drain().then_inc(sp.h, 1)
            else:
                i.then_inc(sp.h, 1)
        sched["s"].append(fn)
        return sp.n

    def top(name, *args, inc=0, **kw):
        tp.n += inc
        def fn(h, name=name, args=args, kw=kw, inc=inc):
            i = getattr(h, name)(*args, **kw)
            if inc:
                i.then_inc(tp.h, inc)
        sched["t"].append(fn)
        return tp.n

    def barrier(engines, vm, sm, tm):
        for e in engines:
            wait(e, vp, vm); wait(e, sp, sm); wait(e, tp, tm)

    eye_b = al.buf(128, BF16)
    eye_f = al.buf(128, F32)
    ones_b = al.buf(512, BF16, parts=1)
    w_buf = al.buf(n_tiles * 8, F32)
    eb1T = al.buf(HEC * NE, F32)
    rb2b = al.buf(NE, BF16, parts=1)
    nscr = al.buf(16, F32)
    nscr2 = al.buf(48, F32)
    zmean = al.buf(n_tiles, F32)
    zrstd = al.buf(n_tiles, F32)

    def newton_rstd(tcol, outcol, s0, s1, s2, lines=NEWTON_LINES, iters=NEWTON_ITERS):
        first = True
        for aa in lines:
            vop("tensor_scalar", s1, tcol, aa * aa, 1.0 / (2 * aa), ALU.add, ALU.mult)
            if first:
                vop("tensor_copy", s0, s1); first = False
            else:
                vop("tensor_tensor", s0, s0, s1, ALU.min)
        vop("tensor_scalar", s2, tcol, 0.5, None, ALU.mult)
        for _ in range(iters):
            vop("reciprocal", s1, s0)
            vop("tensor_tensor", s1, s2, s1, ALU.mult)
            vop("scalar_tensor_tensor", s0, s0, 0.5, s1, ALU.mult, ALU.add)
        return vop("reciprocal", outcol, s0)

    rowst = {"tp_mark": 0, "vp_mark": 0}

    def load_vecT(vec_d_ap, dest, nch, vrow):
        wait("t", lm0, lm0_all)
        for r0 in range(0, nch, 8):
            k = min(8, nch - r0)
            wait("y", tp, rowst["tp_mark"])
            lr = dma("y", ch_row, vrow[0:1, 0:k * 128], vec_d_ap[None, r0 * 128:(r0 + k) * 128])
            wait("t", ch_row, lr)
            wait("t", vp, rowst["vp_mark"])
            for j in range(k):
                top("transpose", P1b[0:128, j:j + 1], vrow[0:1, j * 128:(j + 1) * 128],
                    eye_f[0:1, 0:1], inc=(1 if j == k - 1 else 0))
            rowst["tp_mark"] = tp.n
            wait("v", tp, rowst["tp_mark"])
            rowst["vp_mark"] = vop("tensor_copy", dest[:, r0:r0 + k], P1b[0:128, 0:k])

    dma("g", g_pbf, pbf_d[:], proj_w[:])

    al.push()
    projT = al.buf(DC * D, BF16)
    rw1b = al.buf(DC * D, BF16)
    rw2b = al.buf(DC * NE, BF16)
    xg = [al.buf(4 * D, F32) for _ in range(2)]
    xnf = [al.buf(4 * D, F32) for _ in range(2)]
    xnbb = al.buf(4 * D, BF16)
    xnT_g = [al.buf(8 * 512, BF16) for _ in range(2)]
    uT_g = [al.buf(8 * 512, BF16) for _ in range(2)]
    x1s = [al.buf(D, F32) for _ in range(2)]
    lT_s = [al.buf(512, F32, parts=8) for _ in range(2)]
    l_s = al.buf(32, F32)
    gw = al.buf(128, F32)
    p0stat = al.buf(128, F32)
    bc_c1 = al.buf(D, F32)
    rb1T = al.buf(DC, F32)
    wT_buf = al.buf(n_tiles * 128, BF16, parts=8)
    g1T = al.buf(DC, F32)
    b1T = al.buf(DC, BF16)
    grow1 = al.buf(D, F32, parts=1)
    growb = al.buf(D, BF16, parts=1)
    prow = al.buf(D, F32, parts=1)
    crow = al.buf(D, F32, parts=1)
    crow_b = al.buf(D, BF16, parts=1)
    bcg = al.buf(D, BF16)
    eb1n = al.buf(HE, F32, parts=8)

    dma("g", g_small, rb2b[:], rb2[None, :])
    dma("g", g_small, rw1b.rearrange("p (c n) -> p c n", c=DC),
        rw1.rearrange("(c p) n -> p c n", p=128))
    dma("g", g_small, rw2b.rearrange("p (c n) -> p c n", c=DC),
        rw2.rearrange("(c p) n -> p c n", p=128))
    g_small_all = dma("g", g_small, growb[:], gamma1[None, :])
    g_ew_all = 0

    dma("y", lm0, eye_b[:], eye_b_d[:])
    dma("y", lm0, eye_f[:], eye_f_d[:])
    dma("y", lm0, grow1[:], gamma1[None, :])
    dma("y", lm0, prow[0:1, :], proj_b[None, :])
    lm0_all = lm0.n
    lm0b_all = dma("y", lm0b, eb1n[:], eb1[:])

    ld_x0_early = dma("y", ch_xg[0], xg[0].rearrange("p (s n) -> p s n", s=4),
                      x_d[0:512, :].rearrange("(s p) n -> p s n", p=128))
    wait("y", g_pbf, 16)
    pj_stage = xg[1].bitcast(BF16)
    lm_pj_all = dma("y", lm_pj, pj_stage.rearrange("p (s n) -> p s n", s=8),
                    pbf_d.rearrange("(s p) n -> p s n", p=128))
    wait("t", lm_pj, lm_pj_all)
    wait("t", lm0, lm0_all)
    pj_prev_evac = [0]
    for oc in range(8):
        wait("t", sp, pj_prev_evac[0])
        for ic in range(DC):
            top("transpose", ps_tr_bf[:, ic * 128:(ic + 1) * 128],
                pj_stage[:, oc * D + ic * 128: oc * D + (ic + 1) * 128], eye_b[:],
                inc=(1 if ic == DC - 1 else 0))
        wait("s", tp, tp.n)
        pj_prev_evac[0] = sop("activation",
                              projT.rearrange("p (ic n) -> p ic n", ic=DC)[:, :, oc * 128:(oc + 1) * 128],
                              ps_tr_bf[:].rearrange("p (ic n) -> p ic n", ic=DC), AF.Copy)
    s_pj_done = pj_prev_evac[0]
    pj_tr_done = tp.n
    load_vecT(ln1_g, g1T, DC, crow)
    load_vecT(rb1, rb1T, DC, crow)
    load_vecT(ln1_b, b1T, DC, crow)

    v_ones = vop("memset", ones_b[0:1, :], 1.0)
    wait("v", lm0, lm0_all)
    wait("v", sp, s_pj_done)
    for ic in range(DC):
        vop("tensor_scalar", projT[:, ic * D:(ic + 1) * D], projT[:, ic * D:(ic + 1) * D],
            g1T[:, ic:ic + 1], None, ALU.mult)
    v_p1 = vp.n
    wait("t", vp, v_p1)
    wait("t", g_small, g_small_all)
    for half in range(2):
        n0 = half * 512
        for ic in range(DC):
            top("matmul", P2[0:1, n0:n0 + 512], b1T[:, ic:ic + 1],
                projT[:, ic * D + n0: ic * D + n0 + 512],
                start=(ic == 0), stop=(ic == DC - 1),
                inc=(1 if (half == 1 and ic == DC - 1) else 0))
    t_c1 = tp.n
    wait("v", tp, t_c1)
    vop("tensor_tensor", crow[0:1, :], P2[0:1, 0:D], prow[0:1, :], ALU.add)
    vop("tensor_tensor", crow[0:1, :], crow[0:1, :], grow1[0:1, :], ALU.mult)
    v_c1r = vop("tensor_copy", crow_b[0:1, :], crow[0:1, :])
    wait("t", vp, v_c1r)
    top("matmul", P2[:, 0:512], ones_b[0:1, 0:128], crow_b[0:1, 0:512], start=True, stop=True)
    t_bc1 = top("matmul", P2[:, 512:1024], ones_b[0:1, 0:128], crow_b[0:1, 512:1024],
                start=True, stop=True, inc=1)
    wait("v", tp, t_bc1)
    v_bc1 = vop("tensor_copy", bc_c1[:], P2[:])
    top("matmul", P4[:, 0:512], ones_b[0:1, 0:128], growb[0:1, 0:512], start=True, stop=True)
    t_bg = top("matmul", P4[:, 512:1024], ones_b[0:1, 0:128], growb[0:1, 512:1024],
               start=True, stop=True, inc=1)
    wait("v", tp, t_bg)
    v_bcg = vop("tensor_copy", bcg[:], P4[:, 0:1024])
    for ic in range(DC):
        vop("tensor_tensor", projT[:, ic * D:(ic + 1) * D], projT[:, ic * D:(ic + 1) * D],
            bcg[:], ALU.mult)
    v_projdone = vp.n
    wait("t", lm0b, lm0b_all)
    for hc in range(HEC):
        top("transpose", P1a[0:128, hc * 8:(hc + 1) * 8], eb1n[0:8, hc * 128:(hc + 1) * 128],
            eye_f[0:8, 0:8], inc=(1 if hc == HEC - 1 else 0))
    t_eb1 = tp.n
    wait("v", tp, t_eb1)
    v_eb1 = vop("tensor_copy", eb1T[:], P1a[:, 0:HEC * NE])

    xg_free = [None, None]
    xnf_free = [None, None]
    xnT_free = [None, None]
    uT_free = [None, None]
    x1s_free = [None, None]
    p2_free = (vp, v_bc1)
    p1_free = [(vp, v_eb1), None]
    p4b0_free = (vp, v_bcg)
    p4b1_free = None
    p4b2_free = None
    p4b3_free = None
    lT_free = [None, None]


    w_ready = [0] * n_tiles
    wT_ready = [0] * n_tiles
    pend_lt2 = []
    pend_wT = []

    def flush_wT(force=False):
        nonlocal p4b3_free
        if not pend_wT or (len(pend_wT) < 2 and not force):
            return
        g, w_grp = pend_wT.pop(0)
        wslab = w_buf[:, g * 32:(g + 1) * 32]
        for t in range(4):
            tile = g * 4 + t
            wait("t", vp, w_grp)
            wmark("t", p4b3_free)
            t_wT = top("transpose", ps_wT[:], wslab[:, t * 8:(t + 1) * 8], eye_f[:], inc=1)
            wait("s", tp, t_wT)
            wT_ready[tile] = sop("activation", wT_buf[0:8, tile * 128:(tile + 1) * 128],
                                 ps_wT[:], AF.Copy)
            p4b3_free = (sp, wT_ready[tile])

    def flush_lt2():
        nonlocal p4b2_free, lT_free
        if not pend_lt2:
            return
        g, s_lT = pend_lt2.pop(0)
        wait("t", sp, s_lT)
        wmark("t", p4b2_free)
        for t in range(4):
            top("transpose", ps_lt2[:, t * 8:(t + 1) * 8], lT_s[g % 2][0:8, t * 128:(t + 1) * 128],
                eye_f[0:8, 0:8], inc=(1 if t == 3 else 0))
        t_lt2 = tp.n
        lT_free[g % 2] = (tp, t_lt2)
        wait("v", tp, t_lt2)
        v_l = vop("tensor_copy", l_s[:], ps_lt2[:])
        p4b2_free = (vp, v_l)
        wslab = w_buf[:, g * 32:(g + 1) * 32]
        vop("tensor_reduce", gw[:, 0:4], v3(l_s[:]), AX.X, ALU.max)
        vop("tensor_tensor", v3(gw[:, 32:64]), v3(l_s[:]), bc4(gw[:, 0:4]), ALU.is_equal)
        vop("scalar_tensor_tensor", v3(gw[:, 96:128]), v3(gw[:, 32:64]), -1e30, v3(l_s[:]),
            ALU.mult, ALU.add)
        vop("tensor_reduce", gw[:, 4:8], v3(gw[:, 96:128]), AX.X, ALU.max)
        vop("tensor_tensor", v3(gw[:, 64:96]), v3(l_s[:]), bc4(gw[:, 4:8]), ALU.is_equal)
        vop("tensor_tensor", gw[:, 8:12], gw[:, 0:4], gw[:, 4:8], ALU.subtract)
        v_dc = vop("tensor_scalar", gw[:, 12:16], gw[:, 8:12], 1e-30, None, ALU.max)
        wait("s", vp, v_dc)
        s_sv = sop("activation", gw[:, 16:20], gw[:, 12:16], AF.Silu)
        wait("v", sp, s_sv)
        vop("reciprocal", gw[:, 20:24], gw[:, 12:16])
        vop("tensor_tensor", gw[:, 24:28], gw[:, 16:20], gw[:, 20:24], ALU.mult)
        vop("tensor_scalar", gw[:, 28:32], gw[:, 24:28], -1.0, 1.0, ALU.mult, ALU.add)
        vop("tensor_tensor", v3(wslab), v3(gw[:, 32:64]), bc4(gw[:, 24:28]), ALU.mult, drain=False)
        vop("tensor_tensor", v3(gw[:, 96:128]), v3(gw[:, 64:96]), bc4(gw[:, 28:32]), ALU.mult)
        w_grp = vop("tensor_tensor", wslab, wslab, gw[:, 96:128], ALU.add)
        for t in range(4):
            w_ready[g * 4 + t] = w_grp
        pend_wT.append((g, w_grp))

    def v3(ap32):
        return ap32.rearrange("p (t e) -> p t e", e=8)

    def bc4(ap4):
        return ap4.rearrange("p (t o) -> p t o", o=1).broadcast_to((128, 4, 8))

    def _xv(par):
        o = par * 64
        blk = p0stat[:, o:o + 8].rearrange("p (t w) -> p t w", w=2)
        return dict(mean=blk[:, :, 0:1], var=blk[:, :, 1:2],
                    tx4=p0stat[:, o + 8:o + 12], rx4=p0stat[:, o + 12:o + 16],
                    t24=p0stat[:, o + 16:o + 20], r24=p0stat[:, o + 20:o + 24],
                    ns0=p0stat[:, o + 24:o + 28], ns1=p0stat[:, o + 28:o + 32],
                    aggr=lambda t: p0stat[:, o + 2 * t:o + 2 * t + 2])
    XV = [_xv(0), _xv(1)]
    z_mean = p0stat[:, 32:40].rearrange("p (t w) -> p t w", w=2)[:, :, 0:1]
    z_var = p0stat[:, 32:40].rearrange("p (t w) -> p t w", w=2)[:, :, 1:2]
    ns2 = p0stat[:, 40:44]
    tz4, rz4 = p0stat[:, 44:48], p0stat[:, 48:52]
    zaggr = lambda t: p0stat[:, 32 + 2 * t:34 + 2 * t]

    ld_x_t = [0] * (n_groups + 1)
    v_casts_t = [0] * n_groups
    v_xn_last_t = [0] * n_groups
    t_tr_last = [0]

    xpose_done = [0] * n_groups

    def emit_xpose(g):
        nonlocal p4b0_free
        par = g % 2
        wait("t", sp, v_casts_t[g])
        for t in range(4):
            wmark("t", p4b0_free)
            for ic in range(DC):
                top("transpose", ps_tr_bf[:, ic * 128:(ic + 1) * 128],
                    xnbb[:, t * D + ic * 128: t * D + (ic + 1) * 128], eye_b[:],
                    inc=(1 if ic == DC - 1 else 0))
            t_tr = tp.n
            wait("s", tp, t_tr)
            if t == 0:
                wmark("s", xnT_free[par])
            s_ev = sop("activation",
                       xnT_g[par].rearrange("p (ic n) -> p ic n", ic=DC)[:, :, t * 128:(t + 1) * 128],
                       ps_tr_bf[:].rearrange("p (ic n) -> p ic n", ic=DC), AF.Copy)
            p4b0_free = (sp, s_ev)
        t_tr_last[0] = tp.n
        xpose_done[g] = sp.n

    def emit_stats(g):
        par = g % 2
        X = XV[par]
        wait("v", ch_xg[par], ld_x_t[g])
        for t in range(4):
            xt = xg[par][:, t * D:(t + 1) * D]
            vop("bn_stats", nscr2[:, 0:6], xt[:, 0:512], drain=False)
            vop("bn_stats", nscr2[:, 6:12], xt[:, 512:1024])
            vop("bn_aggr", X["aggr"](t), nscr2[:, 0:12])
        vop("tensor_scalar", X["tx4"], X["var"], EPS, None, ALU.add)
        newton_rstd(X["tx4"], X["rx4"], X["ns0"], X["ns1"], ns2, lines=(1.0,), iters=2)
        vop("tensor_tensor", X["t24"], X["var"], X["rx4"], ALU.mult)
        vop("tensor_tensor", X["t24"], X["t24"], X["rx4"], ALU.mult)
        vop("tensor_scalar", X["t24"], X["t24"], EPS, None, ALU.add)
        newton_rstd(X["t24"], X["r24"], X["ns0"], X["ns1"], ns2, lines=(1.0,), iters=2)
        wmark("v", xnf_free[par])
        for t in range(4):
            vop("tensor_scalar", xnf[par][:, t * D:(t + 1) * D], xg[par][:, t * D:(t + 1) * D],
                X["mean"][:, t], X["rx4"][:, t:t + 1], ALU.subtract, ALU.mult,
                drain=(t == 3))
        v_xn_last_t[g] = vp.n
        wait("s", vp, v_xn_last_t[g])
        wait("s", tp, t_tr_last[0])
        v_casts_t[g] = sop("activation", xnbb[:], xnf[par][:], AF.Copy)

    ld_x_t[0] = ld_x0_early
    emit_stats(0)
    emit_xpose(0)

    for g in range(n_groups):
        par = g % 2
        if g == min(2, n_groups - 1):
            wait("g", sp, v_casts_t[g - 1])
            dma("g", g_f1, f1bf_d[:], fc1_w[:])
            dma("g", g_f2, f2bf_d[:], fc2_w[:])
        if g + 1 < n_groups:
            np_ = (g + 1) % 2
            if g == 0:
                wait("y", tp, pj_tr_done)
            wmark("y", xg_free[np_])
            ld_x_t[g + 1] = dma("y", ch_xg[np_], xg[np_].rearrange("p (s n) -> p s n", s=4),
                                x_d[(g + 1) * 512:(g + 2) * 512, :].rearrange("(s p) n -> p s n", p=128))
        if g > 0:
            emit_xpose(g)
        flush_lt2()
        v_xn_last = v_xn_last_t[g]
        v_xnT = xpose_done[g]
        wait("y", vp, v_xn_last)
        st_xn = dma("y", ch_sxn[par], xn_d[g * 512:(g + 1) * 512, :].rearrange("(s p) n -> p s n", p=128),
                    xnf[par].rearrange("p (s n) -> p s n", s=4))
        wait("y", sp, v_xnT)
        wait("t", sp, v_xnT)
        st_i, hf = g // 2, g % 2
        st_xnT = dma("y", ch_sxT[par],
                     xnT_d[st_i, :, :, hf * 512:(hf + 1) * 512].rearrange("ic p n -> p ic n"),
                     xnT_g[par].rearrange("p (ic n) -> p ic n", ic=DC))
        wait("t", sp, v_xnT)
        for oc in range(DC):
            ppar = oc % 2
            PU = P1a if ppar == 0 else P1b
            wmark("t", p1_free[ppar])
            for ic in range(DC):
                top("matmul", PU[:], rw1b[:, ic * D + oc * 128: ic * D + (oc + 1) * 128],
                    xnT_g[par][:, ic * 512:(ic + 1) * 512],
                    start=(ic == 0), stop=(ic == DC - 1), inc=(1 if ic == DC - 1 else 0))
            t_u = tp.n
            wait("s", tp, t_u)
            if g == 0 and oc == 0:
                wait("s", lm0, lm0_all)
            if oc == 0:
                wmark("s", uT_free[par])
            s_u = sop("activation", uT_g[par][:, oc * 512:(oc + 1) * 512], PU[:],
                      AF.Silu, bias=rb1T[:, oc:oc + 1])
            p1_free[ppar] = (sp, s_u)
        s_uT = sp.n
        wait("t", sp, s_uT)
        wmark("t", p4b1_free)
        for oc in range(DC):
            top("matmul", ps_lT[:], rw2b[:, oc * NE:(oc + 1) * NE],
                uT_g[par][:, oc * 512:(oc + 1) * 512], start=(oc == 0), stop=False)
        t_lg = top("matmul", ps_lT[:], rb2b[0:1, 0:NE], ones_b[0:1, 0:512],
                   start=False, stop=True, inc=1)
        uT_free[par] = (tp, t_lg)
        wait("s", tp, t_lg)
        wmark("s", lT_free[g % 2])
        s_lT = sop("activation", lT_s[g % 2][:], ps_lT[:], AF.Copy)
        p4b1_free = (sp, s_lT)
        pend_lt2.append((g, s_lT))
        for t in range(4):
            tpar = t % 2
            wmark("t", p2_free)
            if g == 0 and t == 0:
                wait("t", vp, v_projdone)
            for hn in range(2):
                n0 = hn * 512
                for ic in range(DC):
                    top("matmul", P2[:, n0:n0 + 512],
                        xnT_g[par][:, ic * 512 + t * 128: ic * 512 + (t + 1) * 128],
                        projT[:, ic * D + n0: ic * D + n0 + 512],
                        start=(ic == 0), stop=(ic == DC - 1),
                        inc=(1 if (hn == 1 and ic == DC - 1) else 0))
            t_proj = tp.n
            wait("v", tp, t_proj)
            wmark("v", x1s_free[tpar])
            vop("scalar_tensor_tensor", x1s[tpar][:], P2[:],
                XV[par]["r24"][:, t:t + 1], bc_c1[:], ALU.mult, ALU.add)
            v_x1 = vop("tensor_tensor", x1s[tpar][:],
                       x1s[tpar][:], xnf[par][:, t * D:(t + 1) * D], ALU.add)
            p2_free = (vp, v_x1)
            wait("y", vp, v_x1)
            st_x1 = dma("y", ch_sx1[tpar], x1_d[(g * 4 + t) * 128:(g * 4 + t + 1) * 128, :],
                        x1s[tpar][:])
            x1s_free[tpar] = (ch_sx1[tpar], st_x1)
            vop("bn_stats", nscr2[:, 0:6], x1s[tpar][:, 0:512], drain=False)
            vop("bn_stats", nscr2[:, 6:12], x1s[tpar][:, 512:1024])
            vop("bn_aggr", zaggr(t), nscr2[:, 0:12])
        vop("tensor_scalar", tz4, z_var, EPS, None, ALU.add)
        newton_rstd(tz4, rz4, XV[par]["ns0"], XV[par]["ns1"], ns2, lines=(1.0,), iters=3)
        vop("tensor_copy", zmean[:, g * 4:(g + 1) * 4], z_mean, drain=False)
        v_zst = vop("tensor_copy", zrstd[:, g * 4:(g + 1) * 4], rz4)
        v_x1done = vp.n
        xg_free[par] = (vp, v_xn_last)
        xnf_free[par] = [(vp, v_x1done), (ch_sxn[par], st_xn)]
        xnT_free[par] = [(tp, tp.n), (ch_sxT[par], st_xnT)]
        if g + 1 < n_groups:
            emit_stats(g + 1)
        flush_wT()

    flush_lt2()
    flush_wT(force=True)
    flush_wT(force=True)
    wait("y", sp, wT_ready[n_tiles - 1])
    st_wt = dma("y", ch_wt, wT_d[:], wT_buf[:])
    p0_v, p0_s, p0_t = vp.n, sp.n, tp.n
    al.pop()

    al.push()
    fc1T = al.buf(DC * H4, BF16)
    fc2T = al.buf(HC4 * D, BF16)
    c1f_b = al.buf(H4, BF16, parts=1)
    c2f_b = al.buf(D, BF16, parts=1)
    al.push()
    g2T = al.buf(DC, F32)
    ghT = al.buf(HC4, F32)
    b2T = al.buf(DC, BF16)
    bhT = al.buf(HC4, BF16)
    grow2 = al.buf(D, F32, parts=1)
    growb2 = al.buf(D, BF16, parts=1)
    f1row = al.buf(H4, F32, parts=1)
    f2row = al.buf(D, F32, parts=1)
    crowA = al.buf(H4, F32, parts=1)
    bcg2 = al.buf(D, BF16)
    al.pop()
    zb = [al.buf(D, BF16) for _ in range(2)]
    zT = [al.buf(D, BF16) for _ in range(2)]
    hb = [al.buf(H4, BF16) for _ in range(2)]
    hT = [al.buf(H4, BF16) for _ in range(2)]
    x1t = [al.buf(D, F32) for _ in range(3)]
    sst = al.buf(D, F32)

    p0_store_done = [(ch_sxn[0], ch_sxn[0].n), (ch_sxn[1], ch_sxn[1].n),
                     (ch_sxT[0], ch_sxT[0].n), (ch_sxT[1], ch_sxT[1].n),
                     (ch_sx1[0], ch_sx1[0].n), (ch_sx1[1], ch_sx1[1].n),
                     (ch_xg[0], ch_xg[0].n), (ch_xg[1], ch_xg[1].n),
                     (ch_wt, ch_wt.n)]
    wait("y", vp, p0_v); wait("y", sp, p0_s); wait("y", tp, p0_t)
    wmark("y", p0_store_done)
    barrier(("v", "s", "t"), p0_v, p0_s, p0_t)
    wait("y", g_f1, 16)
    for ic in range(DC):
        dma("y", lm1, fc1T[:, ic * H4:(ic + 1) * H4], f1bf_d[:, ic * 128:(ic + 1) * 128],
            transpose=True)
    wait("s", g_f2, 16)
    wait("s", vp, p0_v); wait("s", tp, p0_t)
    wmark("s", p0_store_done)
    for hc in range(HC4):
        dma("s", s_f2, fc2T[:, hc * D:(hc + 1) * D], f2bf_d[:, hc * 128:(hc + 1) * 128],
            transpose=True)
    dma("y", lm1, f1row[0:1, :], fc1_b[None, :])
    dma("y", lm1, f2row[0:1, :], fc2_b[None, :])
    lm1_all = dma("y", lm1, grow2[0:1, :], gamma2[None, :])
    load_vecT(ln2_g, g2T, DC, crowA[0:1, 0:D])
    load_vecT(lnh_g, ghT, HC4, crowA[0:1, 0:D])
    load_vecT(ln2_b, b2T, DC, crowA[0:1, 0:D])
    load_vecT(lnh_b, bhT, HC4, crowA[0:1, 0:D])
    wait("g", vp, p0_v); wait("g", sp, p0_s); wait("g", tp, p0_t)
    wmark("g", p0_store_done)
    g_pa_all = dma("g", g_pa, growb2[:], gamma2[None, :])

    wait("v", lm1, lm1_all)
    for ic in range(DC):
        vop("tensor_scalar", fc1T[:, ic * H4:(ic + 1) * H4], fc1T[:, ic * H4:(ic + 1) * H4],
            g2T[:, ic:ic + 1], None, ALU.mult)
    v_f1fold = vp.n
    wait("t", vp, v_f1fold)
    for ch in range(8):
        if ch > 0:
            wait("t", vp, vp.n)
        for ic in range(DC):
            top("matmul", P2[0:1, 0:512], b2T[:, ic:ic + 1],
                fc1T[:, ic * H4 + ch * 512: ic * H4 + ch * 512 + 512],
                start=(ic == 0), stop=(ic == DC - 1), inc=(1 if ic == DC - 1 else 0))
        t_ch = tp.n
        wait("v", tp, t_ch)
        vop("tensor_copy", crowA[0:1, ch * 512:(ch + 1) * 512], P2[0:1, 0:512])
    vop("tensor_tensor", crowA[0:1, :], crowA[0:1, :], f1row[0:1, :], ALU.add)
    v_c1f = vop("tensor_copy", c1f_b[0:1, :], crowA[0:1, :])
    wait("v", s_f2, s_f2.n)
    for hc in range(HC4):
        vop("tensor_scalar", fc2T[:, hc * D:(hc + 1) * D], fc2T[:, hc * D:(hc + 1) * D],
            ghT[:, hc:hc + 1], None, ALU.mult)
    v_f2fold = vp.n
    wait("t", vp, v_f2fold)
    for ch in range(2):
        n0 = ch * 512
        if ch > 0:
            wait("t", vp, vp.n)
        for hc in range(HC4):
            top("matmul", P2[0:1, 0:512], bhT[:, hc:hc + 1],
                fc2T[:, hc * D + n0: hc * D + n0 + 512],
                start=(hc == 0), stop=(hc == HC4 - 1), inc=(1 if hc == HC4 - 1 else 0))
        t_ch = tp.n
        wait("v", tp, t_ch)
        vop("tensor_copy", crowA[0:1, n0:n0 + 512], P2[0:1, 0:512])
    vop("tensor_tensor", crowA[0:1, 0:D], crowA[0:1, 0:D], f2row[0:1, :], ALU.add)
    vop("tensor_tensor", crowA[0:1, 0:D], crowA[0:1, 0:D], grow2[0:1, :], ALU.mult)
    v_c2f = vop("tensor_copy", c2f_b[0:1, :], crowA[0:1, 0:D])
    wait("t", g_pa, g_pa_all)
    top("matmul", P4[:, 0:512], ones_b[0:1, 0:128], growb2[0:1, 0:512], start=True, stop=True)
    t_bg2 = top("matmul", P4[:, 512:1024], ones_b[0:1, 0:128], growb2[0:1, 512:1024],
                start=True, stop=True, inc=1)
    wait("v", tp, t_bg2)
    vop("tensor_copy", bcg2[:], P4[:, 0:1024])
    for hc in range(HC4):
        vop("tensor_tensor", fc2T[:, hc * D:(hc + 1) * D], fc2T[:, hc * D:(hc + 1) * D],
            bcg2[:], ALU.mult)
    v_f2done = vp.n

    x1t_free = [None, None, None]
    zb_free = [None, None]
    zT_free = [None, None]
    hb_free = [None, None]
    hT_free = [None, None]
    sst_free = [None]
    p4h_free = [(tp, t_bg2), (tp, t_bg2)]
    p2_free_pa = None
    p1a_free = None
    p1b_free = None
    ps_trA = P1a[:].bitcast(BF16)
    ps_trB = P1b[:].bitcast(BF16)

    pa_fold_done = [(vp, v_f2done), (vp, v_c1f), (vp, v_c2f)]
    n_t = n_tiles
    v_zb_t = [0] * (n_t + 1)
    v_hq_t = [0] * n_t
    s_zT_t = [0] * n_t
    s_hb_t = [0] * n_t
    s_hT_t = [0] * n_t
    t_ztr_t = [0] * n_t
    t_fc1_t = [[0] * 4 for _ in range(n_t)]
    t_fc2_t = [0] * n_t
    v_s_t = [0] * n_t
    ld_x1_t = [0] * n_t

    wmark("y", pa_fold_done)
    ld_x1_t[0] = dma("y", ch_l1[0], x1t[0], x1_d[0:128, :])
    ld_x1_t[1] = dma("y", ch_l1[1], x1t[1], x1_d[128:256, :])
    wait("v", ch_l1[0], ld_x1_t[0])
    wmark("v", pa_fold_done)
    v_zb_t[0] = vop("tensor_scalar", zb[0][:], x1t[0][:], zmean[:, 0:1], zrstd[:, 0:1],
                    ALU.subtract, ALU.mult)

    ew_jobs = []
    for e in range(NE):
        ew_jobs.append((ew1bf_d[e], ew1[e]))
        ew_jobs.append((ew2bf_d[e], ew2[e]))
    ew_sched = {}
    steps_avail = list(range(2, max(3, n_t)))
    for k, job in enumerate(ew_jobs):
        ew_sched.setdefault(steps_avail[k % len(steps_avail)], []).append(job)

    for t in range(n_t + 1):
        cur = t if t < n_t else None
        prv = t - 1 if t >= 1 else None
        if t in ew_sched:
            wait("g", sp, sp.n)
            for dst, srcm in ew_sched[t]:
                g_ew_all = dma("g", g_ew, dst, srcm)
        if 2 <= t + 1 < n_t:
            sl = (t + 1) % 3
            wmark("y", x1t_free[sl])
            ld_x1_t[t + 1] = dma("y", ch_l1[sl], x1t[sl],
                                 x1_d[(t + 1) * 128:(t + 2) * 128, :])
        if cur is not None:
            par = cur % 2
            wait("t", vp, v_zb_t[cur])
            wmark("t", p1b_free)
            for ic in range(DC):
                top("transpose", ps_trB[:, ic * 128:(ic + 1) * 128],
                    zb[par][:, ic * 128:(ic + 1) * 128], eye_b[:],
                    inc=(1 if ic == DC - 1 else 0))
            t_ztr_t[cur] = tp.n
            zb_free[par] = (tp, t_ztr_t[cur])
            wait("s", tp, t_ztr_t[cur])
            wmark("s", zT_free[par])
            s_zT_t[cur] = sop("activation", zT[par][:], ps_trB[:, 0:1024], AF.Copy)
            p1b_free = (sp, s_zT_t[cur])
            wait("t", sp, s_zT_t[cur])
            if cur == 0:
                wait("t", vp, v_c1f)
            for q in range(4):
                qh = q % 2
                h0 = q * 1024
                PQ = P4[:, qh * 1024:(qh + 1) * 1024]
                wmark("t", p4h_free[qh])
                for ic in range(DC):
                    for nn in range(2):
                        n0 = nn * 512
                        top("matmul", PQ[:, n0:n0 + 512],
                            zT[par][:, ic * 128:(ic + 1) * 128],
                            fc1T[:, ic * H4 + h0 + n0: ic * H4 + h0 + n0 + 512],
                            start=(ic == 0), stop=False)
                for nn in range(2):
                    n0 = nn * 512
                    top("matmul", PQ[:, n0:n0 + 512], ones_b[0:1, 0:128],
                        c1f_b[0:1, h0 + n0: h0 + n0 + 512], start=False, stop=True,
                        inc=(1 if nn == 1 else 0))
                t_fc1_t[cur][q] = tp.n
                wait("s", tp, t_fc1_t[cur][q])
                if q == 0:
                    wmark("s", hb_free[par])
                s_g = sop("activation", hb[par][:, h0:h0 + 1024], PQ[:], AF.Gelu)
                p4h_free[qh] = (sp, s_g)
                if q == 3:
                    s_hb_t[cur] = s_g
            zT_free[par] = (tp, t_fc1_t[cur][3])
        if prv is not None:
            par1 = prv % 2
            wait("v", sp, s_hb_t[prv])
            for ch in range(8):
                vop("bn_stats", nscr2[:, ch * 6:(ch + 1) * 6],
                    hb[par1][:, ch * 512:(ch + 1) * 512], drain=(ch == 7))
            vop("bn_aggr", nscr[:, 8:10], nscr2[:, 0:48])
            vop("tensor_scalar", nscr[:, 0:1], nscr[:, 9:10], EPS, None, ALU.add)
            newton_rstd(nscr[:, 0:1], nscr[:, 10:11], nscr[:, 2:3], nscr[:, 3:4], nscr[:, 4:5])
            v_hq_t[prv] = vop("tensor_scalar", hb[par1][:], hb[par1][:],
                              nscr[:, 8:9], nscr[:, 10:11], ALU.subtract, ALU.mult)
        if t + 1 < n_t:
            nxt = t + 1
            wait("v", ch_l1[nxt % 3 if nxt >= 2 else nxt], ld_x1_t[nxt])
            wmark("v", zb_free[nxt % 2])
            v_zb_t[nxt] = vop("tensor_scalar", zb[nxt % 2][:], x1t[nxt % 3][:],
                              zmean[:, nxt:nxt + 1], zrstd[:, nxt:nxt + 1],
                              ALU.subtract, ALU.mult)
        if prv is not None:
            par1 = prv % 2
            wait("t", vp, v_hq_t[prv])
            for rnd in range(4):
                PB = ps_trA if rnd % 2 == 0 else ps_trB
                wmark("t", p1a_free if rnd % 2 == 0 else p1b_free)
                for j in range(8):
                    hc = rnd * 8 + j
                    top("transpose", PB[:, j * 128:(j + 1) * 128],
                        hb[par1][:, hc * 128:(hc + 1) * 128], eye_b[:],
                        inc=(1 if j == 7 else 0))
                t_htr = tp.n
                wait("s", tp, t_htr)
                if rnd == 0:
                    wmark("s", hT_free[par1])
                s_hT = sop("activation", hT[par1][:, rnd * 1024:(rnd + 1) * 1024],
                           PB[:, 0:1024], AF.Copy)
                if rnd % 2 == 0:
                    p1a_free = (sp, s_hT)
                else:
                    p1b_free = (sp, s_hT)
            s_hT_t[prv] = sp.n
            hb_free[par1] = (tp, tp.n)
            wait("t", sp, s_hT_t[prv])
            wmark("t", p2_free_pa)
            if prv == 0:
                wait("t", vp, v_c2f)
            for hc in range(HC4):
                for nn in range(2):
                    n0 = nn * 512
                    top("matmul", P2[:, n0:n0 + 512], hT[par1][:, hc * 128:(hc + 1) * 128],
                        fc2T[:, hc * D + n0: hc * D + n0 + 512],
                        start=(hc == 0), stop=False)
            for nn in range(2):
                n0 = nn * 512
                top("matmul", P2[:, n0:n0 + 512], ones_b[0:1, 0:128], c2f_b[0:1, n0:n0 + 512],
                    start=False, stop=True, inc=(1 if nn == 1 else 0))
            t_fc2_t[prv] = tp.n
            hT_free[par1] = (tp, t_fc2_t[prv])
            wait("v", tp, t_fc2_t[prv])
            wmark("v", sst_free[0])
            v_s_t[prv] = vop("scalar_tensor_tensor", sst[:], P2[:], 1.0,
                             x1t[prv % 3][:], ALU.mult, ALU.add)
            p2_free_pa = (vp, v_s_t[prv])
            x1t_free[prv % 3] = (vp, v_s_t[prv])
            wait("y", vp, v_s_t[prv])
            st_s = dma("y", ch_ss[0], s_d[prv * 128:(prv + 1) * 128, :], sst[:])
            sst_free[0] = (ch_ss[0], st_s)

    pa_v, pa_s, pa_t = vp.n, sp.n, tp.n
    al.pop()

    al.push()
    xnT_sup = al.buf(DC * 1024, BF16)
    acc = al.buf(8 * D, F32)
    ehT = [al.buf(HEC * 512, BF16) for _ in range(2)]
    wslot = [al.buf(16 * 1024, BF16) for _ in range(3)]
    wT_buf = al.buf(n_tiles * 128, BF16, parts=8)
    eb2b = al.buf(D, BF16, parts=8)
    sf = [al.buf(D, BF16) for _ in range(2)]
    gf = [al.buf(D, BF16) for _ in range(2)]
    xf = [al.buf(D, F32) for _ in range(2)]

    barrier(("v", "s", "t"), pa_v, pa_s, pa_t)
    wait("y", vp, pa_v); wait("y", sp, pa_s); wait("y", tp, pa_t)
    wmark("y", [(ch_ss[0], ch_ss[0].n), (ch_ss[1], ch_ss[1].n),
                (ch_l1[0], ch_l1[0].n), (ch_l1[1], ch_l1[1].n), (ch_l1[2], ch_l1[2].n)])
    wait("y", g_ew, g_ew_all)
    ld_wt = dma("y", ch_wt, wT_buf[:], wT_d[:])
    wait("g", vp, pa_v); wait("g", sp, pa_s); wait("g", tp, pa_t)
    ld_eb2 = dma("g", g_eb2, eb2b[:], eb2[:])

    S1 = lambda k: (0, 2, 1)[k % 3]
    S2 = lambda k: (1, 0, 2)[k % 3]
    slot_free = [None, None, None]
    ps_eh_free = [None, None]
    ps_eo_free = [None, None]
    ps_init_free_h = [None]
    ehT_free = [None, None]
    acc_free = None
    xnT_sup_free = None
    sfxf_free = [None, None, None, None]
    gf_free = [None, None]
    ps_eo = [P2, P4[:, 0:1024]]
    ps_init = P4[:, 1024:2048]
    out_last = 0

    def load_slot(slot, which, e_idx):
        wmark("y", slot_free[slot])
        c = DC if which == 1 else HEC
        src = ew1bf_d if which == 1 else ew2bf_d
        return dma("y", ch_w[slot], wslot[slot].rearrange("p (c n) -> p c n", c=c),
                   src[e_idx].rearrange("(c p) n -> p c n", p=128))

    n_exp_total = n_supers * NE
    ld_w1v = {0: load_slot(S1(0), 1, 0)}
    ld_w2v = {0: load_slot(S2(0), 2, 0)}
    ld_xnT_v = {}
    wmark("y", xnT_sup_free)
    ld_xnT_v[0] = dma("y", ch_xt, xnT_sup.rearrange("p (ic n) -> p ic n", ic=DC),
                      xnT_d[0].rearrange("ic p n -> p ic n"))

    for st in range(n_supers):
        ld_xnT = ld_xnT_v[st]
        wmark("y", acc_free[0] if acc_free else None)
        ld_acc_a = dma("y", ch_ac, acc.rearrange("p (t n) -> p t n", t=8)[:, 0:4, :],
                       xn_d[st * 1024:st * 1024 + 512, :].rearrange("(t p) n -> p t n", p=128))
        wmark("y", acc_free[1] if acc_free else None)
        ld_acc_b = dma("y", ch_ac2, acc.rearrange("p (t n) -> p t n", t=8)[:, 4:8, :],
                       xn_d[st * 1024 + 512:(st + 1) * 1024, :].rearrange("(t p) n -> p t n", p=128))
        def emit_init_tile(t8):
            tile = st * 8 + t8
            if st == 0 and t8 == 0:
                wait("t", ch_wt, ld_wt)
                wait("t", g_eb2, ld_eb2)
            wmark("t", ps_init_free_h[0])
            for nn in range(2):
                n0 = nn * 512
                top("matmul", ps_init[:, n0:n0 + 512],
                    wT_buf[0:8, tile * 128:(tile + 1) * 128],
                    eb2b[0:8, n0:n0 + 512], start=True, stop=True,
                    inc=(1 if nn == 1 else 0))
            t_init = tp.n
            wait("v", tp, t_init)
            if t8 == 0:
                wait("v", ch_ac, ld_acc_a)
            if t8 == 4:
                wait("v", ch_ac2, ld_acc_b)
            v_init = vop("scalar_tensor_tensor", acc[:, t8 * D:(t8 + 1) * D], ps_init,
                         1.0, acc[:, t8 * D:(t8 + 1) * D], ALU.mult, ALU.add)
            ps_init_free_h[0] = (vp, v_init)
        for e in range(NE):
            k = st * 8 + e
            i1, i2 = S1(k), S2(k)
            ld_w1, ld_w2 = ld_w1v[k], ld_w2v[k]
            if k + 1 < n_exp_total:
                ld_w1v[k + 1] = load_slot(S1(k + 1), 1, (k + 1) % NE)
            for hh in range(2):
                hpar = hh
                for hc in range(HEC):
                    pp = hc % 2
                    PE_ = P1a if pp == 0 else P1b
                    wmark("t", ps_eh_free[pp])
                    if hc == 0 and hh == 0:
                        wait("t", ch_w[i1], ld_w1)
                        if e == 0:
                            wait("t", ch_xt, ld_xnT)
                    for ic in range(DC):
                        top("matmul", PE_[:],
                            wslot[i1][:, ic * HE + hc * 128: ic * HE + (hc + 1) * 128],
                            xnT_sup[:, ic * 1024 + hh * 512: ic * 1024 + hh * 512 + 512],
                            start=(ic == 0), stop=(ic == DC - 1), inc=(1 if ic == DC - 1 else 0))
                    t_mm1 = tp.n
                    wait("s", tp, t_mm1)
                    if hc == 0:
                        wmark("s", ehT_free[hpar])
                        if st == 0 and e == 0 and hh == 0:
                            wait("s", vp, v_eb1)
                    s_eh = sop("activation", ehT[hpar][:, hc * 512:(hc + 1) * 512], PE_[:],
                               AF.Silu, bias=eb1T[:, hc * 8 + e: hc * 8 + e + 1])
                    ps_eh_free[pp] = (sp, s_eh)
                    if e == 1 and hc % 4 == 3:
                        emit_init_tile((hh * 16 + hc) // 4)
                if hh == 1:
                    slot_free[i1] = (tp, tp.n)
                    if k + 1 < n_exp_total:
                        ld_w2v[k + 1] = load_slot(S2(k + 1), 2, (k + 1) % NE)
                s_eh_done = sp.n
                wait("t", sp, s_eh_done)
                if hh == 0:
                    wait("t", ch_w[i2], ld_w2)
                for tl in range(4):
                    t8 = hh * 4 + tl
                    tile = st * 8 + t8
                    pp = tl % 2
                    wmark("t", ps_eo_free[pp])
                    for hc in range(HEC):
                        for nn in range(2):
                            n0 = nn * 512
                            top("matmul", ps_eo[pp][:, n0:n0 + 512],
                                ehT[hpar][:, hc * 512 + tl * 128: hc * 512 + (tl + 1) * 128],
                                wslot[i2][:, hc * 1024 + n0: hc * 1024 + n0 + 512],
                                start=(hc == 0), stop=(hc == HEC - 1),
                                inc=(1 if (hc == HEC - 1 and nn == 1) else 0))
                    t_mm2 = tp.n
                    wait("v", tp, t_mm2)
                    if e == 0 and tl == 0:
                        wait("v", ch_ac if hh == 0 else ch_ac2, ld_acc_a if hh == 0 else ld_acc_b)
                    v_acc = vop("scalar_tensor_tensor", acc[:, t8 * D:(t8 + 1) * D], ps_eo[pp],
                                w_buf[:, tile * 8 + e: tile * 8 + e + 1],
                                acc[:, t8 * D:(t8 + 1) * D], ALU.mult, ALU.add)
                    ps_eo_free[pp] = (vp, v_acc)
                ehT_free[hpar] = (tp, tp.n)
            slot_free[i2] = (tp, tp.n)
        v_experts, t_experts, s_experts = vp.n, tp.n, sp.n
        xnT_sup_free = (tp, t_experts)
        if st + 1 < n_supers:
            wmark("y", xnT_sup_free)
            ld_xnT_v[st + 1] = dma("y", ch_xt, xnT_sup.rearrange("p (ic n) -> p ic n", ic=DC),
                                   xnT_d[st + 1].rearrange("ic p n -> p ic n"))
        for t8 in range(8):
            fpar = t8 % 2
            wmark("g", sfxf_free[fpar])
            ld_s = dma("g", ch_sf[fpar], sf[fpar],
                       s_d[(st * 8 + t8) * 128:(st * 8 + t8 + 1) * 128, :])
            wmark("y", sfxf_free[2 + fpar])
            ld_xx = dma("y", ch_xf[fpar], xf[fpar],
                        x_d[(st * 8 + t8) * 128:(st * 8 + t8 + 1) * 128, :])
            if t8 == 0:
                wait("s", vp, v_experts)
            wmark("s", gf_free[fpar])
            s_g = sop("activation", gf[fpar], acc[:, t8 * D:(t8 + 1) * D], AF.Silu)
            wait("v", sp, s_g)
            wait("v", ch_sf[fpar], ld_s)
            vop("tensor_tensor", acc[:, t8 * D:(t8 + 1) * D], gf[fpar], sf[fpar], ALU.mult)
            gf_free[fpar] = (vp, vp.n)
            sfxf_free[fpar] = (vp, vp.n)
            wait("v", ch_xf[fpar], ld_xx)
            v_out = vop("tensor_tensor", acc[:, t8 * D:(t8 + 1) * D],
                        acc[:, t8 * D:(t8 + 1) * D], xf[fpar], ALU.add)
            sfxf_free[2 + fpar] = (vp, v_out)
            wait("g", vp, v_out)
            och = g_out if t8 < 4 else g_outb
            out_last = dma("g", och, out_d[(st * 8 + t8) * 128:(st * 8 + t8 + 1) * 128, :],
                           acc[:, t8 * D:(t8 + 1) * D])
            if t8 == 3:
                out_a = out_last
        acc_free = [(g_out, out_a), (g_outb, out_last)]

    wait("g", g_out, g_out.n)
    wait("g", g_outb, g_outb.n)
    al.pop()

    with nc.Block() as blk:
        @blk.sync
        def _(h):
            for fn in sched["y"]:
                fn(h)

        @blk.gpsimd
        def _(h):
            for fn in sched["g"]:
                fn(h)

        @blk.vector
        def _(h):
            for fn in sched["v"]:
                fn(h)

        @blk.scalar
        def _(h):
            for fn in sched["s"]:
                fn(h)

        @blk.tensor
        def _(h):
            for fn in sched["t"]:
                fn(h)

    es.close()
    return {e: len(sched[e]) for e in sched}

from concourse.bass_utils import run_bass_kernel_spmd
import ml_dtypes

N_CORES = 8
T_CORE = 4096

_WMAP = {
    "ln1_g": "L0_attn_ln1_g", "ln1_b": "L0_attn_ln1_b",
    "proj_w": "L0_attn_proj_w", "proj_b": "L0_attn_proj_b",
    "ln2_g": "L0_attn_ln2_g", "ln2_b": "L0_attn_ln2_b",
    "fc1_w": "L0_attn_fc1_w", "fc1_b": "L0_attn_fc1_b",
    "lnh_g": "L0_attn_lnh_g", "lnh_b": "L0_attn_lnh_b",
    "fc2_w": "L0_attn_fc2_w", "fc2_b": "L0_attn_fc2_b",
    "gamma1": "L0_attn_gamma1", "gamma2": "L0_attn_gamma2",
    "rw1": "L0_ffn_rw1", "rb1": "L0_ffn_rb1",
    "rw2": "L0_ffn_rw2", "rb2": "L0_ffn_rb2",
    "ew1": "L0_ffn_ew1", "eb1": "L0_ffn_eb1",
    "ew2": "L0_ffn_ew2", "eb2": "L0_ffn_eb2",
}


def kernel(**inputs):
    x = np.ascontiguousarray(np.asarray(inputs["L0_attn_x"], np.float32).reshape(-1, 1024))
    assert x.shape[0] == N_CORES * T_CORE
    shards = np.split(x, N_CORES, axis=0)

    base = {k: np.ascontiguousarray(np.asarray(inputs[v], np.float32)) for k, v in _WMAP.items()}
    base["eye_b"] = np.eye(128, dtype=ml_dtypes.bfloat16)
    base["eye_f"] = np.eye(128, dtype=np.float32)

    nc = bass.Bass("TRN2", target_bir_lowering=False)
    build(nc, T=T_CORE)

    in_maps = [dict(base, x=shards[c]) for c in range(N_CORES)]
    res = run_bass_kernel_spmd(nc, in_maps, core_ids=list(range(N_CORES)))
    out = np.concatenate([np.asarray(res.results[c]["out"], np.float32) for c in range(N_CORES)], axis=0)
    return out.reshape(4, 8192, 1024)


# version 21
# speedup vs baseline: 1.2604x; 1.0894x over previous
import numpy as np
import concourse.bass as bass
import concourse.mybir as mybir
from contextlib import ExitStack

F32 = mybir.dt.float32
BF16 = mybir.dt.bfloat16
AF = mybir.ActivationFunctionType
ALU = mybir.AluOpType
AX = mybir.AxisListType

D, DC = 1024, 8
H4, HC4 = 4096, 32
HE, HEC = 2048, 16
NE = 8
EPS = 1e-5
NEWTON_LINES = (4.0, 1.0, 0.25, 0.0625)
NEWTON_ITERS = 3


def build(nc, T=4096):
    assert T % 1024 == 0
    n_tiles, n_groups, n_supers = T // 128, T // 512, T // 1024

    def din(name, shape, dt=F32):
        return nc.dram_tensor(name, shape, dt, kind="ExternalInput").ap()

    x_d = din("x", [T, D])
    ln1_g, ln1_b = din("ln1_g", [D]), din("ln1_b", [D])
    proj_w, proj_b = din("proj_w", [D, D]), din("proj_b", [D])
    ln2_g, ln2_b = din("ln2_g", [D]), din("ln2_b", [D])
    fc1_w, fc1_b = din("fc1_w", [H4, D]), din("fc1_b", [H4])
    lnh_g, lnh_b = din("lnh_g", [H4]), din("lnh_b", [H4])
    fc2_w, fc2_b = din("fc2_w", [D, H4]), din("fc2_b", [D])
    gamma1, gamma2 = din("gamma1", [D]), din("gamma2", [D])
    rw1, rb1 = din("rw1", [D, D]), din("rb1", [D])
    rw2, rb2 = din("rw2", [D, NE]), din("rb2", [NE])
    ew1, eb1 = din("ew1", [NE, D, HE]), din("eb1", [NE, HE])
    ew2, eb2 = din("ew2", [NE, HE, D]), din("eb2", [NE, D])
    eye_b_d = din("eye_b", [128, 128], BF16)
    eye_f_d = din("eye_f", [128, 128], F32)
    out_d = nc.dram_tensor("out", [T, D], F32, kind="ExternalOutput").ap()

    pbf_d = nc.dram_tensor("pbf", [D, D], BF16, kind="Internal").ap()
    f1bf_d = nc.dram_tensor("f1bf", [H4, D], BF16, kind="Internal").ap()
    f2bf_d = nc.dram_tensor("f2bf", [D, H4], BF16, kind="Internal").ap()
    ew1bf_d = nc.dram_tensor("ew1bf", [NE, D, HE], BF16, kind="Internal").ap()
    ew2bf_d = nc.dram_tensor("ew2bf", [NE, HE, D], BF16, kind="Internal").ap()
    xn_d = nc.dram_tensor("xn_sc", [T, D], F32, kind="Internal").ap()
    xnT_d = nc.dram_tensor("xnT_sc", [n_supers, DC, 128, 1024], BF16, kind="Internal").ap()
    x1_d = nc.dram_tensor("x1_sc", [T, D], F32, kind="Internal").ap()
    s_d = nc.dram_tensor("s_sc", [T, D], F32, kind="Internal").ap()
    wT_d = nc.dram_tensor("wT_sc", [8, T // 128 * 128], BF16, kind="Internal").ap()

    es = ExitStack()
    ARENA_BYTES = 207 * 1024
    arena = es.enter_context(nc.sbuf_tensor("arena", [128, ARENA_BYTES], mybir.dt.int8))[:]

    class Alloc:
        def __init__(self):
            self.off, self.marks = 0, []
        def buf(self, cols, dt=F32, parts=128):
            isz = 4 if dt == F32 else 2
            self.off = (self.off + 31) & ~31
            ap = arena[0:parts, self.off:self.off + cols * isz].bitcast(dt)
            self.off += cols * isz
            assert self.off <= ARENA_BYTES, f"arena overflow {self.off}"
            return ap
        def push(self):
            self.marks.append(self.off)
        def pop(self):
            self.off = self.marks.pop()
    al = Alloc()

    P4 = es.enter_context(nc.psum_tensor("P4", [128, 2048], F32))[:]
    P2 = es.enter_context(nc.psum_tensor("P2", [128, 1024], F32))[:]
    P1a = es.enter_context(nc.psum_tensor("P1a", [128, 512], F32))[:]
    P1b = es.enter_context(nc.psum_tensor("P1b", [128, 512], F32))[:]
    ps_tr_bf = P4[:, 0:512].bitcast(BF16)
    ps_lT = P4[0:8, 512:1024]
    ps_lt2 = P4[:, 1024:1056]
    ps_wT = P4[0:8, 1536:1664]

    sched = {e: [] for e in ("y", "g", "v", "s", "t")}

    class Sem:
        def __init__(self, name):
            self.h = es.enter_context(nc.semaphore(name))
            self.n = 0
            self.name = name
    vp, sp, tp = Sem("vp"), Sem("sp"), Sem("tp")
    g_pbf, g_f1, g_f2 = Sem("g_pbf"), Sem("g_f1"), Sem("g_f2")
    g_small, g_pa, g_ew, g_out = Sem("g_small"), Sem("g_pa"), Sem("g_ew"), Sem("g_out")
    lm0, lm0b, lm1, lm_pj = Sem("lm0"), Sem("lm0b"), Sem("lm1"), Sem("lm_pj")
    ch_xg = [Sem("ch_xg0"), Sem("ch_xg1")]
    ch_sxn = [Sem("ch_sxn0"), Sem("ch_sxn1")]
    ch_sxT = [Sem("ch_sxT0"), Sem("ch_sxT1")]
    ch_sx1 = [Sem("ch_sx10"), Sem("ch_sx11")]
    ch_l1 = [Sem("ch_l10"), Sem("ch_l11"), Sem("ch_l12")]
    ch_ss = [Sem("ch_ss0"), Sem("ch_ss1")]
    ch_w = [Sem("ch_w0"), Sem("ch_w1"), Sem("ch_w2")]
    ch_xt, ch_ac, ch_ac2 = Sem("ch_xt"), Sem("ch_ac"), Sem("ch_ac2")
    ch_xt2 = Sem("ch_xt2")
    ch_wt, g_eb2 = Sem("ch_wt"), Sem("g_eb2")
    s_f2, g_outb = Sem("s_f2"), Sem("g_outb")
    ch_row = Sem("ch_row")
    ch_sf = [Sem("ch_sf0"), Sem("ch_sf1")]
    ch_xf = [Sem("ch_xf0"), Sem("ch_xf1")]

    def wait(e, sem, val):
        if val <= 0:
            return
        assert val <= sem.n, f"wait on future {val}>{sem.n} ({sem.name})"
        sched[e].append(lambda h, s=sem.h, v=val: h.wait_ge(s, v))

    def wmark(e, mark):
        if not mark:
            return
        if isinstance(mark, list):
            for m in mark:
                wmark(e, m)
        else:
            wait(e, mark[0], mark[1])

    def dma(e, ch, out, in_, slow=False, transpose=False):
        ch.n += 16
        def fn(h, out=out, in_=in_, slow=slow, tr=transpose, s=ch.h):
            if slow:
                with nc.allow_non_contiguous_dma(reason="tiny chunked vector load"):
                    h.dma_start(out=out, in_=in_).then_inc(s, 16)
            elif tr:
                h.dma_start(out=out, in_=in_, transpose=True).then_inc(s, 16)
            else:
                h.dma_start(out=out, in_=in_).then_inc(s, 16)
        sched[e].append(fn)
        return ch.n

    def vop(name, *args, drain=True, **kw):
        vp.n += 1
        def fn(h, name=name, args=args, kw=kw, drain=drain):
            i = getattr(h, name)(*args, **kw)
            if drain:
                h.drain().then_inc(vp.h, 1)
            else:
                i.then_inc(vp.h, 1)
        sched["v"].append(fn)
        return vp.n

    def sop(name, *args, drain=True, **kw):
        sp.n += 1
        def fn(h, name=name, args=args, kw=kw, drain=drain):
            i = getattr(h, name)(*args, **kw)
            if drain:
                h.drain().then_inc(sp.h, 1)
            else:
                i.then_inc(sp.h, 1)
        sched["s"].append(fn)
        return sp.n

    def top(name, *args, inc=0, **kw):
        tp.n += inc
        def fn(h, name=name, args=args, kw=kw, inc=inc):
            i = getattr(h, name)(*args, **kw)
            if inc:
                i.then_inc(tp.h, inc)
        sched["t"].append(fn)
        return tp.n

    def barrier(engines, vm, sm, tm):
        for e in engines:
            wait(e, vp, vm); wait(e, sp, sm); wait(e, tp, tm)

    eye_b = al.buf(128, BF16)
    eye_f = al.buf(128, F32)
    ones_b = al.buf(512, BF16, parts=1)
    w_buf = al.buf(n_tiles * 8, F32)
    eb1T = al.buf(HEC * NE, F32)
    rb2b = al.buf(NE, BF16, parts=1)
    nscr = al.buf(16, F32)
    nscr2 = al.buf(48, F32)
    zmean = al.buf(n_tiles, F32)
    zrstd = al.buf(n_tiles, F32)

    def newton_rstd(tcol, outcol, s0, s1, s2, lines=NEWTON_LINES, iters=NEWTON_ITERS):
        first = True
        for aa in lines:
            vop("tensor_scalar", s1, tcol, aa * aa, 1.0 / (2 * aa), ALU.add, ALU.mult)
            if first:
                vop("tensor_copy", s0, s1); first = False
            else:
                vop("tensor_tensor", s0, s0, s1, ALU.min)
        vop("tensor_scalar", s2, tcol, 0.5, None, ALU.mult)
        for _ in range(iters):
            vop("reciprocal", s1, s0)
            vop("tensor_tensor", s1, s2, s1, ALU.mult)
            vop("scalar_tensor_tensor", s0, s0, 0.5, s1, ALU.mult, ALU.add)
        return vop("reciprocal", outcol, s0)

    rowst = {"tp_mark": 0, "vp_mark": 0}

    def load_vecT(vec_d_ap, dest, nch, vrow):
        wait("t", lm0, lm0_all)
        for r0 in range(0, nch, 8):
            k = min(8, nch - r0)
            wait("y", tp, rowst["tp_mark"])
            lr = dma("y", ch_row, vrow[0:1, 0:k * 128], vec_d_ap[None, r0 * 128:(r0 + k) * 128])
            wait("t", ch_row, lr)
            wait("t", vp, rowst["vp_mark"])
            for j in range(k):
                top("transpose", P1b[0:128, j:j + 1], vrow[0:1, j * 128:(j + 1) * 128],
                    eye_f[0:1, 0:1], inc=(1 if j == k - 1 else 0))
            rowst["tp_mark"] = tp.n
            wait("v", tp, rowst["tp_mark"])
            rowst["vp_mark"] = vop("tensor_copy", dest[:, r0:r0 + k], P1b[0:128, 0:k])

    dma("g", g_pbf, pbf_d[:], proj_w[:])

    al.push()
    projT = al.buf(DC * D, BF16)
    rw1b = al.buf(DC * D, BF16)
    rw2b = al.buf(DC * NE, BF16)
    xg = [al.buf(4 * D, F32) for _ in range(2)]
    xnf = [al.buf(4 * D, F32) for _ in range(2)]
    xnbb = al.buf(4 * D, BF16)
    xnT_g = [al.buf(8 * 512, BF16) for _ in range(2)]
    uT_g = [al.buf(8 * 512, BF16) for _ in range(2)]
    x1s = [al.buf(D, F32) for _ in range(2)]
    lT_s = [al.buf(512, F32, parts=8) for _ in range(2)]
    l_s = al.buf(32, F32)
    gw = al.buf(128, F32)
    p0stat = al.buf(128, F32)
    bc_c1 = al.buf(D, F32)
    rb1T = al.buf(DC, F32)
    wT_buf = al.buf(n_tiles * 128, BF16, parts=8)
    g1T = al.buf(DC, F32)
    b1T = al.buf(DC, BF16)
    grow1 = al.buf(D, F32, parts=1)
    growb = al.buf(D, BF16, parts=1)
    prow = al.buf(D, F32, parts=1)
    crow = al.buf(D, F32, parts=1)
    crow_b = al.buf(D, BF16, parts=1)
    bcg = al.buf(D, BF16)
    eb1n = al.buf(HE, F32, parts=8)

    dma("g", g_small, rb2b[:], rb2[None, :])
    dma("g", g_small, rw1b.rearrange("p (c n) -> p c n", c=DC),
        rw1.rearrange("(c p) n -> p c n", p=128))
    dma("g", g_small, rw2b.rearrange("p (c n) -> p c n", c=DC),
        rw2.rearrange("(c p) n -> p c n", p=128))
    g_small_all = dma("g", g_small, growb[:], gamma1[None, :])
    g_ew_all = 0

    dma("y", lm0, eye_b[:], eye_b_d[:])
    dma("y", lm0, eye_f[:], eye_f_d[:])
    dma("y", lm0, grow1[:], gamma1[None, :])
    dma("y", lm0, prow[0:1, :], proj_b[None, :])
    lm0_all = lm0.n
    lm0b_all = dma("y", lm0b, eb1n[:], eb1[:])

    ld_x0_early = dma("y", ch_xg[0], xg[0].rearrange("p (s n) -> p s n", s=4),
                      x_d[0:512, :].rearrange("(s p) n -> p s n", p=128))
    wait("y", g_pbf, 16)
    pj_stage = xg[1].bitcast(BF16)
    lm_pj_all = dma("y", lm_pj, pj_stage.rearrange("p (s n) -> p s n", s=8),
                    pbf_d.rearrange("(s p) n -> p s n", p=128))
    wait("t", lm_pj, lm_pj_all)
    wait("t", lm0, lm0_all)
    pj_prev_evac = [0]
    for oc in range(8):
        wait("t", sp, pj_prev_evac[0])
        for ic in range(DC):
            top("transpose", ps_tr_bf[:, ic * 128:(ic + 1) * 128],
                pj_stage[:, oc * D + ic * 128: oc * D + (ic + 1) * 128], eye_b[:],
                inc=(1 if ic == DC - 1 else 0))
        wait("s", tp, tp.n)
        pj_prev_evac[0] = sop("activation",
                              projT.rearrange("p (ic n) -> p ic n", ic=DC)[:, :, oc * 128:(oc + 1) * 128],
                              ps_tr_bf[:].rearrange("p (ic n) -> p ic n", ic=DC), AF.Copy)
    s_pj_done = pj_prev_evac[0]
    pj_tr_done = tp.n
    load_vecT(ln1_g, g1T, DC, crow)
    load_vecT(rb1, rb1T, DC, crow)
    load_vecT(ln1_b, b1T, DC, crow)

    v_ones = vop("memset", ones_b[0:1, :], 1.0)
    wait("v", lm0, lm0_all)
    wait("v", sp, s_pj_done)
    for ic in range(DC):
        vop("tensor_scalar", projT[:, ic * D:(ic + 1) * D], projT[:, ic * D:(ic + 1) * D],
            g1T[:, ic:ic + 1], None, ALU.mult)
    v_p1 = vp.n
    wait("t", vp, v_p1)
    wait("t", g_small, g_small_all)
    for half in range(2):
        n0 = half * 512
        for ic in range(DC):
            top("matmul", P2[0:1, n0:n0 + 512], b1T[:, ic:ic + 1],
                projT[:, ic * D + n0: ic * D + n0 + 512],
                start=(ic == 0), stop=(ic == DC - 1),
                inc=(1 if (half == 1 and ic == DC - 1) else 0))
    t_c1 = tp.n
    wait("v", tp, t_c1)
    vop("tensor_tensor", crow[0:1, :], P2[0:1, 0:D], prow[0:1, :], ALU.add)
    vop("tensor_tensor", crow[0:1, :], crow[0:1, :], grow1[0:1, :], ALU.mult)
    v_c1r = vop("tensor_copy", crow_b[0:1, :], crow[0:1, :])
    wait("t", vp, v_c1r)
    top("matmul", P2[:, 0:512], ones_b[0:1, 0:128], crow_b[0:1, 0:512], start=True, stop=True)
    t_bc1 = top("matmul", P2[:, 512:1024], ones_b[0:1, 0:128], crow_b[0:1, 512:1024],
                start=True, stop=True, inc=1)
    wait("v", tp, t_bc1)
    v_bc1 = vop("tensor_copy", bc_c1[:], P2[:])
    top("matmul", P4[:, 0:512], ones_b[0:1, 0:128], growb[0:1, 0:512], start=True, stop=True)
    t_bg = top("matmul", P4[:, 512:1024], ones_b[0:1, 0:128], growb[0:1, 512:1024],
               start=True, stop=True, inc=1)
    wait("v", tp, t_bg)
    v_bcg = vop("tensor_copy", bcg[:], P4[:, 0:1024])
    for ic in range(DC):
        vop("tensor_tensor", projT[:, ic * D:(ic + 1) * D], projT[:, ic * D:(ic + 1) * D],
            bcg[:], ALU.mult)
    v_projdone = vp.n
    wait("t", lm0b, lm0b_all)
    for hc in range(HEC):
        top("transpose", P1a[0:128, hc * 8:(hc + 1) * 8], eb1n[0:8, hc * 128:(hc + 1) * 128],
            eye_f[0:8, 0:8], inc=(1 if hc == HEC - 1 else 0))
    t_eb1 = tp.n
    wait("v", tp, t_eb1)
    v_eb1 = vop("tensor_copy", eb1T[:], P1a[:, 0:HEC * NE])

    xg_free = [None, None]
    xnf_free = [None, None]
    xnT_free = [None, None]
    uT_free = [None, None]
    x1s_free = [None, None]
    p2_free = (vp, v_bc1)
    p1_free = [(vp, v_eb1), None]
    p4b0_free = (vp, v_bcg)
    p4b1_free = None
    p4b2_free = None
    p4b3_free = None
    lT_free = [None, None]


    w_ready = [0] * n_tiles
    wT_ready = [0] * n_tiles
    pend_lt2 = []
    pend_wT = []

    def flush_wT(force=False):
        nonlocal p4b3_free
        if not pend_wT or (len(pend_wT) < 2 and not force):
            return
        g, w_grp = pend_wT.pop(0)
        wslab = w_buf[:, g * 32:(g + 1) * 32]
        for t in range(4):
            tile = g * 4 + t
            wait("t", vp, w_grp)
            wmark("t", p4b3_free)
            t_wT = top("transpose", ps_wT[:], wslab[:, t * 8:(t + 1) * 8], eye_f[:], inc=1)
            wait("s", tp, t_wT)
            wT_ready[tile] = sop("activation", wT_buf[0:8, tile * 128:(tile + 1) * 128],
                                 ps_wT[:], AF.Copy)
            p4b3_free = (sp, wT_ready[tile])

    def flush_lt2():
        nonlocal p4b2_free, lT_free
        if not pend_lt2:
            return
        g, s_lT = pend_lt2.pop(0)
        wait("t", sp, s_lT)
        wmark("t", p4b2_free)
        for t in range(4):
            top("transpose", ps_lt2[:, t * 8:(t + 1) * 8], lT_s[g % 2][0:8, t * 128:(t + 1) * 128],
                eye_f[0:8, 0:8], inc=(1 if t == 3 else 0))
        t_lt2 = tp.n
        lT_free[g % 2] = (tp, t_lt2)
        wait("v", tp, t_lt2)
        v_l = vop("tensor_copy", l_s[:], ps_lt2[:])
        p4b2_free = (vp, v_l)
        wslab = w_buf[:, g * 32:(g + 1) * 32]
        vop("tensor_reduce", gw[:, 0:4], v3(l_s[:]), AX.X, ALU.max)
        vop("tensor_tensor", v3(gw[:, 32:64]), v3(l_s[:]), bc4(gw[:, 0:4]), ALU.is_equal)
        vop("scalar_tensor_tensor", v3(gw[:, 96:128]), v3(gw[:, 32:64]), -1e30, v3(l_s[:]),
            ALU.mult, ALU.add)
        vop("tensor_reduce", gw[:, 4:8], v3(gw[:, 96:128]), AX.X, ALU.max)
        vop("tensor_tensor", v3(gw[:, 64:96]), v3(l_s[:]), bc4(gw[:, 4:8]), ALU.is_equal)
        vop("tensor_tensor", gw[:, 8:12], gw[:, 0:4], gw[:, 4:8], ALU.subtract)
        v_dc = vop("tensor_scalar", gw[:, 12:16], gw[:, 8:12], 1e-30, None, ALU.max)
        wait("s", vp, v_dc)
        s_sv = sop("activation", gw[:, 16:20], gw[:, 12:16], AF.Silu)
        wait("v", sp, s_sv)
        vop("reciprocal", gw[:, 20:24], gw[:, 12:16])
        vop("tensor_tensor", gw[:, 24:28], gw[:, 16:20], gw[:, 20:24], ALU.mult)
        vop("tensor_scalar", gw[:, 28:32], gw[:, 24:28], -1.0, 1.0, ALU.mult, ALU.add)
        vop("tensor_tensor", v3(wslab), v3(gw[:, 32:64]), bc4(gw[:, 24:28]), ALU.mult, drain=False)
        vop("tensor_tensor", v3(gw[:, 96:128]), v3(gw[:, 64:96]), bc4(gw[:, 28:32]), ALU.mult)
        w_grp = vop("tensor_tensor", wslab, wslab, gw[:, 96:128], ALU.add)
        for t in range(4):
            w_ready[g * 4 + t] = w_grp
        pend_wT.append((g, w_grp))

    def v3(ap32):
        return ap32.rearrange("p (t e) -> p t e", e=8)

    def bc4(ap4):
        return ap4.rearrange("p (t o) -> p t o", o=1).broadcast_to((128, 4, 8))

    def _xv(par):
        o = par * 64
        blk = p0stat[:, o:o + 8].rearrange("p (t w) -> p t w", w=2)
        return dict(mean=blk[:, :, 0:1], var=blk[:, :, 1:2],
                    tx4=p0stat[:, o + 8:o + 12], rx4=p0stat[:, o + 12:o + 16],
                    t24=p0stat[:, o + 16:o + 20], r24=p0stat[:, o + 20:o + 24],
                    ns0=p0stat[:, o + 24:o + 28], ns1=p0stat[:, o + 28:o + 32],
                    aggr=lambda t: p0stat[:, o + 2 * t:o + 2 * t + 2])
    XV = [_xv(0), _xv(1)]
    z_mean = p0stat[:, 32:40].rearrange("p (t w) -> p t w", w=2)[:, :, 0:1]
    z_var = p0stat[:, 32:40].rearrange("p (t w) -> p t w", w=2)[:, :, 1:2]
    ns2 = p0stat[:, 40:44]
    tz4, rz4 = p0stat[:, 44:48], p0stat[:, 48:52]
    zaggr = lambda t: p0stat[:, 32 + 2 * t:34 + 2 * t]

    ld_x_t = [0] * (n_groups + 1)
    v_casts_t = [0] * n_groups
    v_xn_last_t = [0] * n_groups
    t_tr_last = [0]

    xpose_done = [0] * n_groups

    def emit_xpose(g):
        nonlocal p4b0_free
        par = g % 2
        wait("t", sp, v_casts_t[g])
        for t in range(4):
            wmark("t", p4b0_free)
            for ic in range(DC):
                top("transpose", ps_tr_bf[:, ic * 128:(ic + 1) * 128],
                    xnbb[:, t * D + ic * 128: t * D + (ic + 1) * 128], eye_b[:],
                    inc=(1 if ic == DC - 1 else 0))
            t_tr = tp.n
            wait("s", tp, t_tr)
            if t == 0:
                wmark("s", xnT_free[par])
            s_ev = sop("activation",
                       xnT_g[par].rearrange("p (ic n) -> p ic n", ic=DC)[:, :, t * 128:(t + 1) * 128],
                       ps_tr_bf[:].rearrange("p (ic n) -> p ic n", ic=DC), AF.Copy)
            p4b0_free = (sp, s_ev)
        t_tr_last[0] = tp.n
        xpose_done[g] = sp.n

    def emit_stats(g):
        par = g % 2
        X = XV[par]
        wait("v", ch_xg[par], ld_x_t[g])
        for t in range(4):
            xt = xg[par][:, t * D:(t + 1) * D]
            vop("bn_stats", nscr2[:, 0:6], xt[:, 0:512], drain=False)
            vop("bn_stats", nscr2[:, 6:12], xt[:, 512:1024])
            vop("bn_aggr", X["aggr"](t), nscr2[:, 0:12])
        vop("tensor_scalar", X["tx4"], X["var"], EPS, None, ALU.add)
        newton_rstd(X["tx4"], X["rx4"], X["ns0"], X["ns1"], ns2, lines=(1.0,), iters=2)
        vop("tensor_tensor", X["t24"], X["var"], X["rx4"], ALU.mult)
        vop("tensor_tensor", X["t24"], X["t24"], X["rx4"], ALU.mult)
        vop("tensor_scalar", X["t24"], X["t24"], EPS, None, ALU.add)
        newton_rstd(X["t24"], X["r24"], X["ns0"], X["ns1"], ns2, lines=(1.0,), iters=2)
        wmark("v", xnf_free[par])
        for t in range(4):
            vop("tensor_scalar", xnf[par][:, t * D:(t + 1) * D], xg[par][:, t * D:(t + 1) * D],
                X["mean"][:, t], X["rx4"][:, t:t + 1], ALU.subtract, ALU.mult,
                drain=(t == 3))
        v_xn_last_t[g] = vp.n
        wait("s", vp, v_xn_last_t[g])
        wait("s", tp, t_tr_last[0])
        v_casts_t[g] = sop("activation", xnbb[:], xnf[par][:], AF.Copy)

    ld_x_t[0] = ld_x0_early
    emit_stats(0)
    emit_xpose(0)

    for g in range(n_groups):
        par = g % 2
        if n_groups >= 7 and 2 <= g <= 5:
            wait("g", sp, v_casts_t[g - 1])
            q = g - 2
            g_f1_all = dma("g", g_f1, f1bf_d[q * 1024:(q + 1) * 1024, :],
                           fc1_w[q * 1024:(q + 1) * 1024, :])
            g_f2_all = dma("g", g_f2, f2bf_d[q * 256:(q + 1) * 256, :],
                           fc2_w[q * 256:(q + 1) * 256, :])
        elif n_groups < 7 and g == min(2, n_groups - 1):
            wait("g", sp, v_casts_t[g - 1])
            g_f1_all = dma("g", g_f1, f1bf_d[:], fc1_w[:])
            g_f2_all = dma("g", g_f2, f2bf_d[:], fc2_w[:])
        if g + 1 < n_groups:
            np_ = (g + 1) % 2
            if g == 0:
                wait("y", tp, pj_tr_done)
            wmark("y", xg_free[np_])
            ld_x_t[g + 1] = dma("y", ch_xg[np_], xg[np_].rearrange("p (s n) -> p s n", s=4),
                                x_d[(g + 1) * 512:(g + 2) * 512, :].rearrange("(s p) n -> p s n", p=128))
        if g > 0:
            emit_xpose(g)
        flush_lt2()
        v_xn_last = v_xn_last_t[g]
        v_xnT = xpose_done[g]
        wait("y", vp, v_xn_last)
        st_xn = dma("y", ch_sxn[par], xn_d[g * 512:(g + 1) * 512, :].rearrange("(s p) n -> p s n", p=128),
                    xnf[par].rearrange("p (s n) -> p s n", s=4))
        wait("y", sp, v_xnT)
        wait("t", sp, v_xnT)
        st_i, hf = g // 2, g % 2
        st_xnT = dma("y", ch_sxT[par],
                     xnT_d[st_i, :, :, hf * 512:(hf + 1) * 512].rearrange("ic p n -> p ic n"),
                     xnT_g[par].rearrange("p (ic n) -> p ic n", ic=DC))
        wait("t", sp, v_xnT)
        for oc in range(DC):
            ppar = oc % 2
            PU = P1a if ppar == 0 else P1b
            wmark("t", p1_free[ppar])
            for ic in range(DC):
                top("matmul", PU[:], rw1b[:, ic * D + oc * 128: ic * D + (oc + 1) * 128],
                    xnT_g[par][:, ic * 512:(ic + 1) * 512],
                    start=(ic == 0), stop=(ic == DC - 1), inc=(1 if ic == DC - 1 else 0))
            t_u = tp.n
            wait("s", tp, t_u)
            if g == 0 and oc == 0:
                wait("s", lm0, lm0_all)
            if oc == 0:
                wmark("s", uT_free[par])
            s_u = sop("activation", uT_g[par][:, oc * 512:(oc + 1) * 512], PU[:],
                      AF.Silu, bias=rb1T[:, oc:oc + 1])
            p1_free[ppar] = (sp, s_u)
        s_uT = sp.n
        wait("t", sp, s_uT)
        wmark("t", p4b1_free)
        for oc in range(DC):
            top("matmul", ps_lT[:], rw2b[:, oc * NE:(oc + 1) * NE],
                uT_g[par][:, oc * 512:(oc + 1) * 512], start=(oc == 0), stop=False)
        t_lg = top("matmul", ps_lT[:], rb2b[0:1, 0:NE], ones_b[0:1, 0:512],
                   start=False, stop=True, inc=1)
        uT_free[par] = (tp, t_lg)
        wait("s", tp, t_lg)
        wmark("s", lT_free[g % 2])
        s_lT = sop("activation", lT_s[g % 2][:], ps_lT[:], AF.Copy)
        p4b1_free = (sp, s_lT)
        pend_lt2.append((g, s_lT))
        for t in range(4):
            tpar = t % 2
            wmark("t", p2_free)
            if g == 0 and t == 0:
                wait("t", vp, v_projdone)
            for hn in range(2):
                n0 = hn * 512
                for ic in range(DC):
                    top("matmul", P2[:, n0:n0 + 512],
                        xnT_g[par][:, ic * 512 + t * 128: ic * 512 + (t + 1) * 128],
                        projT[:, ic * D + n0: ic * D + n0 + 512],
                        start=(ic == 0), stop=(ic == DC - 1),
                        inc=(1 if (hn == 1 and ic == DC - 1) else 0))
            t_proj = tp.n
            wait("v", tp, t_proj)
            wmark("v", x1s_free[tpar])
            vop("scalar_tensor_tensor", x1s[tpar][:], P2[:],
                XV[par]["r24"][:, t:t + 1], bc_c1[:], ALU.mult, ALU.add)
            v_x1 = vop("tensor_tensor", x1s[tpar][:],
                       x1s[tpar][:], xnf[par][:, t * D:(t + 1) * D], ALU.add)
            p2_free = (vp, v_x1)
            wait("y", vp, v_x1)
            st_x1 = dma("y", ch_sx1[tpar], x1_d[(g * 4 + t) * 128:(g * 4 + t + 1) * 128, :],
                        x1s[tpar][:])
            x1s_free[tpar] = (ch_sx1[tpar], st_x1)
            vop("bn_stats", nscr2[:, 0:6], x1s[tpar][:, 0:512], drain=False)
            vop("bn_stats", nscr2[:, 6:12], x1s[tpar][:, 512:1024])
            vop("bn_aggr", zaggr(t), nscr2[:, 0:12])
        vop("tensor_scalar", tz4, z_var, EPS, None, ALU.add)
        newton_rstd(tz4, rz4, XV[par]["ns0"], XV[par]["ns1"], ns2, lines=(1.0,), iters=3)
        vop("tensor_copy", zmean[:, g * 4:(g + 1) * 4], z_mean, drain=False)
        v_zst = vop("tensor_copy", zrstd[:, g * 4:(g + 1) * 4], rz4)
        v_x1done = vp.n
        xg_free[par] = (vp, v_xn_last)
        xnf_free[par] = [(vp, v_x1done), (ch_sxn[par], st_xn)]
        xnT_free[par] = [(tp, tp.n), (ch_sxT[par], st_xnT)]
        if g + 1 < n_groups:
            emit_stats(g + 1)
        flush_wT()

    flush_lt2()
    flush_wT(force=True)
    flush_wT(force=True)
    wait("y", sp, wT_ready[n_tiles - 1])
    st_wt = dma("y", ch_wt, wT_d[:], wT_buf[:])
    p0_v, p0_s, p0_t = vp.n, sp.n, tp.n
    al.pop()

    al.push()
    fc1T = al.buf(DC * H4, BF16)
    fc2T = al.buf(HC4 * D, BF16)
    c1f_b = al.buf(H4, BF16, parts=1)
    c2f_b = al.buf(D, BF16, parts=1)
    al.push()
    g2T = al.buf(DC, F32)
    ghT = al.buf(HC4, F32)
    b2T = al.buf(DC, BF16)
    bhT = al.buf(HC4, BF16)
    grow2 = al.buf(D, F32, parts=1)
    growb2 = al.buf(D, BF16, parts=1)
    f1row = al.buf(H4, F32, parts=1)
    f2row = al.buf(D, F32, parts=1)
    crowA = al.buf(H4, F32, parts=1)
    bcg2 = al.buf(D, BF16)
    al.pop()
    zb = [al.buf(D, BF16) for _ in range(2)]
    zT = [al.buf(D, BF16) for _ in range(2)]
    hb = [al.buf(H4, BF16) for _ in range(2)]
    hT = [al.buf(H4, BF16) for _ in range(2)]
    x1t = [al.buf(D, F32) for _ in range(3)]
    sst = al.buf(D, F32)

    p0_store_done = [(ch_sxn[0], ch_sxn[0].n), (ch_sxn[1], ch_sxn[1].n),
                     (ch_sxT[0], ch_sxT[0].n), (ch_sxT[1], ch_sxT[1].n),
                     (ch_sx1[0], ch_sx1[0].n), (ch_sx1[1], ch_sx1[1].n),
                     (ch_xg[0], ch_xg[0].n), (ch_xg[1], ch_xg[1].n),
                     (ch_wt, ch_wt.n)]
    wait("y", vp, p0_v); wait("y", sp, p0_s); wait("y", tp, p0_t)
    wmark("y", p0_store_done)
    barrier(("v", "s", "t"), p0_v, p0_s, p0_t)
    wait("y", g_f1, g_f1_all)
    for ic in range(DC):
        dma("y", lm1, fc1T[:, ic * H4:(ic + 1) * H4], f1bf_d[:, ic * 128:(ic + 1) * 128],
            transpose=True)
    wait("s", g_f2, g_f2_all)
    wait("s", vp, p0_v); wait("s", tp, p0_t)
    wmark("s", p0_store_done)
    for hc in range(HC4):
        dma("s", s_f2, fc2T[:, hc * D:(hc + 1) * D], f2bf_d[:, hc * 128:(hc + 1) * 128],
            transpose=True)
    dma("y", lm1, f1row[0:1, :], fc1_b[None, :])
    dma("y", lm1, f2row[0:1, :], fc2_b[None, :])
    lm1_all = dma("y", lm1, grow2[0:1, :], gamma2[None, :])
    load_vecT(ln2_g, g2T, DC, crowA[0:1, 0:D])
    load_vecT(lnh_g, ghT, HC4, crowA[0:1, 0:D])
    load_vecT(ln2_b, b2T, DC, crowA[0:1, 0:D])
    load_vecT(lnh_b, bhT, HC4, crowA[0:1, 0:D])
    wait("g", vp, p0_v); wait("g", sp, p0_s); wait("g", tp, p0_t)
    wmark("g", p0_store_done)
    g_pa_all = dma("g", g_pa, growb2[:], gamma2[None, :])

    wait("v", lm1, lm1_all)
    for ic in range(DC):
        vop("tensor_scalar", fc1T[:, ic * H4:(ic + 1) * H4], fc1T[:, ic * H4:(ic + 1) * H4],
            g2T[:, ic:ic + 1], None, ALU.mult)
    v_f1fold = vp.n
    wait("t", vp, v_f1fold)
    for ch in range(8):
        if ch > 0:
            wait("t", vp, vp.n)
        for ic in range(DC):
            top("matmul", P2[0:1, 0:512], b2T[:, ic:ic + 1],
                fc1T[:, ic * H4 + ch * 512: ic * H4 + ch * 512 + 512],
                start=(ic == 0), stop=(ic == DC - 1), inc=(1 if ic == DC - 1 else 0))
        t_ch = tp.n
        wait("v", tp, t_ch)
        vop("tensor_copy", crowA[0:1, ch * 512:(ch + 1) * 512], P2[0:1, 0:512])
    vop("tensor_tensor", crowA[0:1, :], crowA[0:1, :], f1row[0:1, :], ALU.add)
    v_c1f = vop("tensor_copy", c1f_b[0:1, :], crowA[0:1, :])
    wait("v", s_f2, s_f2.n)
    for hc in range(HC4):
        vop("tensor_scalar", fc2T[:, hc * D:(hc + 1) * D], fc2T[:, hc * D:(hc + 1) * D],
            ghT[:, hc:hc + 1], None, ALU.mult)
    v_f2fold = vp.n
    wait("t", vp, v_f2fold)
    for ch in range(2):
        n0 = ch * 512
        if ch > 0:
            wait("t", vp, vp.n)
        for hc in range(HC4):
            top("matmul", P2[0:1, 0:512], bhT[:, hc:hc + 1],
                fc2T[:, hc * D + n0: hc * D + n0 + 512],
                start=(hc == 0), stop=(hc == HC4 - 1), inc=(1 if hc == HC4 - 1 else 0))
        t_ch = tp.n
        wait("v", tp, t_ch)
        vop("tensor_copy", crowA[0:1, n0:n0 + 512], P2[0:1, 0:512])
    vop("tensor_tensor", crowA[0:1, 0:D], crowA[0:1, 0:D], f2row[0:1, :], ALU.add)
    vop("tensor_tensor", crowA[0:1, 0:D], crowA[0:1, 0:D], grow2[0:1, :], ALU.mult)
    v_c2f = vop("tensor_copy", c2f_b[0:1, :], crowA[0:1, 0:D])
    wait("t", g_pa, g_pa_all)
    top("matmul", P4[:, 0:512], ones_b[0:1, 0:128], growb2[0:1, 0:512], start=True, stop=True)
    t_bg2 = top("matmul", P4[:, 512:1024], ones_b[0:1, 0:128], growb2[0:1, 512:1024],
                start=True, stop=True, inc=1)
    wait("v", tp, t_bg2)
    vop("tensor_copy", bcg2[:], P4[:, 0:1024])
    for hc in range(HC4):
        vop("tensor_tensor", fc2T[:, hc * D:(hc + 1) * D], fc2T[:, hc * D:(hc + 1) * D],
            bcg2[:], ALU.mult)
    v_f2done = vp.n

    x1t_free = [None, None, None]
    zb_free = [None, None]
    zT_free = [None, None]
    hb_free = [None, None]
    hT_free = [None, None]
    sst_free = [None]
    p4h_free = [(tp, t_bg2), (tp, t_bg2)]
    p2_free_pa = None
    p1a_free = None
    p1b_free = None
    ps_trA = P1a[:].bitcast(BF16)
    ps_trB = P1b[:].bitcast(BF16)

    pa_fold_done = [(vp, v_f2done), (vp, v_c1f), (vp, v_c2f)]
    n_t = n_tiles
    v_zb_t = [0] * (n_t + 1)
    v_hq_t = [0] * n_t
    s_zT_t = [0] * n_t
    s_hb_t = [0] * n_t
    s_hT_t = [0] * n_t
    t_ztr_t = [0] * n_t
    t_fc1_t = [[0] * 4 for _ in range(n_t)]
    t_fc2_t = [0] * n_t
    v_s_t = [0] * n_t
    ld_x1_t = [0] * n_t

    wmark("y", pa_fold_done)
    ld_x1_t[0] = dma("y", ch_l1[0], x1t[0], x1_d[0:128, :])
    ld_x1_t[1] = dma("y", ch_l1[1], x1t[1], x1_d[128:256, :])
    wait("v", ch_l1[0], ld_x1_t[0])
    wmark("v", pa_fold_done)
    v_zb_t[0] = vop("tensor_scalar", zb[0][:], x1t[0][:], zmean[:, 0:1], zrstd[:, 0:1],
                    ALU.subtract, ALU.mult)

    ew_jobs = []
    for e in range(NE):
        ew_jobs.append((ew1bf_d[e], ew1[e]))
        ew_jobs.append((ew2bf_d[e], ew2[e]))
    ew_sched = {}
    steps_avail = list(range(2, max(3, n_t)))
    for k, job in enumerate(ew_jobs):
        ew_sched.setdefault(steps_avail[k % len(steps_avail)], []).append(job)

    for t in range(n_t + 1):
        cur = t if t < n_t else None
        prv = t - 1 if t >= 1 else None
        if t in ew_sched:
            wait("g", sp, sp.n)
            for dst, srcm in ew_sched[t]:
                g_ew_all = dma("g", g_ew, dst, srcm)
        if 2 <= t + 1 < n_t:
            sl = (t + 1) % 3
            wmark("y", x1t_free[sl])
            ld_x1_t[t + 1] = dma("y", ch_l1[sl], x1t[sl],
                                 x1_d[(t + 1) * 128:(t + 2) * 128, :])
        if cur is not None:
            par = cur % 2
            wait("t", vp, v_zb_t[cur])
            wmark("t", p1b_free)
            for ic in range(DC):
                top("transpose", ps_trB[:, ic * 128:(ic + 1) * 128],
                    zb[par][:, ic * 128:(ic + 1) * 128], eye_b[:],
                    inc=(1 if ic == DC - 1 else 0))
            t_ztr_t[cur] = tp.n
            zb_free[par] = (tp, t_ztr_t[cur])
            wait("s", tp, t_ztr_t[cur])
            wmark("s", zT_free[par])
            s_zT_t[cur] = sop("activation", zT[par][:], ps_trB[:, 0:1024], AF.Copy)
            p1b_free = (sp, s_zT_t[cur])
            wait("t", sp, s_zT_t[cur])
            if cur == 0:
                wait("t", vp, v_c1f)
            for q in range(4):
                qh = q % 2
                h0 = q * 1024
                PQ = P4[:, qh * 1024:(qh + 1) * 1024]
                wmark("t", p4h_free[qh])
                for ic in range(DC):
                    for nn in range(2):
                        n0 = nn * 512
                        top("matmul", PQ[:, n0:n0 + 512],
                            zT[par][:, ic * 128:(ic + 1) * 128],
                            fc1T[:, ic * H4 + h0 + n0: ic * H4 + h0 + n0 + 512],
                            start=(ic == 0), stop=False)
                for nn in range(2):
                    n0 = nn * 512
                    top("matmul", PQ[:, n0:n0 + 512], ones_b[0:1, 0:128],
                        c1f_b[0:1, h0 + n0: h0 + n0 + 512], start=False, stop=True,
                        inc=(1 if nn == 1 else 0))
                t_fc1_t[cur][q] = tp.n
                wait("s", tp, t_fc1_t[cur][q])
                if q == 0:
                    wmark("s", hb_free[par])
                s_g = sop("activation", hb[par][:, h0:h0 + 1024], PQ[:], AF.Gelu)
                p4h_free[qh] = (sp, s_g)
                if q == 3:
                    s_hb_t[cur] = s_g
            zT_free[par] = (tp, t_fc1_t[cur][3])
        if prv is not None:
            par1 = prv % 2
            wait("v", sp, s_hb_t[prv])
            for ch in range(8):
                vop("bn_stats", nscr2[:, ch * 6:(ch + 1) * 6],
                    hb[par1][:, ch * 512:(ch + 1) * 512], drain=(ch == 7))
            vop("bn_aggr", nscr[:, 8:10], nscr2[:, 0:48])
            vop("tensor_scalar", nscr[:, 0:1], nscr[:, 9:10], EPS, None, ALU.add)
            newton_rstd(nscr[:, 0:1], nscr[:, 10:11], nscr[:, 2:3], nscr[:, 3:4], nscr[:, 4:5])
            v_hq_t[prv] = vop("tensor_scalar", hb[par1][:], hb[par1][:],
                              nscr[:, 8:9], nscr[:, 10:11], ALU.subtract, ALU.mult)
        if t + 1 < n_t:
            nxt = t + 1
            wait("v", ch_l1[nxt % 3 if nxt >= 2 else nxt], ld_x1_t[nxt])
            wmark("v", zb_free[nxt % 2])
            v_zb_t[nxt] = vop("tensor_scalar", zb[nxt % 2][:], x1t[nxt % 3][:],
                              zmean[:, nxt:nxt + 1], zrstd[:, nxt:nxt + 1],
                              ALU.subtract, ALU.mult)
        if prv is not None:
            par1 = prv % 2
            wait("t", vp, v_hq_t[prv])
            for rnd in range(4):
                PB = ps_trA if rnd % 2 == 0 else ps_trB
                wmark("t", p1a_free if rnd % 2 == 0 else p1b_free)
                for j in range(8):
                    hc = rnd * 8 + j
                    top("transpose", PB[:, j * 128:(j + 1) * 128],
                        hb[par1][:, hc * 128:(hc + 1) * 128], eye_b[:],
                        inc=(1 if j == 7 else 0))
                t_htr = tp.n
                wait("s", tp, t_htr)
                if rnd == 0:
                    wmark("s", hT_free[par1])
                s_hT = sop("activation", hT[par1][:, rnd * 1024:(rnd + 1) * 1024],
                           PB[:, 0:1024], AF.Copy)
                if rnd % 2 == 0:
                    p1a_free = (sp, s_hT)
                else:
                    p1b_free = (sp, s_hT)
            s_hT_t[prv] = sp.n
            hb_free[par1] = (tp, tp.n)
            wait("t", sp, s_hT_t[prv])
            wmark("t", p2_free_pa)
            if prv == 0:
                wait("t", vp, v_c2f)
            for hc in range(HC4):
                for nn in range(2):
                    n0 = nn * 512
                    top("matmul", P2[:, n0:n0 + 512], hT[par1][:, hc * 128:(hc + 1) * 128],
                        fc2T[:, hc * D + n0: hc * D + n0 + 512],
                        start=(hc == 0), stop=False)
            for nn in range(2):
                n0 = nn * 512
                top("matmul", P2[:, n0:n0 + 512], ones_b[0:1, 0:128], c2f_b[0:1, n0:n0 + 512],
                    start=False, stop=True, inc=(1 if nn == 1 else 0))
            t_fc2_t[prv] = tp.n
            hT_free[par1] = (tp, t_fc2_t[prv])
            wait("v", tp, t_fc2_t[prv])
            wmark("v", sst_free[0])
            v_s_t[prv] = vop("scalar_tensor_tensor", sst[:], P2[:], 1.0,
                             x1t[prv % 3][:], ALU.mult, ALU.add)
            p2_free_pa = (vp, v_s_t[prv])
            x1t_free[prv % 3] = (vp, v_s_t[prv])
            wait("y", vp, v_s_t[prv])
            st_s = dma("y", ch_ss[0], s_d[prv * 128:(prv + 1) * 128, :], sst[:])
            sst_free[0] = (ch_ss[0], st_s)

    pa_v, pa_s, pa_t = vp.n, sp.n, tp.n
    al.pop()

    al.push()
    xnT_sup = al.buf(DC * 1024, BF16)
    acc = al.buf(8 * D, F32)
    ehT = [al.buf(HEC * 512, BF16) for _ in range(2)]
    wslot = [al.buf(16 * 1024, BF16) for _ in range(3)]
    wT_buf = al.buf(n_tiles * 128, BF16, parts=8)
    eb2b = al.buf(D, BF16, parts=8)
    sf = [al.buf(D, BF16) for _ in range(2)]
    gf = [al.buf(D, BF16) for _ in range(2)]
    xf = [al.buf(D, F32) for _ in range(2)]

    barrier(("v", "s", "t"), pa_v, pa_s, pa_t)
    wait("y", vp, pa_v); wait("y", sp, pa_s); wait("y", tp, pa_t)
    wmark("y", [(ch_ss[0], ch_ss[0].n), (ch_ss[1], ch_ss[1].n),
                (ch_l1[0], ch_l1[0].n), (ch_l1[1], ch_l1[1].n), (ch_l1[2], ch_l1[2].n)])
    wait("y", g_ew, g_ew_all)
    ld_wt = dma("y", ch_wt, wT_buf[:], wT_d[:])
    wait("g", vp, pa_v); wait("g", sp, pa_s); wait("g", tp, pa_t)
    ld_eb2 = dma("g", g_eb2, eb2b[:], eb2[:])

    S1 = lambda k: (0, 2, 1)[k % 3]
    S2 = lambda k: (1, 0, 2)[k % 3]
    slot_free = [None, None, None]
    ps_eh_free = [None, None]
    ps_eo_free = [None, None]
    ps_init_free_h = [None]
    ehT_free = [None, None]
    acc_free = None
    xnT_sup_free = None
    sfxf_free = [None, None, None, None]
    gf_free = [None, None]
    ps_eo = [P2, P4[:, 0:1024]]
    ps_init = P4[:, 1024:2048]
    out_last = 0

    def load_slot(slot, which, e_idx):
        wmark("y", slot_free[slot])
        c = DC if which == 1 else HEC
        src = ew1bf_d if which == 1 else ew2bf_d
        return dma("y", ch_w[slot], wslot[slot].rearrange("p (c n) -> p c n", c=c),
                   src[e_idx].rearrange("(c p) n -> p c n", p=128))

    n_exp_total = n_supers * NE
    ld_w1v = {0: load_slot(S1(0), 1, 0)}
    ld_w2v = {0: load_slot(S2(0), 2, 0)}
    ld_xnT_a_v = {}
    ld_xnT_b_v = {0: 0}
    wmark("y", xnT_sup_free)
    ld_xnT_a_v[0] = dma("y", ch_xt, xnT_sup.rearrange("p (ic n) -> p ic n", ic=DC),
                        xnT_d[0].rearrange("ic p n -> p ic n"))

    for st in range(n_supers):
        ld_xnT_a, ld_xnT_b = ld_xnT_a_v[st], ld_xnT_b_v[st]
        v_acc_last = [0] * 8
        wmark("y", acc_free[0] if acc_free else None)
        ld_acc_a = dma("y", ch_ac, acc.rearrange("p (t n) -> p t n", t=8)[:, 0:4, :],
                       xn_d[st * 1024:st * 1024 + 512, :].rearrange("(t p) n -> p t n", p=128))
        wmark("y", acc_free[1] if acc_free else None)
        ld_acc_b = dma("y", ch_ac2, acc.rearrange("p (t n) -> p t n", t=8)[:, 4:8, :],
                       xn_d[st * 1024 + 512:(st + 1) * 1024, :].rearrange("(t p) n -> p t n", p=128))
        def emit_init_tile(t8):
            tile = st * 8 + t8
            if st == 0 and t8 == 0:
                wait("t", ch_wt, ld_wt)
                wait("t", g_eb2, ld_eb2)
            wmark("t", ps_init_free_h[0])
            for nn in range(2):
                n0 = nn * 512
                top("matmul", ps_init[:, n0:n0 + 512],
                    wT_buf[0:8, tile * 128:(tile + 1) * 128],
                    eb2b[0:8, n0:n0 + 512], start=True, stop=True,
                    inc=(1 if nn == 1 else 0))
            t_init = tp.n
            wait("v", tp, t_init)
            if t8 == 0:
                wait("v", ch_ac, ld_acc_a)
            if t8 == 4:
                wait("v", ch_ac2, ld_acc_b)
            v_init = vop("scalar_tensor_tensor", acc[:, t8 * D:(t8 + 1) * D], ps_init,
                         1.0, acc[:, t8 * D:(t8 + 1) * D], ALU.mult, ALU.add)
            ps_init_free_h[0] = (vp, v_init)
        for e in range(NE):
            k = st * 8 + e
            i1, i2 = S1(k), S2(k)
            ld_w1, ld_w2 = ld_w1v[k], ld_w2v[k]
            if k + 1 < n_exp_total:
                ld_w1v[k + 1] = load_slot(S1(k + 1), 1, (k + 1) % NE)
            for hh in range(2):
                hpar = hh
                for hc in range(HEC):
                    pp = hc % 2
                    PE_ = P1a if pp == 0 else P1b
                    wmark("t", ps_eh_free[pp])
                    if hc == 0 and hh == 0:
                        wait("t", ch_w[i1], ld_w1)
                        if e == 0:
                            wait("t", ch_xt, ld_xnT_a)
                    if hc == 0 and hh == 1 and e == 0 and ld_xnT_b:
                        wait("t", ch_xt2, ld_xnT_b)
                    for ic in range(DC):
                        top("matmul", PE_[:],
                            wslot[i1][:, ic * HE + hc * 128: ic * HE + (hc + 1) * 128],
                            xnT_sup[:, ic * 1024 + hh * 512: ic * 1024 + hh * 512 + 512],
                            start=(ic == 0), stop=(ic == DC - 1), inc=(1 if ic == DC - 1 else 0))
                    t_mm1 = tp.n
                    wait("s", tp, t_mm1)
                    if hc == 0:
                        wmark("s", ehT_free[hpar])
                        if st == 0 and e == 0 and hh == 0:
                            wait("s", vp, v_eb1)
                    s_eh = sop("activation", ehT[hpar][:, hc * 512:(hc + 1) * 512], PE_[:],
                               AF.Silu, bias=eb1T[:, hc * 8 + e: hc * 8 + e + 1])
                    ps_eh_free[pp] = (sp, s_eh)
                    if e == 1 and hc % 4 == 3:
                        emit_init_tile((hh * 16 + hc) // 4)
                if hh == 1:
                    slot_free[i1] = (tp, tp.n)
                    if k + 1 < n_exp_total:
                        ld_w2v[k + 1] = load_slot(S2(k + 1), 2, (k + 1) % NE)
                if e == NE - 1 and st + 1 < n_supers:
                    t_xnh = tp.n
                    wmark("y", (tp, t_xnh))
                    chn = ch_xt if hh == 0 else ch_xt2
                    val = dma("y", chn,
                              xnT_sup.rearrange("p (ic n) -> p ic n", ic=DC)[:, :, hh * 512:(hh + 1) * 512],
                              xnT_d[st + 1].rearrange("ic p n -> p ic n")[:, :, hh * 512:(hh + 1) * 512])
                    if hh == 0:
                        ld_xnT_a_v[st + 1] = val
                    else:
                        ld_xnT_b_v[st + 1] = val
                s_eh_done = sp.n
                wait("t", sp, s_eh_done)
                if hh == 0:
                    wait("t", ch_w[i2], ld_w2)
                for tl in range(4):
                    t8 = hh * 4 + tl
                    tile = st * 8 + t8
                    pp = tl % 2
                    wmark("t", ps_eo_free[pp])
                    for hc in range(HEC):
                        for nn in range(2):
                            n0 = nn * 512
                            top("matmul", ps_eo[pp][:, n0:n0 + 512],
                                ehT[hpar][:, hc * 512 + tl * 128: hc * 512 + (tl + 1) * 128],
                                wslot[i2][:, hc * 1024 + n0: hc * 1024 + n0 + 512],
                                start=(hc == 0), stop=(hc == HEC - 1),
                                inc=(1 if (hc == HEC - 1 and nn == 1) else 0))
                    t_mm2 = tp.n
                    wait("v", tp, t_mm2)
                    if e == 0 and tl == 0:
                        wait("v", ch_ac if hh == 0 else ch_ac2, ld_acc_a if hh == 0 else ld_acc_b)
                    v_acc = vop("scalar_tensor_tensor", acc[:, t8 * D:(t8 + 1) * D], ps_eo[pp],
                                w_buf[:, tile * 8 + e: tile * 8 + e + 1],
                                acc[:, t8 * D:(t8 + 1) * D], ALU.mult, ALU.add)
                    ps_eo_free[pp] = (vp, v_acc)
                    if e == NE - 1:
                        v_acc_last[t8] = v_acc
                ehT_free[hpar] = (tp, tp.n)
            slot_free[i2] = (tp, tp.n)
        v_experts, t_experts, s_experts = vp.n, tp.n, sp.n
        for t8 in range(8):
            fpar = t8 % 2
            wmark("g", sfxf_free[fpar])
            ld_s = dma("g", ch_sf[fpar], sf[fpar],
                       s_d[(st * 8 + t8) * 128:(st * 8 + t8 + 1) * 128, :])
            wmark("y", sfxf_free[2 + fpar])
            ld_xx = dma("y", ch_xf[fpar], xf[fpar],
                        x_d[(st * 8 + t8) * 128:(st * 8 + t8 + 1) * 128, :])
            wait("s", vp, v_acc_last[t8])
            wmark("s", gf_free[fpar])
            s_g = sop("activation", gf[fpar], acc[:, t8 * D:(t8 + 1) * D], AF.Silu)
            wait("v", sp, s_g)
            wait("v", ch_sf[fpar], ld_s)
            vop("tensor_tensor", acc[:, t8 * D:(t8 + 1) * D], gf[fpar], sf[fpar], ALU.mult)
            gf_free[fpar] = (vp, vp.n)
            sfxf_free[fpar] = (vp, vp.n)
            wait("v", ch_xf[fpar], ld_xx)
            v_out = vop("tensor_tensor", acc[:, t8 * D:(t8 + 1) * D],
                        acc[:, t8 * D:(t8 + 1) * D], xf[fpar], ALU.add)
            sfxf_free[2 + fpar] = (vp, v_out)
            wait("g", vp, v_out)
            och = g_out if t8 < 4 else g_outb
            out_last = dma("g", och, out_d[(st * 8 + t8) * 128:(st * 8 + t8 + 1) * 128, :],
                           acc[:, t8 * D:(t8 + 1) * D])
            if t8 == 3:
                out_a = out_last
        acc_free = [(g_out, out_a), (g_outb, out_last)]

    wait("g", g_out, g_out.n)
    wait("g", g_outb, g_outb.n)
    al.pop()

    with nc.Block() as blk:
        @blk.sync
        def _(h):
            for fn in sched["y"]:
                fn(h)

        @blk.gpsimd
        def _(h):
            for fn in sched["g"]:
                fn(h)

        @blk.vector
        def _(h):
            for fn in sched["v"]:
                fn(h)

        @blk.scalar
        def _(h):
            for fn in sched["s"]:
                fn(h)

        @blk.tensor
        def _(h):
            for fn in sched["t"]:
                fn(h)

    es.close()
    return {e: len(sched[e]) for e in sched}

from concourse.bass_utils import run_bass_kernel_spmd
import ml_dtypes

N_CORES = 8
T_CORE = 4096

_WMAP = {
    "ln1_g": "L0_attn_ln1_g", "ln1_b": "L0_attn_ln1_b",
    "proj_w": "L0_attn_proj_w", "proj_b": "L0_attn_proj_b",
    "ln2_g": "L0_attn_ln2_g", "ln2_b": "L0_attn_ln2_b",
    "fc1_w": "L0_attn_fc1_w", "fc1_b": "L0_attn_fc1_b",
    "lnh_g": "L0_attn_lnh_g", "lnh_b": "L0_attn_lnh_b",
    "fc2_w": "L0_attn_fc2_w", "fc2_b": "L0_attn_fc2_b",
    "gamma1": "L0_attn_gamma1", "gamma2": "L0_attn_gamma2",
    "rw1": "L0_ffn_rw1", "rb1": "L0_ffn_rb1",
    "rw2": "L0_ffn_rw2", "rb2": "L0_ffn_rb2",
    "ew1": "L0_ffn_ew1", "eb1": "L0_ffn_eb1",
    "ew2": "L0_ffn_ew2", "eb2": "L0_ffn_eb2",
}


def kernel(**inputs):
    x = np.ascontiguousarray(np.asarray(inputs["L0_attn_x"], np.float32).reshape(-1, 1024))
    assert x.shape[0] == N_CORES * T_CORE
    shards = np.split(x, N_CORES, axis=0)

    base = {k: np.ascontiguousarray(np.asarray(inputs[v], np.float32)) for k, v in _WMAP.items()}
    base["eye_b"] = np.eye(128, dtype=ml_dtypes.bfloat16)
    base["eye_f"] = np.eye(128, dtype=np.float32)

    nc = bass.Bass("TRN2", target_bir_lowering=False)
    build(nc, T=T_CORE)

    in_maps = [dict(base, x=shards[c]) for c in range(N_CORES)]
    res = run_bass_kernel_spmd(nc, in_maps, core_ids=list(range(N_CORES)))
    out = np.concatenate([np.asarray(res.results[c]["out"], np.float32) for c in range(N_CORES)], axis=0)
    return out.reshape(4, 8192, 1024)


# version 22
# speedup vs baseline: 1.2687x; 1.0066x over previous
import numpy as np
import concourse.bass as bass
import concourse.mybir as mybir
from contextlib import ExitStack

F32 = mybir.dt.float32
BF16 = mybir.dt.bfloat16
AF = mybir.ActivationFunctionType
ALU = mybir.AluOpType
AX = mybir.AxisListType

D, DC = 1024, 8
H4, HC4 = 4096, 32
HE, HEC = 2048, 16
NE = 8
EPS = 1e-5
NEWTON_LINES = (4.0, 1.0, 0.25, 0.0625)
NEWTON_ITERS = 3


def build(nc, T=4096):
    assert T % 1024 == 0
    n_tiles, n_groups, n_supers = T // 128, T // 512, T // 1024

    def din(name, shape, dt=F32):
        return nc.dram_tensor(name, shape, dt, kind="ExternalInput").ap()

    x_d = din("x", [T, D])
    ln1_g, ln1_b = din("ln1_g", [D]), din("ln1_b", [D])
    proj_w, proj_b = din("proj_w", [D, D]), din("proj_b", [D])
    ln2_g, ln2_b = din("ln2_g", [D]), din("ln2_b", [D])
    fc1_w, fc1_b = din("fc1_w", [H4, D]), din("fc1_b", [H4])
    lnh_g, lnh_b = din("lnh_g", [H4]), din("lnh_b", [H4])
    fc2_w, fc2_b = din("fc2_w", [D, H4]), din("fc2_b", [D])
    gamma1, gamma2 = din("gamma1", [D]), din("gamma2", [D])
    rw1, rb1 = din("rw1", [D, D]), din("rb1", [D])
    rw2, rb2 = din("rw2", [D, NE]), din("rb2", [NE])
    ew1, eb1 = din("ew1", [NE, D, HE]), din("eb1", [NE, HE])
    ew2, eb2 = din("ew2", [NE, HE, D]), din("eb2", [NE, D])
    eye_b_d = din("eye_b", [128, 128], BF16)
    eye_f_d = din("eye_f", [128, 128], F32)
    out_d = nc.dram_tensor("out", [T, D], F32, kind="ExternalOutput").ap()

    pbf_d = nc.dram_tensor("pbf", [D, D], BF16, kind="Internal").ap()
    f1bf_d = nc.dram_tensor("f1bf", [H4, D], BF16, kind="Internal").ap()
    f2bf_d = nc.dram_tensor("f2bf", [D, H4], BF16, kind="Internal").ap()
    ew1bf_d = nc.dram_tensor("ew1bf", [NE, D, HE], BF16, kind="Internal").ap()
    ew2bf_d = nc.dram_tensor("ew2bf", [NE, HE, D], BF16, kind="Internal").ap()
    xn_d = nc.dram_tensor("xn_sc", [T, D], F32, kind="Internal").ap()
    xnT_d = nc.dram_tensor("xnT_sc", [n_supers, 2, 128, DC * 512], BF16, kind="Internal").ap()
    x1_d = nc.dram_tensor("x1_sc", [T, D], F32, kind="Internal").ap()
    s_d = nc.dram_tensor("s_sc", [T, D], F32, kind="Internal").ap()
    wT_d = nc.dram_tensor("wT_sc", [8, T // 128 * 128], BF16, kind="Internal").ap()

    es = ExitStack()
    ARENA_BYTES = 207 * 1024
    arena = es.enter_context(nc.sbuf_tensor("arena", [128, ARENA_BYTES], mybir.dt.int8))[:]

    class Alloc:
        def __init__(self):
            self.off, self.marks = 0, []
        def buf(self, cols, dt=F32, parts=128):
            isz = 4 if dt == F32 else 2
            self.off = (self.off + 31) & ~31
            ap = arena[0:parts, self.off:self.off + cols * isz].bitcast(dt)
            self.off += cols * isz
            assert self.off <= ARENA_BYTES, f"arena overflow {self.off}"
            return ap
        def push(self):
            self.marks.append(self.off)
        def pop(self):
            self.off = self.marks.pop()
    al = Alloc()

    P4 = es.enter_context(nc.psum_tensor("P4", [128, 2048], F32))[:]
    P2 = es.enter_context(nc.psum_tensor("P2", [128, 1024], F32))[:]
    P1a = es.enter_context(nc.psum_tensor("P1a", [128, 512], F32))[:]
    P1b = es.enter_context(nc.psum_tensor("P1b", [128, 512], F32))[:]
    ps_tr_bf = P4[:, 0:512].bitcast(BF16)
    ps_lT = P4[0:8, 512:1024]
    ps_lt2 = P4[:, 1024:1056]
    ps_wT = P4[0:8, 1536:1664]

    sched = {e: [] for e in ("y", "g", "v", "s", "t")}

    class Sem:
        def __init__(self, name):
            self.h = es.enter_context(nc.semaphore(name))
            self.n = 0
            self.name = name
    vp, sp, tp = Sem("vp"), Sem("sp"), Sem("tp")
    g_pbf, g_f1, g_f2 = Sem("g_pbf"), Sem("g_f1"), Sem("g_f2")
    g_small, g_pa, g_ew, g_out = Sem("g_small"), Sem("g_pa"), Sem("g_ew"), Sem("g_out")
    lm0, lm0b, lm1, lm_pj = Sem("lm0"), Sem("lm0b"), Sem("lm1"), Sem("lm_pj")
    ch_xg = [Sem("ch_xg0"), Sem("ch_xg1")]
    ch_sxn = [Sem("ch_sxn0"), Sem("ch_sxn1")]
    ch_sxT = [Sem("ch_sxT0"), Sem("ch_sxT1")]
    ch_sx1 = [Sem("ch_sx10"), Sem("ch_sx11")]
    ch_l1 = [Sem("ch_l10"), Sem("ch_l11"), Sem("ch_l12")]
    ch_ss = [Sem("ch_ss0"), Sem("ch_ss1")]
    ch_w = [Sem("ch_w0"), Sem("ch_w1"), Sem("ch_w2")]
    ch_xt, ch_ac, ch_ac2 = Sem("ch_xt"), Sem("ch_ac"), Sem("ch_ac2")
    ch_xt2 = Sem("ch_xt2")
    ch_wt, g_eb2 = Sem("ch_wt"), Sem("g_eb2")
    s_f2, g_outb = Sem("s_f2"), Sem("g_outb")
    ch_row = Sem("ch_row")
    ch_sf = [Sem("ch_sf0"), Sem("ch_sf1")]
    ch_xf = [Sem("ch_xf0"), Sem("ch_xf1")]

    def wait(e, sem, val):
        if val <= 0:
            return
        assert val <= sem.n, f"wait on future {val}>{sem.n} ({sem.name})"
        sched[e].append(lambda h, s=sem.h, v=val: h.wait_ge(s, v))

    def wmark(e, mark):
        if not mark:
            return
        if isinstance(mark, list):
            for m in mark:
                wmark(e, m)
        else:
            wait(e, mark[0], mark[1])

    def dma(e, ch, out, in_, slow=False, transpose=False):
        ch.n += 16
        def fn(h, out=out, in_=in_, slow=slow, tr=transpose, s=ch.h):
            if slow:
                with nc.allow_non_contiguous_dma(reason="tiny chunked vector load"):
                    h.dma_start(out=out, in_=in_).then_inc(s, 16)
            elif tr:
                h.dma_start(out=out, in_=in_, transpose=True).then_inc(s, 16)
            else:
                h.dma_start(out=out, in_=in_).then_inc(s, 16)
        sched[e].append(fn)
        return ch.n

    def vop(name, *args, drain=True, **kw):
        vp.n += 1
        def fn(h, name=name, args=args, kw=kw, drain=drain):
            i = getattr(h, name)(*args, **kw)
            if drain:
                h.drain().then_inc(vp.h, 1)
            else:
                i.then_inc(vp.h, 1)
        sched["v"].append(fn)
        return vp.n

    def sop(name, *args, drain=True, **kw):
        sp.n += 1
        def fn(h, name=name, args=args, kw=kw, drain=drain):
            i = getattr(h, name)(*args, **kw)
            if drain:
                h.drain().then_inc(sp.h, 1)
            else:
                i.then_inc(sp.h, 1)
        sched["s"].append(fn)
        return sp.n

    def top(name, *args, inc=0, **kw):
        tp.n += inc
        def fn(h, name=name, args=args, kw=kw, inc=inc):
            i = getattr(h, name)(*args, **kw)
            if inc:
                i.then_inc(tp.h, inc)
        sched["t"].append(fn)
        return tp.n

    def barrier(engines, vm, sm, tm):
        for e in engines:
            wait(e, vp, vm); wait(e, sp, sm); wait(e, tp, tm)

    eye_b = al.buf(128, BF16)
    eye_f = al.buf(128, F32)
    ones_b = al.buf(512, BF16, parts=1)
    w_buf = al.buf(n_tiles * 8, F32)
    eb1T = al.buf(HEC * NE, F32)
    rb2b = al.buf(NE, BF16, parts=1)
    nscr = al.buf(16, F32)
    nscr2 = al.buf(48, F32)
    zmean = al.buf(n_tiles, F32)
    zrstd = al.buf(n_tiles, F32)

    def newton_rstd(tcol, outcol, s0, s1, s2, lines=NEWTON_LINES, iters=NEWTON_ITERS):
        first = True
        for aa in lines:
            vop("tensor_scalar", s1, tcol, aa * aa, 1.0 / (2 * aa), ALU.add, ALU.mult)
            if first:
                vop("tensor_copy", s0, s1); first = False
            else:
                vop("tensor_tensor", s0, s0, s1, ALU.min)
        vop("tensor_scalar", s2, tcol, 0.5, None, ALU.mult)
        for _ in range(iters):
            vop("reciprocal", s1, s0)
            vop("tensor_tensor", s1, s2, s1, ALU.mult)
            vop("scalar_tensor_tensor", s0, s0, 0.5, s1, ALU.mult, ALU.add)
        return vop("reciprocal", outcol, s0)

    rowst = {"tp_mark": 0, "vp_mark": 0}

    def load_vecT(vec_d_ap, dest, nch, vrow):
        wait("t", lm0, lm0_all)
        for r0 in range(0, nch, 8):
            k = min(8, nch - r0)
            wait("y", tp, rowst["tp_mark"])
            lr = dma("y", ch_row, vrow[0:1, 0:k * 128], vec_d_ap[None, r0 * 128:(r0 + k) * 128])
            wait("t", ch_row, lr)
            wait("t", vp, rowst["vp_mark"])
            for j in range(k):
                top("transpose", P1b[0:128, j:j + 1], vrow[0:1, j * 128:(j + 1) * 128],
                    eye_f[0:1, 0:1], inc=(1 if j == k - 1 else 0))
            rowst["tp_mark"] = tp.n
            wait("v", tp, rowst["tp_mark"])
            rowst["vp_mark"] = vop("tensor_copy", dest[:, r0:r0 + k], P1b[0:128, 0:k])

    dma("g", g_pbf, pbf_d[:], proj_w[:])

    al.push()
    projT = al.buf(DC * D, BF16)
    rw1b = al.buf(DC * D, BF16)
    rw2b = al.buf(DC * NE, BF16)
    xg = [al.buf(4 * D, F32) for _ in range(2)]
    xnf = [al.buf(4 * D, F32) for _ in range(2)]
    xnbb = al.buf(4 * D, BF16)
    xnT_g = [al.buf(8 * 512, BF16) for _ in range(2)]
    uT_g = [al.buf(8 * 512, BF16) for _ in range(2)]
    x1s = [al.buf(D, F32) for _ in range(2)]
    lT_s = [al.buf(512, F32, parts=8) for _ in range(2)]
    l_s = al.buf(32, F32)
    gw = al.buf(128, F32)
    p0stat = al.buf(128, F32)
    bc_c1 = al.buf(D, F32)
    rb1T = al.buf(DC, F32)
    wT_buf = al.buf(n_tiles * 128, BF16, parts=8)
    g1T = al.buf(DC, F32)
    b1T = al.buf(DC, BF16)
    grow1 = al.buf(D, F32, parts=1)
    growb = al.buf(D, BF16, parts=1)
    prow = al.buf(D, F32, parts=1)
    crow = al.buf(D, F32, parts=1)
    crow_b = al.buf(D, BF16, parts=1)
    bcg = al.buf(D, BF16)
    eb1n = al.buf(HE, F32, parts=8)

    dma("g", g_small, rb2b[:], rb2[None, :])
    dma("g", g_small, rw1b.rearrange("p (c n) -> p c n", c=DC),
        rw1.rearrange("(c p) n -> p c n", p=128))
    dma("g", g_small, rw2b.rearrange("p (c n) -> p c n", c=DC),
        rw2.rearrange("(c p) n -> p c n", p=128))
    g_small_all = dma("g", g_small, growb[:], gamma1[None, :])
    g_ew_all = 0

    dma("y", lm0, eye_b[:], eye_b_d[:])
    dma("y", lm0, eye_f[:], eye_f_d[:])
    dma("y", lm0, grow1[:], gamma1[None, :])
    dma("y", lm0, prow[0:1, :], proj_b[None, :])
    lm0_all = lm0.n
    lm0b_all = dma("y", lm0b, eb1n[:], eb1[:])

    ld_x0_early = dma("y", ch_xg[0], xg[0].rearrange("p (s n) -> p s n", s=4),
                      x_d[0:512, :].rearrange("(s p) n -> p s n", p=128))
    wait("y", g_pbf, 16)
    pj_stage = xg[1].bitcast(BF16)
    lm_pj_all = dma("y", lm_pj, pj_stage.rearrange("p (s n) -> p s n", s=8),
                    pbf_d.rearrange("(s p) n -> p s n", p=128))
    wait("t", lm_pj, lm_pj_all)
    wait("t", lm0, lm0_all)
    pj_prev_evac = [0]
    for oc in range(8):
        wait("t", sp, pj_prev_evac[0])
        for ic in range(DC):
            top("transpose", ps_tr_bf[:, ic * 128:(ic + 1) * 128],
                pj_stage[:, oc * D + ic * 128: oc * D + (ic + 1) * 128], eye_b[:],
                inc=(1 if ic == DC - 1 else 0))
        wait("s", tp, tp.n)
        pj_prev_evac[0] = sop("activation",
                              projT.rearrange("p (ic n) -> p ic n", ic=DC)[:, :, oc * 128:(oc + 1) * 128],
                              ps_tr_bf[:].rearrange("p (ic n) -> p ic n", ic=DC), AF.Copy)
    s_pj_done = pj_prev_evac[0]
    pj_tr_done = tp.n
    load_vecT(ln1_g, g1T, DC, crow)
    load_vecT(rb1, rb1T, DC, crow)
    load_vecT(ln1_b, b1T, DC, crow)

    v_ones = vop("memset", ones_b[0:1, :], 1.0)
    wait("v", lm0, lm0_all)
    wait("v", sp, s_pj_done)
    for ic in range(DC):
        vop("tensor_scalar", projT[:, ic * D:(ic + 1) * D], projT[:, ic * D:(ic + 1) * D],
            g1T[:, ic:ic + 1], None, ALU.mult)
    v_p1 = vp.n
    wait("t", vp, v_p1)
    wait("t", g_small, g_small_all)
    for half in range(2):
        n0 = half * 512
        for ic in range(DC):
            top("matmul", P2[0:1, n0:n0 + 512], b1T[:, ic:ic + 1],
                projT[:, ic * D + n0: ic * D + n0 + 512],
                start=(ic == 0), stop=(ic == DC - 1),
                inc=(1 if (half == 1 and ic == DC - 1) else 0))
    t_c1 = tp.n
    wait("v", tp, t_c1)
    vop("tensor_tensor", crow[0:1, :], P2[0:1, 0:D], prow[0:1, :], ALU.add)
    vop("tensor_tensor", crow[0:1, :], crow[0:1, :], grow1[0:1, :], ALU.mult)
    v_c1r = vop("tensor_copy", crow_b[0:1, :], crow[0:1, :])
    wait("t", vp, v_c1r)
    top("matmul", P2[:, 0:512], ones_b[0:1, 0:128], crow_b[0:1, 0:512], start=True, stop=True)
    t_bc1 = top("matmul", P2[:, 512:1024], ones_b[0:1, 0:128], crow_b[0:1, 512:1024],
                start=True, stop=True, inc=1)
    wait("v", tp, t_bc1)
    v_bc1 = vop("tensor_copy", bc_c1[:], P2[:])
    top("matmul", P4[:, 0:512], ones_b[0:1, 0:128], growb[0:1, 0:512], start=True, stop=True)
    t_bg = top("matmul", P4[:, 512:1024], ones_b[0:1, 0:128], growb[0:1, 512:1024],
               start=True, stop=True, inc=1)
    wait("v", tp, t_bg)
    v_bcg = vop("tensor_copy", bcg[:], P4[:, 0:1024])
    for ic in range(DC):
        vop("tensor_tensor", projT[:, ic * D:(ic + 1) * D], projT[:, ic * D:(ic + 1) * D],
            bcg[:], ALU.mult)
    v_projdone = vp.n
    wait("t", lm0b, lm0b_all)
    for hc in range(HEC):
        top("transpose", P1a[0:128, hc * 8:(hc + 1) * 8], eb1n[0:8, hc * 128:(hc + 1) * 128],
            eye_f[0:8, 0:8], inc=(1 if hc == HEC - 1 else 0))
    t_eb1 = tp.n
    wait("v", tp, t_eb1)
    v_eb1 = vop("tensor_copy", eb1T[:], P1a[:, 0:HEC * NE])

    xg_free = [None, None]
    xnf_free = [None, None]
    xnT_free = [None, None]
    uT_free = [None, None]
    x1s_free = [None, None]
    p2_free = (vp, v_bc1)
    p1_free = [(vp, v_eb1), None]
    p4b0_free = (vp, v_bcg)
    p4b1_free = None
    p4b2_free = None
    p4b3_free = None
    lT_free = [None, None]


    w_ready = [0] * n_tiles
    wT_ready = [0] * n_tiles
    pend_lt2 = []
    pend_wT = []

    def flush_wT(force=False):
        nonlocal p4b3_free
        if not pend_wT or (len(pend_wT) < 2 and not force):
            return
        g, w_grp = pend_wT.pop(0)
        wslab = w_buf[:, g * 32:(g + 1) * 32]
        for t in range(4):
            tile = g * 4 + t
            wait("t", vp, w_grp)
            wmark("t", p4b3_free)
            t_wT = top("transpose", ps_wT[:], wslab[:, t * 8:(t + 1) * 8], eye_f[:], inc=1)
            wait("s", tp, t_wT)
            wT_ready[tile] = sop("activation", wT_buf[0:8, tile * 128:(tile + 1) * 128],
                                 ps_wT[:], AF.Copy)
            p4b3_free = (sp, wT_ready[tile])

    def flush_lt2():
        nonlocal p4b2_free, lT_free
        if not pend_lt2:
            return
        g, s_lT = pend_lt2.pop(0)
        wait("t", sp, s_lT)
        wmark("t", p4b2_free)
        for t in range(4):
            top("transpose", ps_lt2[:, t * 8:(t + 1) * 8], lT_s[g % 2][0:8, t * 128:(t + 1) * 128],
                eye_f[0:8, 0:8], inc=(1 if t == 3 else 0))
        t_lt2 = tp.n
        lT_free[g % 2] = (tp, t_lt2)
        wait("v", tp, t_lt2)
        v_l = vop("tensor_copy", l_s[:], ps_lt2[:])
        p4b2_free = (vp, v_l)
        wslab = w_buf[:, g * 32:(g + 1) * 32]
        vop("tensor_reduce", gw[:, 0:4], v3(l_s[:]), AX.X, ALU.max)
        vop("tensor_tensor", v3(gw[:, 32:64]), v3(l_s[:]), bc4(gw[:, 0:4]), ALU.is_equal)
        vop("scalar_tensor_tensor", v3(gw[:, 96:128]), v3(gw[:, 32:64]), -1e30, v3(l_s[:]),
            ALU.mult, ALU.add)
        vop("tensor_reduce", gw[:, 4:8], v3(gw[:, 96:128]), AX.X, ALU.max)
        vop("tensor_tensor", v3(gw[:, 64:96]), v3(l_s[:]), bc4(gw[:, 4:8]), ALU.is_equal)
        vop("tensor_tensor", gw[:, 8:12], gw[:, 0:4], gw[:, 4:8], ALU.subtract)
        v_dc = vop("tensor_scalar", gw[:, 12:16], gw[:, 8:12], 1e-30, None, ALU.max)
        wait("s", vp, v_dc)
        s_sv = sop("activation", gw[:, 16:20], gw[:, 12:16], AF.Silu)
        wait("v", sp, s_sv)
        vop("reciprocal", gw[:, 20:24], gw[:, 12:16])
        vop("tensor_tensor", gw[:, 24:28], gw[:, 16:20], gw[:, 20:24], ALU.mult)
        vop("tensor_scalar", gw[:, 28:32], gw[:, 24:28], -1.0, 1.0, ALU.mult, ALU.add)
        vop("tensor_tensor", v3(wslab), v3(gw[:, 32:64]), bc4(gw[:, 24:28]), ALU.mult, drain=False)
        vop("tensor_tensor", v3(gw[:, 96:128]), v3(gw[:, 64:96]), bc4(gw[:, 28:32]), ALU.mult)
        w_grp = vop("tensor_tensor", wslab, wslab, gw[:, 96:128], ALU.add)
        for t in range(4):
            w_ready[g * 4 + t] = w_grp
        pend_wT.append((g, w_grp))

    def v3(ap32):
        return ap32.rearrange("p (t e) -> p t e", e=8)

    def bc4(ap4):
        return ap4.rearrange("p (t o) -> p t o", o=1).broadcast_to((128, 4, 8))

    def _xv(par):
        o = par * 64
        blk = p0stat[:, o:o + 8].rearrange("p (t w) -> p t w", w=2)
        return dict(mean=blk[:, :, 0:1], var=blk[:, :, 1:2],
                    tx4=p0stat[:, o + 8:o + 12], rx4=p0stat[:, o + 12:o + 16],
                    t24=p0stat[:, o + 16:o + 20], r24=p0stat[:, o + 20:o + 24],
                    ns0=p0stat[:, o + 24:o + 28], ns1=p0stat[:, o + 28:o + 32],
                    aggr=lambda t: p0stat[:, o + 2 * t:o + 2 * t + 2])
    XV = [_xv(0), _xv(1)]
    z_mean = p0stat[:, 32:40].rearrange("p (t w) -> p t w", w=2)[:, :, 0:1]
    z_var = p0stat[:, 32:40].rearrange("p (t w) -> p t w", w=2)[:, :, 1:2]
    ns2 = p0stat[:, 40:44]
    tz4, rz4 = p0stat[:, 44:48], p0stat[:, 48:52]
    zaggr = lambda t: p0stat[:, 32 + 2 * t:34 + 2 * t]

    ld_x_t = [0] * (n_groups + 1)
    v_casts_t = [0] * n_groups
    v_xn_last_t = [0] * n_groups
    t_tr_last = [0]

    xpose_done = [0] * n_groups

    def emit_xpose(g):
        nonlocal p4b0_free
        par = g % 2
        wait("t", sp, v_casts_t[g])
        for t in range(4):
            wmark("t", p4b0_free)
            for ic in range(DC):
                top("transpose", ps_tr_bf[:, ic * 128:(ic + 1) * 128],
                    xnbb[:, t * D + ic * 128: t * D + (ic + 1) * 128], eye_b[:],
                    inc=(1 if ic == DC - 1 else 0))
            t_tr = tp.n
            wait("s", tp, t_tr)
            if t == 0:
                wmark("s", xnT_free[par])
            s_ev = sop("activation",
                       xnT_g[par].rearrange("p (ic n) -> p ic n", ic=DC)[:, :, t * 128:(t + 1) * 128],
                       ps_tr_bf[:].rearrange("p (ic n) -> p ic n", ic=DC), AF.Copy)
            p4b0_free = (sp, s_ev)
        t_tr_last[0] = tp.n
        xpose_done[g] = sp.n

    def emit_stats(g):
        par = g % 2
        X = XV[par]
        wait("v", ch_xg[par], ld_x_t[g])
        for t in range(4):
            xt = xg[par][:, t * D:(t + 1) * D]
            vop("bn_stats", nscr2[:, 0:6], xt[:, 0:512], drain=False)
            vop("bn_stats", nscr2[:, 6:12], xt[:, 512:1024])
            vop("bn_aggr", X["aggr"](t), nscr2[:, 0:12])
        vop("tensor_scalar", X["tx4"], X["var"], EPS, None, ALU.add)
        newton_rstd(X["tx4"], X["rx4"], X["ns0"], X["ns1"], ns2, lines=(1.0,), iters=2)
        vop("tensor_tensor", X["t24"], X["var"], X["rx4"], ALU.mult)
        vop("tensor_tensor", X["t24"], X["t24"], X["rx4"], ALU.mult)
        vop("tensor_scalar", X["t24"], X["t24"], EPS, None, ALU.add)
        newton_rstd(X["t24"], X["r24"], X["ns0"], X["ns1"], ns2, lines=(1.0,), iters=2)
        wmark("v", xnf_free[par])
        for t in range(4):
            vop("tensor_scalar", xnf[par][:, t * D:(t + 1) * D], xg[par][:, t * D:(t + 1) * D],
                X["mean"][:, t], X["rx4"][:, t:t + 1], ALU.subtract, ALU.mult,
                drain=(t == 3))
        v_xn_last_t[g] = vp.n
        wait("s", vp, v_xn_last_t[g])
        wait("s", tp, t_tr_last[0])
        v_casts_t[g] = sop("activation", xnbb[:], xnf[par][:], AF.Copy)

    ld_x_t[0] = ld_x0_early
    emit_stats(0)
    emit_xpose(0)

    for g in range(n_groups):
        par = g % 2
        if n_groups >= 7 and 2 <= g <= 5:
            wait("g", sp, v_casts_t[g - 1])
            q = g - 2
            g_f1_all = dma("g", g_f1, f1bf_d[q * 1024:(q + 1) * 1024, :],
                           fc1_w[q * 1024:(q + 1) * 1024, :])
            g_f2_all = dma("g", g_f2, f2bf_d[q * 256:(q + 1) * 256, :],
                           fc2_w[q * 256:(q + 1) * 256, :])
        elif n_groups < 7 and g == min(2, n_groups - 1):
            wait("g", sp, v_casts_t[g - 1])
            g_f1_all = dma("g", g_f1, f1bf_d[:], fc1_w[:])
            g_f2_all = dma("g", g_f2, f2bf_d[:], fc2_w[:])
        if g + 1 < n_groups:
            np_ = (g + 1) % 2
            if g == 0:
                wait("y", tp, pj_tr_done)
            wmark("y", xg_free[np_])
            ld_x_t[g + 1] = dma("y", ch_xg[np_], xg[np_].rearrange("p (s n) -> p s n", s=4),
                                x_d[(g + 1) * 512:(g + 2) * 512, :].rearrange("(s p) n -> p s n", p=128))
        if g > 0:
            emit_xpose(g)
        flush_lt2()
        v_xn_last = v_xn_last_t[g]
        v_xnT = xpose_done[g]
        wait("y", vp, v_xn_last)
        st_xn = dma("y", ch_sxn[par], xn_d[g * 512:(g + 1) * 512, :].rearrange("(s p) n -> p s n", p=128),
                    xnf[par].rearrange("p (s n) -> p s n", s=4))
        wait("y", sp, v_xnT)
        wait("t", sp, v_xnT)
        st_i, hf = g // 2, g % 2
        st_xnT = dma("y", ch_sxT[par],
                     xnT_d[st_i, hf], xnT_g[par][:])
        wait("t", sp, v_xnT)
        for oc in range(DC):
            ppar = oc % 2
            PU = P1a if ppar == 0 else P1b
            wmark("t", p1_free[ppar])
            for ic in range(DC):
                top("matmul", PU[:], rw1b[:, ic * D + oc * 128: ic * D + (oc + 1) * 128],
                    xnT_g[par][:, ic * 512:(ic + 1) * 512],
                    start=(ic == 0), stop=(ic == DC - 1), inc=(1 if ic == DC - 1 else 0))
            t_u = tp.n
            wait("s", tp, t_u)
            if g == 0 and oc == 0:
                wait("s", lm0, lm0_all)
            if oc == 0:
                wmark("s", uT_free[par])
            s_u = sop("activation", uT_g[par][:, oc * 512:(oc + 1) * 512], PU[:],
                      AF.Silu, bias=rb1T[:, oc:oc + 1])
            p1_free[ppar] = (sp, s_u)
        s_uT = sp.n
        wait("t", sp, s_uT)
        wmark("t", p4b1_free)
        for oc in range(DC):
            top("matmul", ps_lT[:], rw2b[:, oc * NE:(oc + 1) * NE],
                uT_g[par][:, oc * 512:(oc + 1) * 512], start=(oc == 0), stop=False)
        t_lg = top("matmul", ps_lT[:], rb2b[0:1, 0:NE], ones_b[0:1, 0:512],
                   start=False, stop=True, inc=1)
        uT_free[par] = (tp, t_lg)
        wait("s", tp, t_lg)
        wmark("s", lT_free[g % 2])
        s_lT = sop("activation", lT_s[g % 2][:], ps_lT[:], AF.Copy)
        p4b1_free = (sp, s_lT)
        pend_lt2.append((g, s_lT))
        for t in range(4):
            tpar = t % 2
            wmark("t", p2_free)
            if g == 0 and t == 0:
                wait("t", vp, v_projdone)
            for hn in range(2):
                n0 = hn * 512
                for ic in range(DC):
                    top("matmul", P2[:, n0:n0 + 512],
                        xnT_g[par][:, ic * 512 + t * 128: ic * 512 + (t + 1) * 128],
                        projT[:, ic * D + n0: ic * D + n0 + 512],
                        start=(ic == 0), stop=(ic == DC - 1),
                        inc=(1 if (hn == 1 and ic == DC - 1) else 0))
            t_proj = tp.n
            wait("v", tp, t_proj)
            wmark("v", x1s_free[tpar])
            vop("scalar_tensor_tensor", x1s[tpar][:], P2[:],
                XV[par]["r24"][:, t:t + 1], bc_c1[:], ALU.mult, ALU.add)
            v_x1 = vop("tensor_tensor", x1s[tpar][:],
                       x1s[tpar][:], xnf[par][:, t * D:(t + 1) * D], ALU.add)
            p2_free = (vp, v_x1)
            wait("y", vp, v_x1)
            st_x1 = dma("y", ch_sx1[tpar], x1_d[(g * 4 + t) * 128:(g * 4 + t + 1) * 128, :],
                        x1s[tpar][:])
            x1s_free[tpar] = (ch_sx1[tpar], st_x1)
            vop("bn_stats", nscr2[:, 0:6], x1s[tpar][:, 0:512], drain=False)
            vop("bn_stats", nscr2[:, 6:12], x1s[tpar][:, 512:1024])
            vop("bn_aggr", zaggr(t), nscr2[:, 0:12])
        vop("tensor_scalar", tz4, z_var, EPS, None, ALU.add)
        newton_rstd(tz4, rz4, XV[par]["ns0"], XV[par]["ns1"], ns2, lines=(1.0,), iters=3)
        vop("tensor_copy", zmean[:, g * 4:(g + 1) * 4], z_mean, drain=False)
        v_zst = vop("tensor_copy", zrstd[:, g * 4:(g + 1) * 4], rz4)
        v_x1done = vp.n
        xg_free[par] = (vp, v_xn_last)
        xnf_free[par] = [(vp, v_x1done), (ch_sxn[par], st_xn)]
        xnT_free[par] = [(tp, tp.n), (ch_sxT[par], st_xnT)]
        if g + 1 < n_groups:
            emit_stats(g + 1)
        flush_wT()

    flush_lt2()
    flush_wT(force=True)
    flush_wT(force=True)
    wait("y", sp, wT_ready[n_tiles - 1])
    st_wt = dma("y", ch_wt, wT_d[:], wT_buf[:])
    p0_v, p0_s, p0_t = vp.n, sp.n, tp.n
    al.pop()

    al.push()
    fc1T = al.buf(DC * H4, BF16)
    fc2T = al.buf(HC4 * D, BF16)
    c1f_b = al.buf(H4, BF16, parts=1)
    c2f_b = al.buf(D, BF16, parts=1)
    al.push()
    g2T = al.buf(DC, F32)
    ghT = al.buf(HC4, F32)
    b2T = al.buf(DC, BF16)
    bhT = al.buf(HC4, BF16)
    grow2 = al.buf(D, F32, parts=1)
    growb2 = al.buf(D, BF16, parts=1)
    f1row = al.buf(H4, F32, parts=1)
    f2row = al.buf(D, F32, parts=1)
    crowA = al.buf(H4, F32, parts=1)
    bcg2 = al.buf(D, BF16)
    al.pop()
    zb = [al.buf(D, BF16) for _ in range(2)]
    zT = [al.buf(D, BF16) for _ in range(2)]
    hb = [al.buf(H4, BF16) for _ in range(2)]
    hT = [al.buf(H4, BF16) for _ in range(2)]
    x1t = [al.buf(D, F32) for _ in range(3)]
    sst = al.buf(D, F32)

    p0_store_done = [(ch_sxn[0], ch_sxn[0].n), (ch_sxn[1], ch_sxn[1].n),
                     (ch_sxT[0], ch_sxT[0].n), (ch_sxT[1], ch_sxT[1].n),
                     (ch_sx1[0], ch_sx1[0].n), (ch_sx1[1], ch_sx1[1].n),
                     (ch_xg[0], ch_xg[0].n), (ch_xg[1], ch_xg[1].n),
                     (ch_wt, ch_wt.n)]
    wait("y", vp, p0_v); wait("y", sp, p0_s); wait("y", tp, p0_t)
    wmark("y", p0_store_done)
    barrier(("v", "s", "t"), p0_v, p0_s, p0_t)
    wait("y", g_f1, g_f1_all)
    for ic in range(DC):
        dma("y", lm1, fc1T[:, ic * H4:(ic + 1) * H4], f1bf_d[:, ic * 128:(ic + 1) * 128],
            transpose=True)
    wait("s", g_f2, g_f2_all)
    wait("s", vp, p0_v); wait("s", tp, p0_t)
    wmark("s", p0_store_done)
    for hc in range(HC4):
        dma("s", s_f2, fc2T[:, hc * D:(hc + 1) * D], f2bf_d[:, hc * 128:(hc + 1) * 128],
            transpose=True)
    dma("y", lm1, f1row[0:1, :], fc1_b[None, :])
    dma("y", lm1, f2row[0:1, :], fc2_b[None, :])
    lm1_all = dma("y", lm1, grow2[0:1, :], gamma2[None, :])
    load_vecT(ln2_g, g2T, DC, crowA[0:1, 0:D])
    load_vecT(lnh_g, ghT, HC4, crowA[0:1, 0:D])
    load_vecT(ln2_b, b2T, DC, crowA[0:1, 0:D])
    load_vecT(lnh_b, bhT, HC4, crowA[0:1, 0:D])
    wait("g", vp, p0_v); wait("g", sp, p0_s); wait("g", tp, p0_t)
    wmark("g", p0_store_done)
    g_pa_all = dma("g", g_pa, growb2[:], gamma2[None, :])

    wait("v", lm1, lm1_all)
    for ic in range(DC):
        vop("tensor_scalar", fc1T[:, ic * H4:(ic + 1) * H4], fc1T[:, ic * H4:(ic + 1) * H4],
            g2T[:, ic:ic + 1], None, ALU.mult)
    v_f1fold = vp.n
    wait("t", vp, v_f1fold)
    for ch in range(8):
        if ch > 0:
            wait("t", vp, vp.n)
        for ic in range(DC):
            top("matmul", P2[0:1, 0:512], b2T[:, ic:ic + 1],
                fc1T[:, ic * H4 + ch * 512: ic * H4 + ch * 512 + 512],
                start=(ic == 0), stop=(ic == DC - 1), inc=(1 if ic == DC - 1 else 0))
        t_ch = tp.n
        wait("v", tp, t_ch)
        vop("tensor_copy", crowA[0:1, ch * 512:(ch + 1) * 512], P2[0:1, 0:512])
    vop("tensor_tensor", crowA[0:1, :], crowA[0:1, :], f1row[0:1, :], ALU.add)
    v_c1f = vop("tensor_copy", c1f_b[0:1, :], crowA[0:1, :])
    wait("v", s_f2, s_f2.n)
    for hc in range(HC4):
        vop("tensor_scalar", fc2T[:, hc * D:(hc + 1) * D], fc2T[:, hc * D:(hc + 1) * D],
            ghT[:, hc:hc + 1], None, ALU.mult)
    v_f2fold = vp.n
    wait("t", vp, v_f2fold)
    for ch in range(2):
        n0 = ch * 512
        if ch > 0:
            wait("t", vp, vp.n)
        for hc in range(HC4):
            top("matmul", P2[0:1, 0:512], bhT[:, hc:hc + 1],
                fc2T[:, hc * D + n0: hc * D + n0 + 512],
                start=(hc == 0), stop=(hc == HC4 - 1), inc=(1 if hc == HC4 - 1 else 0))
        t_ch = tp.n
        wait("v", tp, t_ch)
        vop("tensor_copy", crowA[0:1, n0:n0 + 512], P2[0:1, 0:512])
    vop("tensor_tensor", crowA[0:1, 0:D], crowA[0:1, 0:D], f2row[0:1, :], ALU.add)
    vop("tensor_tensor", crowA[0:1, 0:D], crowA[0:1, 0:D], grow2[0:1, :], ALU.mult)
    v_c2f = vop("tensor_copy", c2f_b[0:1, :], crowA[0:1, 0:D])
    wait("t", g_pa, g_pa_all)
    top("matmul", P4[:, 0:512], ones_b[0:1, 0:128], growb2[0:1, 0:512], start=True, stop=True)
    t_bg2 = top("matmul", P4[:, 512:1024], ones_b[0:1, 0:128], growb2[0:1, 512:1024],
                start=True, stop=True, inc=1)
    wait("v", tp, t_bg2)
    vop("tensor_copy", bcg2[:], P4[:, 0:1024])
    for hc in range(HC4):
        vop("tensor_tensor", fc2T[:, hc * D:(hc + 1) * D], fc2T[:, hc * D:(hc + 1) * D],
            bcg2[:], ALU.mult)
    v_f2done = vp.n

    x1t_free = [None, None, None]
    zb_free = [None, None]
    zT_free = [None, None]
    hb_free = [None, None]
    hT_free = [None, None]
    sst_free = [None]
    p4h_free = [(tp, t_bg2), (tp, t_bg2)]
    p2_free_pa = None
    p1a_free = None
    p1b_free = None
    ps_trA = P1a[:].bitcast(BF16)
    ps_trB = P1b[:].bitcast(BF16)

    pa_fold_done = [(vp, v_f2done), (vp, v_c1f), (vp, v_c2f)]
    n_t = n_tiles
    v_zb_t = [0] * (n_t + 1)
    v_hq_t = [0] * n_t
    s_zT_t = [0] * n_t
    s_hb_t = [0] * n_t
    s_hT_t = [0] * n_t
    t_ztr_t = [0] * n_t
    t_fc1_t = [[0] * 4 for _ in range(n_t)]
    t_fc2_t = [0] * n_t
    v_s_t = [0] * n_t
    ld_x1_t = [0] * n_t

    wmark("y", pa_fold_done)
    ld_x1_t[0] = dma("y", ch_l1[0], x1t[0], x1_d[0:128, :])
    ld_x1_t[1] = dma("y", ch_l1[1], x1t[1], x1_d[128:256, :])
    wait("v", ch_l1[0], ld_x1_t[0])
    wmark("v", pa_fold_done)
    v_zb_t[0] = vop("tensor_scalar", zb[0][:], x1t[0][:], zmean[:, 0:1], zrstd[:, 0:1],
                    ALU.subtract, ALU.mult)

    ew_jobs = []
    for e in range(NE):
        ew_jobs.append((ew1bf_d[e], ew1[e]))
        ew_jobs.append((ew2bf_d[e], ew2[e]))
    ew_sched = {}
    steps_avail = list(range(2, max(3, n_t)))
    for k, job in enumerate(ew_jobs):
        ew_sched.setdefault(steps_avail[k % len(steps_avail)], []).append(job)

    for t in range(n_t + 1):
        cur = t if t < n_t else None
        prv = t - 1 if t >= 1 else None
        if t in ew_sched:
            wait("g", sp, sp.n)
            for dst, srcm in ew_sched[t]:
                g_ew_all = dma("g", g_ew, dst, srcm)
        if 2 <= t + 1 < n_t:
            sl = (t + 1) % 3
            wmark("y", x1t_free[sl])
            ld_x1_t[t + 1] = dma("y", ch_l1[sl], x1t[sl],
                                 x1_d[(t + 1) * 128:(t + 2) * 128, :])
        if cur is not None:
            par = cur % 2
            wait("t", vp, v_zb_t[cur])
            wmark("t", p1b_free)
            for ic in range(DC):
                top("transpose", ps_trB[:, ic * 128:(ic + 1) * 128],
                    zb[par][:, ic * 128:(ic + 1) * 128], eye_b[:],
                    inc=(1 if ic == DC - 1 else 0))
            t_ztr_t[cur] = tp.n
            zb_free[par] = (tp, t_ztr_t[cur])
            wait("s", tp, t_ztr_t[cur])
            wmark("s", zT_free[par])
            s_zT_t[cur] = sop("activation", zT[par][:], ps_trB[:, 0:1024], AF.Copy)
            p1b_free = (sp, s_zT_t[cur])
            wait("t", sp, s_zT_t[cur])
            if cur == 0:
                wait("t", vp, v_c1f)
            for q in range(4):
                qh = q % 2
                h0 = q * 1024
                PQ = P4[:, qh * 1024:(qh + 1) * 1024]
                wmark("t", p4h_free[qh])
                for ic in range(DC):
                    for nn in range(2):
                        n0 = nn * 512
                        top("matmul", PQ[:, n0:n0 + 512],
                            zT[par][:, ic * 128:(ic + 1) * 128],
                            fc1T[:, ic * H4 + h0 + n0: ic * H4 + h0 + n0 + 512],
                            start=(ic == 0), stop=False)
                for nn in range(2):
                    n0 = nn * 512
                    top("matmul", PQ[:, n0:n0 + 512], ones_b[0:1, 0:128],
                        c1f_b[0:1, h0 + n0: h0 + n0 + 512], start=False, stop=True,
                        inc=(1 if nn == 1 else 0))
                t_fc1_t[cur][q] = tp.n
                wait("s", tp, t_fc1_t[cur][q])
                if q == 0:
                    wmark("s", hb_free[par])
                s_g = sop("activation", hb[par][:, h0:h0 + 1024], PQ[:], AF.Gelu)
                p4h_free[qh] = (sp, s_g)
                if q == 3:
                    s_hb_t[cur] = s_g
            zT_free[par] = (tp, t_fc1_t[cur][3])
        if prv is not None:
            par1 = prv % 2
            wait("v", sp, s_hb_t[prv])
            for ch in range(8):
                vop("bn_stats", nscr2[:, ch * 6:(ch + 1) * 6],
                    hb[par1][:, ch * 512:(ch + 1) * 512], drain=(ch == 7))
            vop("bn_aggr", nscr[:, 8:10], nscr2[:, 0:48])
            vop("tensor_scalar", nscr[:, 0:1], nscr[:, 9:10], EPS, None, ALU.add)
            newton_rstd(nscr[:, 0:1], nscr[:, 10:11], nscr[:, 2:3], nscr[:, 3:4], nscr[:, 4:5])
            v_hq_t[prv] = vop("tensor_scalar", hb[par1][:], hb[par1][:],
                              nscr[:, 8:9], nscr[:, 10:11], ALU.subtract, ALU.mult)
        if t + 1 < n_t:
            nxt = t + 1
            wait("v", ch_l1[nxt % 3 if nxt >= 2 else nxt], ld_x1_t[nxt])
            wmark("v", zb_free[nxt % 2])
            v_zb_t[nxt] = vop("tensor_scalar", zb[nxt % 2][:], x1t[nxt % 3][:],
                              zmean[:, nxt:nxt + 1], zrstd[:, nxt:nxt + 1],
                              ALU.subtract, ALU.mult)
        if prv is not None:
            par1 = prv % 2
            wait("t", vp, v_hq_t[prv])
            for rnd in range(4):
                PB = ps_trA if rnd % 2 == 0 else ps_trB
                wmark("t", p1a_free if rnd % 2 == 0 else p1b_free)
                for j in range(8):
                    hc = rnd * 8 + j
                    top("transpose", PB[:, j * 128:(j + 1) * 128],
                        hb[par1][:, hc * 128:(hc + 1) * 128], eye_b[:],
                        inc=(1 if j == 7 else 0))
                t_htr = tp.n
                wait("s", tp, t_htr)
                if rnd == 0:
                    wmark("s", hT_free[par1])
                s_hT = sop("activation", hT[par1][:, rnd * 1024:(rnd + 1) * 1024],
                           PB[:, 0:1024], AF.Copy)
                if rnd % 2 == 0:
                    p1a_free = (sp, s_hT)
                else:
                    p1b_free = (sp, s_hT)
            s_hT_t[prv] = sp.n
            hb_free[par1] = (tp, tp.n)
            wait("t", sp, s_hT_t[prv])
            wmark("t", p2_free_pa)
            if prv == 0:
                wait("t", vp, v_c2f)
            for hc in range(HC4):
                for nn in range(2):
                    n0 = nn * 512
                    top("matmul", P2[:, n0:n0 + 512], hT[par1][:, hc * 128:(hc + 1) * 128],
                        fc2T[:, hc * D + n0: hc * D + n0 + 512],
                        start=(hc == 0), stop=False)
            for nn in range(2):
                n0 = nn * 512
                top("matmul", P2[:, n0:n0 + 512], ones_b[0:1, 0:128], c2f_b[0:1, n0:n0 + 512],
                    start=False, stop=True, inc=(1 if nn == 1 else 0))
            t_fc2_t[prv] = tp.n
            hT_free[par1] = (tp, t_fc2_t[prv])
            wait("v", tp, t_fc2_t[prv])
            wmark("v", sst_free[0])
            v_s_t[prv] = vop("scalar_tensor_tensor", sst[:], P2[:], 1.0,
                             x1t[prv % 3][:], ALU.mult, ALU.add)
            p2_free_pa = (vp, v_s_t[prv])
            x1t_free[prv % 3] = (vp, v_s_t[prv])
            wait("y", vp, v_s_t[prv])
            st_s = dma("y", ch_ss[0], s_d[prv * 128:(prv + 1) * 128, :], sst[:])
            sst_free[0] = (ch_ss[0], st_s)

    pa_v, pa_s, pa_t = vp.n, sp.n, tp.n
    al.pop()

    al.push()
    xnT_sup = al.buf(DC * 1024, BF16)
    acc = al.buf(8 * D, F32)
    ehT = [al.buf(HEC * 512, BF16) for _ in range(2)]
    wslot = [al.buf(16 * 1024, BF16) for _ in range(3)]
    wT_buf = al.buf(n_tiles * 128, BF16, parts=8)
    eb2b = al.buf(D, BF16, parts=8)
    sf = [al.buf(D, BF16) for _ in range(2)]
    gf = [al.buf(D, BF16) for _ in range(2)]
    xf = [al.buf(D, F32) for _ in range(2)]

    barrier(("v", "s", "t"), pa_v, pa_s, pa_t)
    wait("y", vp, pa_v); wait("y", sp, pa_s); wait("y", tp, pa_t)
    wmark("y", [(ch_ss[0], ch_ss[0].n), (ch_ss[1], ch_ss[1].n),
                (ch_l1[0], ch_l1[0].n), (ch_l1[1], ch_l1[1].n), (ch_l1[2], ch_l1[2].n)])
    wait("y", g_ew, g_ew_all)
    ld_wt = dma("y", ch_wt, wT_buf[:], wT_d[:])
    wait("g", vp, pa_v); wait("g", sp, pa_s); wait("g", tp, pa_t)
    ld_eb2 = dma("g", g_eb2, eb2b[:], eb2[:])

    S1 = lambda k: (0, 2, 1)[k % 3]
    S2 = lambda k: (1, 0, 2)[k % 3]
    slot_free = [None, None, None]
    ps_eh_free = [None, None]
    ps_eo_free = [None, None]
    ps_init_free_h = [None]
    ehT_free = [None, None]
    acc_free = None
    xnT_sup_free = None
    sfxf_free = [None, None, None, None]
    gf_free = [None, None]
    ps_eo = [P2, P4[:, 0:1024]]
    ps_init = P4[:, 1024:2048]
    out_last = 0

    def load_slot(slot, which, e_idx):
        wmark("y", slot_free[slot])
        c = DC if which == 1 else HEC
        src = ew1bf_d if which == 1 else ew2bf_d
        return dma("y", ch_w[slot], wslot[slot].rearrange("p (c n) -> p c n", c=c),
                   src[e_idx].rearrange("(c p) n -> p c n", p=128))

    n_exp_total = n_supers * NE
    ld_w1v = {0: load_slot(S1(0), 1, 0)}
    ld_w2v = {0: load_slot(S2(0), 2, 0)}
    ld_xnT_a_v = {}
    ld_xnT_b_v = {}
    wmark("y", xnT_sup_free)
    ld_xnT_a_v[0] = dma("y", ch_xt, xnT_sup[:, 0:DC * 512], xnT_d[0, 0])
    ld_xnT_b_v[0] = dma("y", ch_xt2, xnT_sup[:, DC * 512:2 * DC * 512], xnT_d[0, 1])

    for st in range(n_supers):
        ld_xnT_a, ld_xnT_b = ld_xnT_a_v[st], ld_xnT_b_v[st]
        v_acc_last = [0] * 8
        wmark("y", acc_free[0] if acc_free else None)
        ld_acc_a = dma("y", ch_ac, acc.rearrange("p (t n) -> p t n", t=8)[:, 0:4, :],
                       xn_d[st * 1024:st * 1024 + 512, :].rearrange("(t p) n -> p t n", p=128))
        wmark("y", acc_free[1] if acc_free else None)
        ld_acc_b = dma("y", ch_ac2, acc.rearrange("p (t n) -> p t n", t=8)[:, 4:8, :],
                       xn_d[st * 1024 + 512:(st + 1) * 1024, :].rearrange("(t p) n -> p t n", p=128))
        def emit_init_tile(t8):
            tile = st * 8 + t8
            if st == 0 and t8 == 0:
                wait("t", ch_wt, ld_wt)
                wait("t", g_eb2, ld_eb2)
            wmark("t", ps_init_free_h[0])
            for nn in range(2):
                n0 = nn * 512
                top("matmul", ps_init[:, n0:n0 + 512],
                    wT_buf[0:8, tile * 128:(tile + 1) * 128],
                    eb2b[0:8, n0:n0 + 512], start=True, stop=True,
                    inc=(1 if nn == 1 else 0))
            t_init = tp.n
            wait("v", tp, t_init)
            if t8 == 0:
                wait("v", ch_ac, ld_acc_a)
            if t8 == 4:
                wait("v", ch_ac2, ld_acc_b)
            v_init = vop("scalar_tensor_tensor", acc[:, t8 * D:(t8 + 1) * D], ps_init,
                         1.0, acc[:, t8 * D:(t8 + 1) * D], ALU.mult, ALU.add)
            ps_init_free_h[0] = (vp, v_init)
        for e in range(NE):
            k = st * 8 + e
            i1, i2 = S1(k), S2(k)
            ld_w1, ld_w2 = ld_w1v[k], ld_w2v[k]
            if k + 1 < n_exp_total:
                ld_w1v[k + 1] = load_slot(S1(k + 1), 1, (k + 1) % NE)
            for hh in range(2):
                hpar = hh
                for hc in range(HEC):
                    pp = hc % 2
                    PE_ = P1a if pp == 0 else P1b
                    wmark("t", ps_eh_free[pp])
                    if hc == 0 and hh == 0:
                        wait("t", ch_w[i1], ld_w1)
                        if e == 0:
                            wait("t", ch_xt, ld_xnT_a)
                    if hc == 0 and hh == 1 and e == 0:
                        wait("t", ch_xt2, ld_xnT_b)
                    for ic in range(DC):
                        top("matmul", PE_[:],
                            wslot[i1][:, ic * HE + hc * 128: ic * HE + (hc + 1) * 128],
                            xnT_sup[:, hh * DC * 512 + ic * 512: hh * DC * 512 + (ic + 1) * 512],
                            start=(ic == 0), stop=(ic == DC - 1), inc=(1 if ic == DC - 1 else 0))
                    t_mm1 = tp.n
                    wait("s", tp, t_mm1)
                    if hc == 0:
                        wmark("s", ehT_free[hpar])
                        if st == 0 and e == 0 and hh == 0:
                            wait("s", vp, v_eb1)
                    s_eh = sop("activation", ehT[hpar][:, hc * 512:(hc + 1) * 512], PE_[:],
                               AF.Silu, bias=eb1T[:, hc * 8 + e: hc * 8 + e + 1])
                    ps_eh_free[pp] = (sp, s_eh)
                    if e == 1 and hc % 4 == 3:
                        emit_init_tile((hh * 16 + hc) // 4)
                if hh == 1:
                    slot_free[i1] = (tp, tp.n)
                    if k + 1 < n_exp_total:
                        ld_w2v[k + 1] = load_slot(S2(k + 1), 2, (k + 1) % NE)
                if e == NE - 1 and st + 1 < n_supers:
                    t_xnh = tp.n
                    wmark("y", (tp, t_xnh))
                    chn = ch_xt if hh == 0 else ch_xt2
                    val = dma("y", chn, xnT_sup[:, hh * DC * 512:(hh + 1) * DC * 512],
                              xnT_d[st + 1, hh])
                    if hh == 0:
                        ld_xnT_a_v[st + 1] = val
                    else:
                        ld_xnT_b_v[st + 1] = val
                s_eh_done = sp.n
                wait("t", sp, s_eh_done)
                if hh == 0:
                    wait("t", ch_w[i2], ld_w2)
                for tl in range(4):
                    t8 = hh * 4 + tl
                    tile = st * 8 + t8
                    pp = tl % 2
                    wmark("t", ps_eo_free[pp])
                    for hc in range(HEC):
                        for nn in range(2):
                            n0 = nn * 512
                            top("matmul", ps_eo[pp][:, n0:n0 + 512],
                                ehT[hpar][:, hc * 512 + tl * 128: hc * 512 + (tl + 1) * 128],
                                wslot[i2][:, hc * 1024 + n0: hc * 1024 + n0 + 512],
                                start=(hc == 0), stop=(hc == HEC - 1),
                                inc=(1 if (hc == HEC - 1 and nn == 1) else 0))
                    t_mm2 = tp.n
                    wait("v", tp, t_mm2)
                    if e == 0 and tl == 0:
                        wait("v", ch_ac if hh == 0 else ch_ac2, ld_acc_a if hh == 0 else ld_acc_b)
                    v_acc = vop("scalar_tensor_tensor", acc[:, t8 * D:(t8 + 1) * D], ps_eo[pp],
                                w_buf[:, tile * 8 + e: tile * 8 + e + 1],
                                acc[:, t8 * D:(t8 + 1) * D], ALU.mult, ALU.add)
                    ps_eo_free[pp] = (vp, v_acc)
                    if e == NE - 1:
                        v_acc_last[t8] = v_acc
                ehT_free[hpar] = (tp, tp.n)
            slot_free[i2] = (tp, tp.n)
        v_experts, t_experts, s_experts = vp.n, tp.n, sp.n
        for t8 in range(8):
            fpar = t8 % 2
            wmark("g", sfxf_free[fpar])
            ld_s = dma("g", ch_sf[fpar], sf[fpar],
                       s_d[(st * 8 + t8) * 128:(st * 8 + t8 + 1) * 128, :])
            wmark("y", sfxf_free[2 + fpar])
            ld_xx = dma("y", ch_xf[fpar], xf[fpar],
                        x_d[(st * 8 + t8) * 128:(st * 8 + t8 + 1) * 128, :])
            wait("s", vp, v_acc_last[t8])
            wmark("s", gf_free[fpar])
            s_g = sop("activation", gf[fpar], acc[:, t8 * D:(t8 + 1) * D], AF.Silu)
            wait("v", sp, s_g)
            wait("v", ch_sf[fpar], ld_s)
            vop("tensor_tensor", acc[:, t8 * D:(t8 + 1) * D], gf[fpar], sf[fpar], ALU.mult)
            gf_free[fpar] = (vp, vp.n)
            sfxf_free[fpar] = (vp, vp.n)
            wait("v", ch_xf[fpar], ld_xx)
            v_out = vop("tensor_tensor", acc[:, t8 * D:(t8 + 1) * D],
                        acc[:, t8 * D:(t8 + 1) * D], xf[fpar], ALU.add)
            sfxf_free[2 + fpar] = (vp, v_out)
            wait("g", vp, v_out)
            och = g_out if t8 < 4 else g_outb
            out_last = dma("g", och, out_d[(st * 8 + t8) * 128:(st * 8 + t8 + 1) * 128, :],
                           acc[:, t8 * D:(t8 + 1) * D])
            if t8 == 3:
                out_a = out_last
        acc_free = [(g_out, out_a), (g_outb, out_last)]

    wait("g", g_out, g_out.n)
    wait("g", g_outb, g_outb.n)
    al.pop()

    with nc.Block() as blk:
        @blk.sync
        def _(h):
            for fn in sched["y"]:
                fn(h)

        @blk.gpsimd
        def _(h):
            for fn in sched["g"]:
                fn(h)

        @blk.vector
        def _(h):
            for fn in sched["v"]:
                fn(h)

        @blk.scalar
        def _(h):
            for fn in sched["s"]:
                fn(h)

        @blk.tensor
        def _(h):
            for fn in sched["t"]:
                fn(h)

    es.close()
    return {e: len(sched[e]) for e in sched}

from concourse.bass_utils import run_bass_kernel_spmd
import ml_dtypes

N_CORES = 8
T_CORE = 4096

_WMAP = {
    "ln1_g": "L0_attn_ln1_g", "ln1_b": "L0_attn_ln1_b",
    "proj_w": "L0_attn_proj_w", "proj_b": "L0_attn_proj_b",
    "ln2_g": "L0_attn_ln2_g", "ln2_b": "L0_attn_ln2_b",
    "fc1_w": "L0_attn_fc1_w", "fc1_b": "L0_attn_fc1_b",
    "lnh_g": "L0_attn_lnh_g", "lnh_b": "L0_attn_lnh_b",
    "fc2_w": "L0_attn_fc2_w", "fc2_b": "L0_attn_fc2_b",
    "gamma1": "L0_attn_gamma1", "gamma2": "L0_attn_gamma2",
    "rw1": "L0_ffn_rw1", "rb1": "L0_ffn_rb1",
    "rw2": "L0_ffn_rw2", "rb2": "L0_ffn_rb2",
    "ew1": "L0_ffn_ew1", "eb1": "L0_ffn_eb1",
    "ew2": "L0_ffn_ew2", "eb2": "L0_ffn_eb2",
}


def kernel(**inputs):
    x = np.ascontiguousarray(np.asarray(inputs["L0_attn_x"], np.float32).reshape(-1, 1024))
    assert x.shape[0] == N_CORES * T_CORE
    shards = np.split(x, N_CORES, axis=0)

    base = {k: np.ascontiguousarray(np.asarray(inputs[v], np.float32)) for k, v in _WMAP.items()}
    base["eye_b"] = np.eye(128, dtype=ml_dtypes.bfloat16)
    base["eye_f"] = np.eye(128, dtype=np.float32)

    nc = bass.Bass("TRN2", target_bir_lowering=False)
    build(nc, T=T_CORE)

    in_maps = [dict(base, x=shards[c]) for c in range(N_CORES)]
    res = run_bass_kernel_spmd(nc, in_maps, core_ids=list(range(N_CORES)))
    out = np.concatenate([np.asarray(res.results[c]["out"], np.float32) for c in range(N_CORES)], axis=0)
    return out.reshape(4, 8192, 1024)
